# version 05
# baseline (speedup 1.0000x reference)
import numpy as np
import ml_dtypes
import contextlib
import concourse.bass as bass
import concourse.mybir as mybir
from concourse.bass_utils import run_bass_kernel_spmd

F32 = mybir.dt.float32
BF16 = mybir.dt.bfloat16
FP8 = mybir.dt.float8e4
AF = mybir.ActivationFunctionType
ALU = mybir.AluOpType

S, D, NO, FF = 4096, 1024, 3072, 4096
G, NP, NC = 8, 8, 32
NW = 4194304.0
EPS = 1e-5
ENGS = ("sync", "gpsimd", "tensor", "vector", "scalar")

WQ = 0
KT = (0, 4096)
VB = 8192
QT = 12288
WOUT = 12800
PT = (20992, 22016)
ATT = 23040
TREE = 27136
RRA, RRB = 32256, 32768
RSB = 33280
HT = 33792
XNBF = (24576, 25600)
TRASH = 26624
STAGE = 27648
ABT = 24576
QPOS, QNEG = 33792, 35840
BIGN = 50176


def build_nc():
    nc = bass.Bass("TRN2", target_bir_lowering=False)
    I = lambda n, s, d: nc.dram_tensor(n, s, d, kind="ExternalInput").ap()
    xc = I("xc", [S, D], F32)
    wqkvT = I("wqkvT", [D, NO], F32)
    mqkvT = I("mqkvT", [D, NO], F32)
    woutT = I("woutT", [D, D], F32)
    moutT = I("moutT", [D, D], F32)
    w1T = I("w1T", [D, FF], F32)
    w2T = I("w2T", [FF, D], F32)
    b1cI = I("b1c", [128, 32], F32)
    b2r = I("b2r", [1, D], F32)
    obr = I("obr", [1, D], F32)
    obm = I("obm", [1, D], F32)
    iden = I("iden", [128, 128], BF16)
    yc = nc.dram_tensor("yc", [S, D], F32, kind="ExternalOutput").ap()
    kt_scr = nc.dram_tensor("kt_scr", [NP, 128, S], BF16, kind="Internal").ap()
    v_scr = nc.dram_tensor("v_scr", [NP, 128, S], BF16, kind="Internal").ap()
    q_scr = nc.dram_tensor("q_scr", [NP, 128, S], BF16, kind="Internal").ap()
    xn_scr = nc.dram_tensor("xn_scr", [8, 128, S], BF16, kind="Internal").ap()
    s_scr = nc.dram_tensor("s_scr", [S, D], F32, kind="Internal").ap()
    cells_scr = nc.dram_tensor("cells_scr", [1, 4], F32, kind="Internal").ap()

    with contextlib.ExitStack() as est:
        E = est.enter_context
        big = E(nc.sbuf_tensor("big", [128, BIGN], BF16))
        w1q = E(nc.sbuf_tensor("w1q", [128, 32768], FP8))
        w2q = E(nc.sbuf_tensor("w2q", [128, 32768], FP8))
        stg = E(nc.sbuf_tensor("stg", [128, 4096], F32))
        xbuf = E(nc.sbuf_tensor("xbuf", [128, 2048], F32))
        rows = E(nc.sbuf_tensor("rows", [128, 2048], F32))
        xnt = E(nc.sbuf_tensor("xnt", [128, 4096], BF16))
        cols = E(nc.sbuf_tensor("cols", [128, 16], F32))
        gcols = E(nc.sbuf_tensor("gcols", [128, 64], F32))
        b1c = E(nc.sbuf_tensor("b1cs", [128, 32], F32))
        onesb = E(nc.sbuf_tensor("onesb", [128, 1], BF16))
        onesrow = E(nc.sbuf_tensor("onesrow", [1, 128], F32))
        onesrb = E(nc.sbuf_tensor("onesrb", [1, 64], BF16))
        idsb = E(nc.sbuf_tensor("idsb", [128, 128], BF16))
        PS1 = E(nc.psum_tensor("PS1", [128, 1024], F32))
        PS2 = E(nc.psum_tensor("PS2", [128, 1024], F32))
        PS3 = E(nc.psum_tensor("PS3", [128, 512], F32))
        PS4 = E(nc.psum_tensor("PS4", [128, 512], F32))
        PS5 = E(nc.psum_tensor("PS5", [128, 512], F32))
        PS6 = E(nc.psum_tensor("PS6", [128, 1024], BF16))
        sem = {n: E(nc.semaphore(n)) for n in ("ld", "lp", "pe", "ac", "dv", "st")}
        block = E(nc.Block())

        streams = {e: [] for e in ENGS}
        cnt = dict(ld=0, lp=0, pe=0, ac=0, dv=0, st=0)

        def wt(e, s_, v):
            if v and v > 0:
                streams[e].append(lambda eng, ss=sem[s_], vv=v: eng.wait_ge(ss, vv))

        def op(e, f, inc=None, k=1):
            if inc is None:
                streams[e].append(f)
                return None
            cnt[inc] += k
            streams[e].append(lambda eng, f=f, ss=sem[inc], kk=k: f(eng).then_inc(ss, kk))
            return cnt[inc]

        def B(lo, n):
            return big[:, lo:lo + n]

        op("vector", lambda e: e.memset(onesb[:], 1.0))
        op("vector", lambda e: e.memset(onesrow[:], 1.0))
        op("vector", lambda e: e.memset(onesrb[:], 1.0))
        op("vector", lambda e: e.memset(cols[:, 14:15], 1.0))
        op("vector", lambda e: e.memset(cols[:, 15:16], EPS))
        op("vector", lambda e: e.memset(cols[:, 8:9], 0.5 * EPS))
        v_mem = op("vector", lambda e: e.memset(cols[:, 9:10], -0.5 * EPS), inc="dv")
        ld_id = op("sync", lambda e: e.dma_start(out=idsb[:], in_=iden), inc="ld", k=16)
        op("sync", lambda e: e.dma_start(out=b1c[:], in_=b1cI), inc="ld", k=16)
        op("sync", lambda e: e.dma_start(
            out=rows[:, 0:1024], in_=b2r.partition_broadcast(128).squeeze(1)),
            inc="ld", k=16)
        op("sync", lambda e: e.dma_start(
            out=rows[:, 1024:2048], in_=obr.partition_broadcast(128).squeeze(1)),
            inc="ld", k=16)
        ld_obm = op("sync", lambda e: e.dma_start(
            out=stg[:, 0:1024], in_=obm.partition_broadcast(128).squeeze(1)),
            inc="ld", k=16)
        wt("vector", "ld", ld_obm)
        op("vector", lambda e: e.tensor_scalar(
            stg[:, 0:1024], stg[:, 0:1024], 0.0, None, ALU.is_gt))
        v_ob = op("vector", lambda e: e.tensor_tensor(
            rows[:, 1024:2048], rows[:, 1024:2048], stg[:, 0:1024], ALU.mult),
            inc="dv")

        gsl = [(0, 1024), (2048, 3072)]
        g_free = [v_ob, v_ob]
        for j in range(24):
            kc, part = divmod(j, 3)
            sl = j % 2
            woff, moff = gsl[sl]
            wt("sync", "dv", g_free[sl])
            op("sync", lambda e, kc=kc, part=part, woff=woff: e.dma_start(
                out=stg[:, woff:woff + 1024],
                in_=wqkvT[kc * 128:(kc + 1) * 128, part * 1024:(part + 1) * 1024]),
                inc="ld", k=16)
            lm = op("sync", lambda e, kc=kc, part=part, moff=moff: e.dma_start(
                out=stg[:, moff:moff + 1024],
                in_=mqkvT[kc * 128:(kc + 1) * 128, part * 1024:(part + 1) * 1024]),
                inc="ld", k=16)
            wt("vector", "ld", lm)
            op("vector", lambda e, moff=moff: e.tensor_scalar(
                stg[:, moff:moff + 1024], stg[:, moff:moff + 1024], 0.0, None,
                ALU.is_gt))
            gv = op("vector", lambda e, kc=kc, part=part, woff=woff, moff=moff:
                    e.tensor_tensor(
                        B(WQ + kc * 3072 + part * 1024, 1024),
                        stg[:, woff:woff + 1024], stg[:, moff:moff + 1024],
                        ALU.mult), inc="dv")
            g_free[sl] = gv
        wq_all_ready = gv

        xload = [0, 0]
        xfree = [0, 0]
        xnbf_free = [0, 0]
        ps6_free = 0
        xnT_evac = [0] * G
        xn_w = [0] * G
        qkv_free = [(None, 0)] * 3
        stage_free = [0] * 6
        kvq_w = 0
        xnt_pe_free = 0
        psi = 0

        for g in range(G):
            for su in range(4):
                t = g * 4 + su
                bi = t % 2
                wt("sync", "dv", xfree[bi])
                xload[bi] = op("sync", lambda e, t=t, bi=bi: e.dma_start(
                    out=xbuf[:, bi * 1024:(bi + 1) * 1024],
                    in_=xc[t * 128:(t + 1) * 128, :]), inc="ld", k=16)
                wt("vector", "ld", xload[bi])
                if t == 0:
                    wt("vector", "dv", v_mem)
                op("vector", lambda e, bi=bi: e.tensor_scalar(
                    B(TRASH, 1024), xbuf[:, bi * 1024:(bi + 1) * 1024], 0.0, None,
                    ALU.add, ALU.add, accum_out=cols[:, 0:1]))
                op("vector", lambda e, bi=bi: e.tensor_tensor(
                    B(TRASH, 1024), xbuf[:, bi * 1024:(bi + 1) * 1024],
                    xbuf[:, bi * 1024:(bi + 1) * 1024], ALU.mult))
                v_var = op("vector", lambda e: e.tensor_scalar(
                    B(TRASH, 1024), B(TRASH, 1024), 0.0, None, ALU.add, ALU.add,
                    accum_out=cols[:, 1:2]), inc="dv")
                wt("scalar", "dv", v_var)
                op("scalar", lambda e: e.mul(cols[:, 2:3], cols[:, 0:1], 1.0 / 1024.0))
                op("scalar", lambda e: e.activation(
                    cols[:, 4:5], cols[:, 2:3], AF.Identity, scale=cols[:, 2:3]))
                op("scalar", lambda e: e.activation(
                    cols[:, 5:6], cols[:, 4:5], AF.Identity, scale=-1.0,
                    bias=cols[:, 15:16]))
                op("scalar", lambda e: e.mul(cols[:, 3:4], cols[:, 1:2], 1.0 / 1024.0))
                op("scalar", lambda e: e.activation(
                    cols[:, 6:7], cols[:, 3:4], AF.Ln, bias=cols[:, 5:6]))
                a_rstd = op("scalar", lambda e: e.activation(
                    cols[:, 7:8], cols[:, 6:7], AF.Exp, scale=-0.5), inc="ac")
                wt("vector", "ac", a_rstd)
                wt("vector", "pe", xnbf_free[bi])
                xv = op("vector", lambda e, bi=bi: e.tensor_scalar(
                    B(XNBF[bi], 1024), xbuf[:, bi * 1024:(bi + 1) * 1024],
                    cols[:, 2:3], cols[:, 7:8], ALU.subtract, ALU.mult), inc="dv")
                xfree[bi] = xv
                wt("tensor", "dv", xv)
                wt("tensor", "dv", ps6_free)
                if t == 0:
                    wt("tensor", "ld", ld_id)
                for kc in range(8):
                    tp = op("tensor", lambda e, kc=kc, bi=bi: e.transpose(
                        PS6[:, kc * 128:(kc + 1) * 128],
                        B(XNBF[bi] + kc * 128, 128), idsb[:]),
                        inc="pe" if kc == 7 else None)
                xnbf_free[bi] = tp
                wt("vector", "pe", tp)
                if su == 0:
                    wt("vector", "pe", xnt_pe_free)
                    wt("vector", "lp", xn_w[g - 1] if g > 0 else 0)
                ev = op("vector", lambda e, su=su: e.tensor_copy(
                    xnt[:, :].rearrange("p (k r) -> p k r", r=512)
                    [:, 0:8, su * 128:(su + 1) * 128],
                    PS6[:].rearrange("p (k r) -> p k r", r=128)), inc="dv")
                ps6_free = ev
            xnT_evac[g] = ps6_free
            wt("gpsimd", "dv", xnT_evac[g])
            xn_w[g] = op("gpsimd", lambda e, g=g: e.dma_start(
                out=xn_scr[:, :, g * 512:(g + 1) * 512].rearrange("k p q -> p k q"),
                in_=xnt[:].rearrange("p (k q) -> p k q", k=8)), inc="lp", k=16)

            if g == 0:
                wt("tensor", "dv", wq_all_ready)
            for sect in range(2):
                obase = 1024 if sect == 0 else 0
                for p in range(NP):
                    ps = (PS3, PS4, PS5)[psi % 3]
                    fs, fv_ = qkv_free[psi % 3]
                    if fs:
                        wt("tensor", fs, fv_)
                    if sect == 0 and p == 0:
                        wt("tensor", "dv", xnT_evac[g])
                    for kc in range(8):
                        mm = op("tensor", lambda e, ps=ps, kc=kc, p=p, obase=obase:
                                e.matmul(
                                    ps[:, 0:512],
                                    B(WQ + kc * 3072 + obase + p * 128, 128),
                                    xnt[:, kc * 512:(kc + 1) * 512],
                                    start=(kc == 0), stop=(kc == 7)),
                                inc="pe" if kc == 7 else None)
                    slot = psi % 6
                    wt("scalar", "pe", mm)
                    wt("scalar", "lp", stage_free[slot])
                    if sect == 0:
                        a = op("scalar", lambda e, ps=ps, slot=slot: e.copy(
                            B(STAGE + slot * 512, 512), ps[:, 0:512]), inc="ac")
                    else:
                        a = op("scalar", lambda e, ps=ps, slot=slot: e.mul(
                            B(STAGE + slot * 512, 512), ps[:, 0:512], 0.03125),
                            inc="ac")
                    qkv_free[psi % 3] = ("ac", a)
                    wt("gpsimd", "ac", a)
                    dst = kt_scr if sect == 0 else q_scr
                    w = op("gpsimd", lambda e, dst=dst, p=p, g=g, slot=slot:
                           e.dma_start(
                               out=dst[p, :, g * 512:(g + 1) * 512],
                               in_=B(STAGE + slot * 512, 512)), inc="lp", k=16)
                    stage_free[slot] = w
                    kvq_w = w
                    psi += 1
            for su in range(4):
                c = g * 4 + su
                for half in range(2):
                    ps = (PS3, PS4, PS5)[psi % 3]
                    fs, fv_ = qkv_free[psi % 3]
                    if fs:
                        wt("tensor", fs, fv_)
                    for kc in range(8):
                        mm = op("tensor", lambda e, ps=ps, kc=kc, su=su, half=half:
                                e.matmul(
                                    ps[:, 0:512],
                                    xnt[:, kc * 512 + su * 128: kc * 512 + (su + 1) * 128],
                                    B(WQ + kc * 3072 + 2048 + half * 512, 512),
                                    start=(kc == 0), stop=(kc == 7)),
                                inc="pe" if kc == 7 else None)
                    slot = psi % 6
                    wt("scalar", "pe", mm)
                    wt("scalar", "lp", stage_free[slot])
                    a = op("scalar", lambda e, ps=ps, slot=slot: e.copy(
                        B(STAGE + slot * 512, 512), ps[:, 0:512]), inc="ac")
                    qkv_free[psi % 3] = ("ac", a)
                    wt("gpsimd", "ac", a)
                    for pp in range(4):
                        p = half * 4 + pp
                        w = op("gpsimd", lambda e, p=p, pp=pp, c=c, slot=slot:
                               e.dma_start(
                                   out=v_scr[p, :, c * 128:(c + 1) * 128],
                                   in_=B(STAGE + slot * 512 + pp * 128, 128)),
                               inc="lp", k=16)
                    stage_free[slot] = w
                    kvq_w = w
                    psi += 1
            xnt_pe_free = mm

        pe_p1_end = cnt["pe"]

        g_free2 = [0, 0]
        for kc in range(8):
            sl = kc % 2
            woff, moff = gsl[sl]
            wt("sync", "dv", g_free2[sl] or g_free[sl])
            op("sync", lambda e, kc=kc, woff=woff: e.dma_start(
                out=stg[:, woff:woff + 1024],
                in_=woutT[kc * 128:(kc + 1) * 128, :]), inc="ld", k=16)
            lm = op("sync", lambda e, kc=kc, moff=moff: e.dma_start(
                out=stg[:, moff:moff + 1024],
                in_=moutT[kc * 128:(kc + 1) * 128, :]), inc="ld", k=16)
            wt("vector", "ld", lm)
            if kc == 0:
                wt("vector", "pe", pe_p1_end)
            op("vector", lambda e, moff=moff: e.tensor_scalar(
                stg[:, moff:moff + 1024], stg[:, moff:moff + 1024], 0.0, None,
                ALU.is_gt))
            gv2 = op("vector", lambda e, kc=kc, woff=woff, moff=moff: e.tensor_tensor(
                B(WOUT + kc * 1024, 1024),
                stg[:, woff:woff + 1024], stg[:, moff:moff + 1024], ALU.mult),
                inc="dv")
            g_free2[sl] = gv2
        wout_ready = gv2

        q_ready = [0, 0]
        sslots = [0, 2048]
        s_free = [("dv", g_free2[0]), ("dv", g_free2[1])]
        ps4_free_init = 0
        for wi, (wten, nchunk, width, qdst) in enumerate(
                [(w1T, 8, 4096, w1q), (w2T, 32, 1024, w2q)]):
            cw = min(width, 2048)
            ncc = nchunk * (width // cw)
            for ch in range(ncc):
                kc, part = divmod(ch, width // cw)
                sl = ch % 2
                wt("sync", s_free[sl][0], s_free[sl][1])
                lw = op("sync", lambda e, wten=wten, kc=kc, part=part, sl=sl, cw=cw:
                        e.dma_start(
                            out=stg[:, sslots[sl]:sslots[sl] + cw],
                            in_=wten[kc * 128:(kc + 1) * 128,
                                     part * cw:(part + 1) * cw]),
                        inc="ld", k=16)
                wt("scalar", "ld", lw)
                if ch == 0 and wi == 0:
                    wt("scalar", "pe", pe_p1_end)
                a_abs = op("scalar", lambda e, sl=sl, cw=cw, ch=ch, wi=wi:
                           e.activation(
                               B(ABT, cw), stg[:, sslots[sl]:sslots[sl] + cw],
                               AF.Abs,
                               accum_out=gcols[:, 16 * wi + ch:16 * wi + ch + 1]
                               if wi == 0 else gcols[:, 16 + ch:17 + ch]),
                           inc="ac")
                s_free[sl] = ("ac", a_abs)
            wt("tensor", "ac", a_abs)
            wt("tensor", "dv", ps4_free_init)
            goff = 0 if wi == 0 else 16
            fm = op("tensor", lambda e, goff=goff, ncc=ncc: e.matmul(
                PS4[0:1, 0:ncc], cols[:, 14:15], gcols[:, goff:goff + ncc],
                start=True, stop=True), inc="pe")
            tcol = 10 + 2 * wi
            wt("vector", "pe", fm)
            trd = op("vector", lambda e, ncc=ncc: e.tensor_reduce(
                cols[0:1, 0:1], PS4[0:1, 0:ncc], mybir.AxisListType.X, ALU.add),
                inc="dv")
            ps4_free_init = trd
            wt("scalar", "dv", trd)
            op("scalar", lambda e, tcol=tcol: e.activation(
                cols[0:1, tcol:tcol + 1], cols[0:1, 0:1], AF.Identity,
                scale=0.5 / NW, bias=cols[0:1, 8:9]))
            tv = op("scalar", lambda e, tcol=tcol: e.activation(
                cols[0:1, tcol + 1:tcol + 2], cols[0:1, 0:1], AF.Identity,
                scale=-0.5 / NW, bias=cols[0:1, 9:10]), inc="ac")
            wt("gpsimd", "ac", tv)
            cw_ = op("gpsimd", lambda e, wi=wi, tcol=tcol: e.dma_start(
                out=cells_scr[0:1, 2 * wi:2 * wi + 2],
                in_=cols[0:1, tcol:tcol + 2]), inc="lp", k=16)
            wt("sync", "lp", cw_)
            tbc = op("sync", lambda e, wi=wi, tcol=tcol: e.dma_start(
                out=cols[:, tcol:tcol + 2],
                in_=cells_scr[0:1, 2 * wi:2 * wi + 2]
                .partition_broadcast(128).squeeze(1)), inc="ld", k=16)
            for ch in range(ncc):
                kc, part = divmod(ch, width // cw)
                sl = ch % 2
                wt("sync", s_free[sl][0], s_free[sl][1])
                lw = op("sync", lambda e, wten=wten, kc=kc, part=part, sl=sl, cw=cw:
                        e.dma_start(
                            out=stg[:, sslots[sl]:sslots[sl] + cw],
                            in_=wten[kc * 128:(kc + 1) * 128,
                                     part * cw:(part + 1) * cw]),
                        inc="ld", k=16)
                wt("vector", "ld", lw)
                if ch == 0:
                    wt("vector", "ld", tbc)
                    if wi == 0:
                        wt("vector", "lp", kvq_w)
                op("vector", lambda e, sl=sl, tcol=tcol, cw=cw: e.tensor_scalar(
                    B(QPOS, cw), stg[:, sslots[sl]:sslots[sl] + cw],
                    cols[:, tcol:tcol + 1], None, ALU.is_ge))
                op("vector", lambda e, sl=sl, tcol=tcol, cw=cw: e.tensor_scalar(
                    B(QNEG, cw), stg[:, sslots[sl]:sslots[sl] + cw],
                    cols[:, tcol + 1:tcol + 2], None, ALU.is_le))
                qv = op("vector", lambda e, qdst=qdst, kc=kc, part=part, cw=cw,
                        width=width: e.tensor_tensor(
                            qdst[:, kc * width + part * cw:
                                 kc * width + (part + 1) * cw],
                            B(QPOS, cw), B(QNEG, cw), ALU.subtract),
                        inc="dv")
                s_free[sl] = ("dv", qv)
            q_ready[wi] = qv

        PS6F = PS6[:, 0:1024].bitcast(F32)
        ps12 = [PS1, PS2]
        ss = dict(
            ktv_free=[0, 0], v_free=0, qt_free=0,
            v_load=0, qt_load=0, ktv_load=[0, 0],
            pt_pv_free=[0, 0], pt_tree_free=[0, 0],
            ps12_free=[0, 0], ps12_dv_free=0,
            att_free=0, psat_free=[0, 0],
            ps4_free_dv=ps4_free_init, tree_slot_free=0, exp_last=0,
            s_w_of=[0] * G, sslot_free=[q_ready[1], q_ready[1]],
            x_st=[0, 0], x_free=[xfree[0], xfree[1]],
            ps3_free=("dv", q_ready[1]),
            ht_reader_pe=0, xnt_reader_pe=pe_p1_end,
        )

        def ffn_round_gen(g):
            wt("sync", "lp", xn_w[g])
            wt("sync", "pe", ss["xnt_reader_pe"])
            xl = op("sync", lambda e, g=g: e.dma_start(
                out=xnt[:].rearrange("p (k q) -> p k q", k=8),
                in_=xn_scr[:, :, g * 512:(g + 1) * 512]
                .rearrange("k p q -> p k q")), inc="ld", k=16)
            yield
            last_f1 = 0
            ha = 0
            for fc in range(32):
                fs, fv_ = ss["ps3_free"]
                if fs:
                    wt("tensor", fs, fv_)
                if fc == 0:
                    wt("tensor", "ld", xl)
                    wt("tensor", "dv", q_ready[0])
                    wt("tensor", "pe", ss["ht_reader_pe"])
                for kc in range(8):
                    fmm = op("tensor", lambda e, kc=kc, fc=fc: e.matmul(
                        PS3[:, 0:512],
                        w1q[:, kc * 4096 + fc * 128: kc * 4096 + (fc + 1) * 128],
                        xnt[:, kc * 512:(kc + 1) * 512],
                        start=(kc == 0), stop=(kc == 7)),
                        inc="pe" if kc == 7 else None)
                last_f1 = fmm
                wt("vector", "pe", fmm)
                ha = op("vector", lambda e, fc=fc: e.tensor_scalar(
                    B(HT + fc * 512, 512), PS3[:, 0:512],
                    b1c[:, fc:fc + 1], 0.0, ALU.add, ALU.max), inc="dv")
                ss["ps3_free"] = ("dv", ha)
                yield
            ss["xnt_reader_pe"] = last_f1
            ht_ready_dv = ha
            last_f2 = 0
            for su in range(4):
                t = g * 4 + su
                bi = su % 2
                wt("sync", "lp", ss["s_w_of"][g])
                wt("sync", "dv", ss["sslot_free"][bi])
                sl_ = op("sync", lambda e, t=t, bi=bi: e.dma_start(
                    out=stg[:, 2048 + bi * 1024: 2048 + (bi + 1) * 1024],
                    in_=s_scr[t * 128:(t + 1) * 128, :]), inc="ld", k=16)
                wt("sync", "st", ss["x_st"][bi])
                wt("sync", "dv", ss["x_free"][bi])
                ss["x_free"][bi] = 0
                xl_ = op("sync", lambda e, t=t, bi=bi: e.dma_start(
                    out=xbuf[:, bi * 1024:(bi + 1) * 1024],
                    in_=xc[t * 128:(t + 1) * 128, :]), inc="ld", k=16)
                yield
                fv2 = 0
                gm = 0
                for half in range(2):
                    for blk in range(4):
                        if blk == 0:
                            fs, fv_ = ss["ps3_free"]
                            if fs:
                                wt("tensor", fs, fv_)
                        if su == 0 and half == 0 and blk == 0:
                            wt("tensor", "dv", ht_ready_dv)
                            wt("tensor", "dv", q_ready[1])
                        for fc in range(blk * 8, (blk + 1) * 8):
                            gm = op("tensor", lambda e, fc=fc, su=su, half=half:
                                    e.matmul(
                                        PS3[:, 0:512],
                                        B(HT + fc * 512 + su * 128, 128),
                                        w2q[:, fc * 1024 + half * 512:
                                            fc * 1024 + (half + 1) * 512],
                                        start=(fc == 0), stop=(fc == 31)),
                                    inc="pe" if fc == 31 else None)
                        yield
                    last_f2 = gm
                    wt("vector", "pe", gm)
                    fv2 = op("vector", lambda e, half=half: e.tensor_tensor(
                        stg[:, 1024 + half * 512:1024 + (half + 1) * 512],
                        PS3[:, 0:512], rows[:, half * 512:(half + 1) * 512],
                        ALU.add), inc="dv")
                    ss["ps3_free"] = ("dv", fv2)
                    yield
                wt("scalar", "dv", fv2)
                th = op("scalar", lambda e: e.activation(
                    stg[:, 0:1024], stg[:, 1024:2048], AF.Tanh, scale=0.5),
                    inc="ac")
                wt("vector", "ac", th)
                op("vector", lambda e: e.tensor_scalar(
                    stg[:, 0:1024], stg[:, 0:1024], 0.5, 0.5, ALU.mult, ALU.add))
                op("vector", lambda e: e.tensor_tensor(
                    stg[:, 1024:2048], stg[:, 1024:2048], stg[:, 0:1024],
                    ALU.mult))
                wt("vector", "ld", max(sl_, xl_))
                sv2 = op("vector", lambda e, bi=bi: e.tensor_tensor(
                    stg[:, 1024:2048],
                    stg[:, 2048 + bi * 1024: 2048 + (bi + 1) * 1024],
                    stg[:, 1024:2048], ALU.mult), inc="dv")
                ss["sslot_free"][bi] = sv2
                ov = op("vector", lambda e, bi=bi: e.tensor_tensor(
                    xbuf[:, bi * 1024:(bi + 1) * 1024],
                    xbuf[:, bi * 1024:(bi + 1) * 1024], stg[:, 1024:2048],
                    ALU.add), inc="dv")
                wt("sync", "dv", ov)
                ss["x_st"][bi] = op("sync", lambda e, t=t, bi=bi: e.dma_start(
                    out=yc[t * 128:(t + 1) * 128, :],
                    in_=xbuf[:, bi * 1024:(bi + 1) * 1024]), inc="st", k=16)
                yield
            ss["ht_reader_pe"] = last_f2

        wt("tensor", "dv", v_mem)
        wt("tensor", "ac", cnt["ac"])
        for g in range(G):
            fq = ffn_round_gen(g - 1) if g > 0 else iter(())

            def pop(n=1, fq=fq):
                for _ in range(n):
                    if next(fq, "END") == "END":
                        break
            for p in range(NP):
                bi = p % 2
                ko = KT[bi]
                psat = PS5[:, 0:512] if p % 2 == 0 else PS6F
                if g == 0 and p == 0:
                    wt("sync", "lp", kvq_w)
                wt("sync", "pe", ss["ktv_free"][bi])
                l1 = op("sync", lambda e, p=p, ko=ko: e.dma_start(
                    out=B(ko, 4096), in_=kt_scr[p, :, :]), inc="ld", k=16)
                wt("sync", "pe", ss["v_free"])
                l2 = op("sync", lambda e, p=p: e.dma_start(
                    out=B(VB, 4096), in_=v_scr[p, :, :]), inc="ld", k=16)
                wt("sync", "pe", ss["qt_free"])
                l3 = op("sync", lambda e, p=p, g=g: e.dma_start(
                    out=B(QT, 512), in_=q_scr[p, :, g * 512:(g + 1) * 512]),
                    inc="ld", k=16)
                ss["ktv_load"][bi] = l1
                ss["v_load"] = l2
                ss["qt_load"] = l3
                tree_stack = [[], []]
                tree_pool = [list(range(5)), list(range(5))]
                troot = [None, None]
                ax_of = [0] * NC
                last_pv = 0
                dvv = 0

                def emit_pv(c, p=p, psat=psat):
                    nonlocal last_pv
                    wt("tensor", "ac", ax_of[c])
                    if c == 0:
                        wt("tensor", "ld", ss["v_load"])
                        wt("tensor", "dv", ss["psat_free"][p % 2])
                    for h in range(2):
                        pv = op("tensor", lambda e, c=c, ci=c % 2, h=h, psat=psat:
                                e.matmul(
                                    psat[h * 64:(h + 1) * 64, :],
                                    B(VB + c * 128 + h * 64, 64),
                                    B(PT[ci] + h * 512, 512),
                                    start=(c == 0), stop=(c == NC - 1)),
                                inc="pe" if h == 1 else None)
                    ss["pt_pv_free"][c % 2] = pv
                    last_pv = pv

                for c in range(NC):
                    ci = c % 2
                    ps = ps12[ci]
                    if c < 2:
                        wt("tensor", "ld", max(ss["ktv_load"][bi], ss["qt_load"]))
                        wt("tensor", "dv", ss["ps12_dv_free"])
                    wt("tensor", "ac", ss["ps12_free"][ci])
                    op("tensor", lambda e, ps=ps, ko=ko, c=c: e.matmul(
                        ps[:, 0:512],
                        big[0:64, ko + c * 128: ko + (c + 1) * 128],
                        big[0:64, QT:QT + 512], start=True, stop=True))
                    sm = op("tensor", lambda e, ps=ps, ko=ko, c=c: e.matmul(
                        ps[:, 512:1024],
                        big[64:128, ko + c * 128: ko + (c + 1) * 128],
                        big[64:128, QT:QT + 512], start=True, stop=True),
                        inc="pe")
                    if c >= 1:
                        emit_pv(c - 1)
                    if c % 3 == 2:
                        pop(1)
                    wt("scalar", "pe", sm)
                    wt("scalar", "pe", ss["pt_pv_free"][ci])
                    wt("scalar", "dv", ss["pt_tree_free"][ci])
                    ax = op("scalar", lambda e, ps=ps, ci=ci: e.activation(
                        B(PT[ci], 1024), ps[:, 0:1024], AF.Exp), inc="ac")
                    ax_of[c] = ax
                    ss["ps12_free"][ci] = ax
                    ss["exp_last"] = ax
                    if c % 2 == 1:
                        wt("vector", "ac", ax)
                        if c == 1:
                            wt("vector", "pe", ss["tree_slot_free"])
                        for h in range(2):
                            slot = tree_pool[h].pop(0)
                            dvv = op("vector", lambda e, h=h, slot=slot:
                                     e.tensor_tensor(
                                         B(TREE + (h * 5 + slot) * 512, 512),
                                         B(PT[0] + h * 512, 512),
                                         B(PT[1] + h * 512, 512), ALU.add),
                                     inc="dv")
                            lev = 1
                            while tree_stack[h] and tree_stack[h][-1][0] == lev:
                                _, pslot = tree_stack[h].pop()
                                dvv = op("vector",
                                         lambda e, h=h, slot=slot, pslot=pslot:
                                         e.tensor_tensor(
                                             B(TREE + (h * 5 + slot) * 512, 512),
                                             B(TREE + (h * 5 + slot) * 512, 512),
                                             B(TREE + (h * 5 + pslot) * 512, 512),
                                             ALU.add), inc="dv")
                                tree_pool[h].append(pslot)
                                lev += 1
                            tree_stack[h].append((lev, slot))
                            if c == NC - 1:
                                troot[h] = (tree_stack[h][-1][1], dvv)
                        ss["pt_tree_free"][0] = dvv
                        ss["pt_tree_free"][1] = dvv
                emit_pv(NC - 1)
                wt("tensor", "dv", troot[1][1])
                wt("tensor", "dv", ss["ps4_free_dv"])
                op("tensor", lambda e, sl=troot[0][0]: e.matmul(
                    PS4[0:1, 0:512], onesb[:], B(TREE + sl * 512, 512),
                    start=True, stop=True))
                d1 = op("tensor", lambda e, sl=troot[1][0]: e.matmul(
                    PS4[64:65, 0:512], onesb[:], B(TREE + (5 + sl) * 512, 512),
                    start=True, stop=True), inc="pe")
                ss["tree_slot_free"] = d1
                wt("vector", "pe", d1)

                def _recip(e, dst_off):
                    with nc.allow_low_precision(
                            reason="softmax recip row stored bf16 as mm operand"):
                        return e.reciprocal(big[0:1, dst_off:dst_off + 512],
                                            PS4[0:1, 0:512] if dst_off == RRA
                                            else PS4[64:65, 0:512])
                op("vector", lambda e: _recip(e, RRA))
                rv = op("vector", lambda e: _recip(e, RRB), inc="dv")
                wt("tensor", "dv", rv)
                op("tensor", lambda e: e.matmul(
                    PS4[0:64, 0:512], onesrb[:], big[0:1, RRA:RRA + 512],
                    start=True, stop=True))
                rb = op("tensor", lambda e: e.matmul(
                    PS4[64:128, 0:512], onesrb[:], big[0:1, RRB:RRB + 512],
                    start=True, stop=True), inc="pe")
                wt("vector", "pe", rb)
                rc = op("vector", lambda e: e.tensor_copy(B(RSB, 512),
                                                          PS4[:, 0:512]),
                        inc="dv")
                ss["ps4_free_dv"] = rc
                wt("vector", "pe", last_pv)
                wt("vector", "pe", ss["att_free"])
                nv = op("vector", lambda e, p=p, psat=psat: e.tensor_tensor(
                    B(ATT + p * 512, 512), psat, B(RSB, 512), ALU.mult),
                    inc="dv")
                ss["psat_free"][p % 2] = nv
                ss["ktv_free"][bi] = last_pv
                ss["v_free"] = last_pv
                ss["qt_free"] = sm
                pop(2)
            att_ready = nv
            pop(200)
            s_slots = [2048, 3072]
            for su in range(4):
                if su == 0:
                    wt("tensor", "dv", att_ready)
                    wt("tensor", "dv", wout_ready)
                wt("tensor", "ac", ss["exp_last"])
                wt("tensor", "dv", ss["ps12_dv_free"])
                for p_ in range(NP):
                    for half in range(2):
                        om = op("tensor", lambda e, p_=p_, su=su, half=half:
                                e.matmul(
                                    PS1[:, half * 512:(half + 1) * 512],
                                    B(ATT + p_ * 512 + su * 128, 128),
                                    B(WOUT + p_ * 1024 + half * 512, 512),
                                    start=(p_ == 0), stop=(p_ == NP - 1)),
                                inc="pe" if (p_ == NP - 1 and half == 1) else None)
                wt("vector", "pe", om)
                wt("vector", "dv", ss["sslot_free"][su % 2])
                sv = op("vector", lambda e, su=su: e.tensor_tensor(
                    stg[:, s_slots[su % 2]:s_slots[su % 2] + 1024],
                    PS1[:, 0:1024], rows[:, 1024:2048], ALU.add), inc="dv")
                ss["ps12_dv_free"] = sv
                ss["sslot_free"][su % 2] = sv
                wt("gpsimd", "dv", sv)
                sw = op("gpsimd", lambda e, g=g, su=su: e.dma_start(
                    out=s_scr[(g * 4 + su) * 128:(g * 4 + su + 1) * 128, :],
                    in_=stg[:, s_slots[su % 2]:s_slots[su % 2] + 1024]),
                    inc="lp", k=16)
                ss["s_w_of"][g] = sw
            ss["att_free"] = om
        fq = ffn_round_gen(G - 1)
        for _ in range(500):
            if next(fq, "END") == "END":
                break

        wt("sync", "st", cnt["st"])

        @block.sync
        def _(eng):
            for f in streams["sync"]:
                f(eng)

        @block.gpsimd
        def _(eng):
            for f in streams["gpsimd"]:
                f(eng)

        @block.tensor
        def _(eng):
            for f in streams["tensor"]:
                f(eng)

        @block.vector
        def _(eng):
            for f in streams["vector"]:
                f(eng)

        @block.scalar
        def _(eng):
            for f in streams["scalar"]:
                f(eng)

    return nc


def kernel(**inputs):
    x = np.ascontiguousarray(np.asarray(inputs["L0_attn_x"], np.float32))
    C = lambda a: np.ascontiguousarray(np.asarray(a, np.float32))
    shared = dict(
        wqkvT=C(np.asarray(inputs["L0_attn_qkv_weight"]).T),
        mqkvT=C(np.asarray(inputs["L0_attn_qkv_weight_mask"]).T),
        woutT=C(np.asarray(inputs["L0_attn_out_weight"]).T),
        moutT=C(np.asarray(inputs["L0_attn_out_weight_mask"]).T),
        w1T=C(np.asarray(inputs["L0_ffn_W1"]).T),
        w2T=C(np.asarray(inputs["L0_ffn_W2"]).T),
        b1c=C(np.asarray(inputs["L0_ffn_b1"]).reshape(32, 128).T),
        b2r=C(np.asarray(inputs["L0_ffn_b2"]).reshape(1, D)),
        obr=C(np.asarray(inputs["L0_attn_out_bias"]).reshape(1, D)),
        obm=C(np.asarray(inputs["L0_attn_out_bias_mask"]).reshape(1, D)),
        iden=np.eye(128).astype(ml_dtypes.bfloat16),
    )
    nc = build_nc()
    in_maps = [dict(xc=np.ascontiguousarray(x[c]), **shared) for c in range(8)]
    res = run_bass_kernel_spmd(nc, in_maps, core_ids=list(range(8)))
    return np.stack([np.asarray(r["yc"], np.float32) for r in res.results], axis=0)


# version 10
# speedup vs baseline: 1.3850x; 1.3850x over previous
import numpy as np
import ml_dtypes
import contextlib
import concourse.bass as bass
import concourse.mybir as mybir
from concourse.bass_utils import run_bass_kernel_spmd

F32 = mybir.dt.float32
BF16 = mybir.dt.bfloat16
FP8 = mybir.dt.float8e4
AF = mybir.ActivationFunctionType
ALU = mybir.AluOpType

S, D, NO, FF = 4096, 1024, 3072, 4096
G, NP, NC = 8, 8, 32
NW = 4194304.0
EPS = 1e-5
ENGS = ("sync", "gpsimd", "tensor", "vector", "scalar")

WQ = 0
KT = 0
VB = 4096
QT = 8192
WOUT = 8704
PT = (16896, 17920, 18944, 19968)
ATT = 20992
TREE = 25088
HT = 31232
RRA, RRB = 47616, 48128
RSB = 48640
XNBF = (25088, 26112)
TRASH = 27136
STAGE = 28160
ABT = 25088
QPOS, QNEG = 31232, 33280
BIGN = 50176


def build_nc():
    nc = bass.Bass("TRN2", target_bir_lowering=False)
    I = lambda n, s, d: nc.dram_tensor(n, s, d, kind="ExternalInput").ap()
    xc = I("xc", [S, D], F32)
    wqkvT = I("wqkvT", [D, NO], F32)
    mqkvT = I("mqkvT", [D, NO], F32)
    woutT = I("woutT", [D, D], F32)
    moutT = I("moutT", [D, D], F32)
    w1T = I("w1T", [D, FF], F32)
    w2T = I("w2T", [FF, D], F32)
    b1cI = I("b1c", [128, 32], F32)
    b2r = I("b2r", [1, D], F32)
    obr = I("obr", [1, D], F32)
    obm = I("obm", [1, D], F32)
    iden = I("iden", [128, 128], BF16)
    yc = nc.dram_tensor("yc", [S, D], F32, kind="ExternalOutput").ap()
    kt_scr = nc.dram_tensor("kt_scr", [NP, 128, S], BF16, kind="Internal").ap()
    v_scr = nc.dram_tensor("v_scr", [NP, 128, S], BF16, kind="Internal").ap()
    q_scr = nc.dram_tensor("q_scr", [NP, 128, S], BF16, kind="Internal").ap()
    xn_scr = nc.dram_tensor("xn_scr", [8, 128, S], BF16, kind="Internal").ap()
    s_scr = nc.dram_tensor("s_scr", [S, D], F32, kind="Internal").ap()
    cells_scr = nc.dram_tensor("cells_scr", [1, 4], F32, kind="Internal").ap()

    with contextlib.ExitStack() as est:
        E = est.enter_context
        big = E(nc.sbuf_tensor("big", [128, BIGN], BF16))
        w1q = E(nc.sbuf_tensor("w1q", [128, 32768], FP8))
        w2q = E(nc.sbuf_tensor("w2q", [128, 32768], FP8))
        stg = E(nc.sbuf_tensor("stg", [128, 4096], F32))
        xbuf = E(nc.sbuf_tensor("xbuf", [128, 2048], F32))
        rows = E(nc.sbuf_tensor("rows", [128, 2048], F32))
        xnt = E(nc.sbuf_tensor("xnt", [128, 4096], BF16))
        cols = E(nc.sbuf_tensor("cols", [128, 16], F32))
        gcols = E(nc.sbuf_tensor("gcols", [128, 64], F32))
        b1c = E(nc.sbuf_tensor("b1cs", [128, 32], F32))
        onesb = E(nc.sbuf_tensor("onesb", [128, 1], BF16))
        onesrow = E(nc.sbuf_tensor("onesrow", [1, 128], F32))
        onesrb = E(nc.sbuf_tensor("onesrb", [1, 64], BF16))
        idsb = E(nc.sbuf_tensor("idsb", [128, 128], BF16))
        padsc = E(nc.sbuf_tensor("padsc", [128, 1024], BF16))
        PS1 = E(nc.psum_tensor("PS1", [128, 1024], F32))
        PS2 = E(nc.psum_tensor("PS2", [128, 1024], F32))
        PS3 = E(nc.psum_tensor("PS3", [128, 512], F32))
        PS4 = E(nc.psum_tensor("PS4", [128, 512], F32))
        PS5 = E(nc.psum_tensor("PS5", [128, 512], F32))
        PS6 = E(nc.psum_tensor("PS6", [128, 1024], BF16))
        sem = {n: E(nc.semaphore(n)) for n in ("ld", "lp", "pe", "ac", "dv", "st")}
        block = E(nc.Block())

        streams = {e: [] for e in ENGS}
        cnt = dict(ld=0, lp=0, pe=0, ac=0, dv=0, st=0)

        def wt(e, s_, v):
            if v and v > 0:
                streams[e].append(lambda eng, ss=sem[s_], vv=v: eng.wait_ge(ss, vv))

        def op(e, f, inc=None, k=1):
            if inc is None:
                streams[e].append(f)
                return None
            cnt[inc] += k
            streams[e].append(lambda eng, f=f, ss=sem[inc], kk=k: f(eng).then_inc(ss, kk))
            return cnt[inc]

        def B(lo, n):
            return big[:, lo:lo + n]

        op("vector", lambda e: e.memset(onesb[:], 1.0))
        op("vector", lambda e: e.memset(onesrow[:], 1.0))
        op("vector", lambda e: e.memset(onesrb[:], 1.0))
        op("vector", lambda e: e.memset(cols[:, 14:15], 1.0))
        op("vector", lambda e: e.memset(cols[:, 15:16], EPS))
        op("vector", lambda e: e.memset(cols[:, 8:9], 0.5 * EPS))
        v_mem = op("vector", lambda e: e.memset(cols[:, 9:10], -0.5 * EPS), inc="dv")
        ld_id = op("sync", lambda e: e.dma_start(out=idsb[:], in_=iden), inc="ld", k=16)
        op("sync", lambda e: e.dma_start(out=b1c[:], in_=b1cI), inc="ld", k=16)
        op("sync", lambda e: e.dma_start(
            out=rows[:, 0:1024], in_=b2r.partition_broadcast(128).squeeze(1)),
            inc="ld", k=16)
        op("sync", lambda e: e.dma_start(
            out=rows[:, 1024:2048], in_=obr.partition_broadcast(128).squeeze(1)),
            inc="ld", k=16)
        ld_obm = op("sync", lambda e: e.dma_start(
            out=stg[:, 0:1024], in_=obm.partition_broadcast(128).squeeze(1)),
            inc="ld", k=16)
        wt("vector", "ld", ld_obm)
        op("vector", lambda e: e.tensor_scalar(
            stg[:, 0:1024], stg[:, 0:1024], 0.0, None, ALU.is_gt))
        v_ob = op("vector", lambda e: e.tensor_tensor(
            rows[:, 1024:2048], rows[:, 1024:2048], stg[:, 0:1024], ALU.mult),
            inc="dv")

        gsl = [(0, 1024), (2048, 3072)]
        g_free = [v_ob, v_ob]
        for j in range(24):
            kc, part = divmod(j, 3)
            sl = j % 2
            woff, moff = gsl[sl]
            wt("sync", "dv", g_free[sl])
            op("sync", lambda e, kc=kc, part=part, woff=woff: e.dma_start(
                out=stg[:, woff:woff + 1024],
                in_=wqkvT[kc * 128:(kc + 1) * 128, part * 1024:(part + 1) * 1024]),
                inc="ld", k=16)
            lm = op("sync", lambda e, kc=kc, part=part, moff=moff: e.dma_start(
                out=stg[:, moff:moff + 1024],
                in_=mqkvT[kc * 128:(kc + 1) * 128, part * 1024:(part + 1) * 1024]),
                inc="ld", k=16)
            wt("vector", "ld", lm)
            op("vector", lambda e, moff=moff: e.tensor_scalar(
                stg[:, moff:moff + 1024], stg[:, moff:moff + 1024], 0.0, None,
                ALU.is_gt))
            gv = op("vector", lambda e, kc=kc, part=part, woff=woff, moff=moff:
                    e.tensor_tensor(
                        B(WQ + kc * 3072 + part * 1024, 1024),
                        stg[:, woff:woff + 1024], stg[:, moff:moff + 1024],
                        ALU.mult), inc="dv")
            g_free[sl] = gv
        wq_all_ready = gv

        xload = [0, 0]
        xfree = [0, 0]
        xnbf_free = [0, 0]
        ps6_free = 0
        xnT_evac = [0] * G
        xn_w = [0] * G
        qkv_free = [(None, 0)] * 3
        stage_free = [0] * 6
        kvq_w = 0
        xnt_pe_free = 0
        psi = 0

        for g in range(G):
            for su in range(4):
                t = g * 4 + su
                bi = t % 2
                wt("sync", "dv", xfree[bi])
                xload[bi] = op("sync", lambda e, t=t, bi=bi: e.dma_start(
                    out=xbuf[:, bi * 1024:(bi + 1) * 1024],
                    in_=xc[t * 128:(t + 1) * 128, :]), inc="ld", k=16)
                wt("vector", "ld", xload[bi])
                if t == 0:
                    wt("vector", "dv", v_mem)
                op("vector", lambda e, bi=bi: e.tensor_scalar(
                    B(TRASH, 1024), xbuf[:, bi * 1024:(bi + 1) * 1024], 0.0, None,
                    ALU.add, ALU.add, accum_out=cols[:, 0:1]))
                op("vector", lambda e, bi=bi: e.tensor_tensor(
                    B(TRASH, 1024), xbuf[:, bi * 1024:(bi + 1) * 1024],
                    xbuf[:, bi * 1024:(bi + 1) * 1024], ALU.mult))
                v_var = op("vector", lambda e: e.tensor_scalar(
                    B(TRASH, 1024), B(TRASH, 1024), 0.0, None, ALU.add, ALU.add,
                    accum_out=cols[:, 1:2]), inc="dv")
                wt("scalar", "dv", v_var)
                op("scalar", lambda e: e.mul(cols[:, 2:3], cols[:, 0:1], 1.0 / 1024.0))
                op("scalar", lambda e: e.activation(
                    cols[:, 4:5], cols[:, 2:3], AF.Identity, scale=cols[:, 2:3]))
                op("scalar", lambda e: e.activation(
                    cols[:, 5:6], cols[:, 4:5], AF.Identity, scale=-1.0,
                    bias=cols[:, 15:16]))
                op("scalar", lambda e: e.mul(cols[:, 3:4], cols[:, 1:2], 1.0 / 1024.0))
                op("scalar", lambda e: e.activation(
                    cols[:, 6:7], cols[:, 3:4], AF.Ln, bias=cols[:, 5:6]))
                op("scalar", lambda e: e.activation(
                    cols[:, 7:8], cols[:, 6:7], AF.Exp, scale=-0.5))
                a_rstd = op("scalar", lambda e: e.copy(
                    padsc[:, 0:512], padsc[:, 512:1024]), inc="ac")
                wt("vector", "ac", a_rstd)
                wt("vector", "pe", xnbf_free[bi])
                xv = op("vector", lambda e, bi=bi: e.tensor_scalar(
                    B(XNBF[bi], 1024), xbuf[:, bi * 1024:(bi + 1) * 1024],
                    cols[:, 2:3], cols[:, 7:8], ALU.subtract, ALU.mult), inc="dv")
                xfree[bi] = xv
                wt("tensor", "dv", xv)
                wt("tensor", "dv", ps6_free)
                if t == 0:
                    wt("tensor", "ld", ld_id)
                for kc in range(8):
                    tp = op("tensor", lambda e, kc=kc, bi=bi: e.transpose(
                        PS6[:, kc * 128:(kc + 1) * 128],
                        B(XNBF[bi] + kc * 128, 128), idsb[:]),
                        inc="pe" if kc == 7 else None)
                xnbf_free[bi] = tp
                wt("vector", "pe", tp)
                if su == 0:
                    wt("vector", "pe", xnt_pe_free)
                    wt("vector", "lp", xn_w[g - 1] if g > 0 else 0)
                ev = op("vector", lambda e, su=su: e.tensor_copy(
                    xnt[:, :].rearrange("p (k r) -> p k r", r=512)
                    [:, 0:8, su * 128:(su + 1) * 128],
                    PS6[:].rearrange("p (k r) -> p k r", r=128)), inc="dv")
                ps6_free = ev
            xnT_evac[g] = ps6_free
            wt("gpsimd", "dv", xnT_evac[g])
            xn_w[g] = op("gpsimd", lambda e, g=g: e.dma_start(
                out=xn_scr[:, :, g * 512:(g + 1) * 512].rearrange("k p q -> p k q"),
                in_=xnt[:].rearrange("p (k q) -> p k q", k=8)), inc="lp", k=16)

            if g == 0:
                wt("tensor", "dv", wq_all_ready)
            for sect in range(2):
                obase = 1024 if sect == 0 else 0
                for p in range(NP):
                    ps = (PS3, PS4, PS5)[psi % 3]
                    fs, fv_ = qkv_free[psi % 3]
                    if fs:
                        wt("tensor", fs, fv_)
                    if sect == 0 and p == 0:
                        wt("tensor", "dv", xnT_evac[g])
                    for kc in range(8):
                        mm = op("tensor", lambda e, ps=ps, kc=kc, p=p, obase=obase:
                                e.matmul(
                                    ps[:, 0:512],
                                    B(WQ + kc * 3072 + obase + p * 128, 128),
                                    xnt[:, kc * 512:(kc + 1) * 512],
                                    start=(kc == 0), stop=(kc == 7)),
                                inc="pe" if kc == 7 else None)
                    slot = psi % 6
                    wt("scalar", "pe", mm)
                    wt("scalar", "lp", stage_free[slot])
                    if sect == 0:
                        a = op("scalar", lambda e, ps=ps, slot=slot: e.copy(
                            B(STAGE + slot * 512, 512), ps[:, 0:512]), inc="ac")
                    else:
                        a = op("scalar", lambda e, ps=ps, slot=slot: e.mul(
                            B(STAGE + slot * 512, 512), ps[:, 0:512], 0.03125),
                            inc="ac")
                    qkv_free[psi % 3] = ("ac", a)
                    wt("gpsimd", "ac", a)
                    dst = kt_scr if sect == 0 else q_scr
                    w = op("gpsimd", lambda e, dst=dst, p=p, g=g, slot=slot:
                           e.dma_start(
                               out=dst[p, :, g * 512:(g + 1) * 512],
                               in_=B(STAGE + slot * 512, 512)), inc="lp", k=16)
                    stage_free[slot] = w
                    kvq_w = w
                    psi += 1
            for su in range(4):
                c = g * 4 + su
                for half in range(2):
                    ps = (PS3, PS4, PS5)[psi % 3]
                    fs, fv_ = qkv_free[psi % 3]
                    if fs:
                        wt("tensor", fs, fv_)
                    for kc in range(8):
                        mm = op("tensor", lambda e, ps=ps, kc=kc, su=su, half=half:
                                e.matmul(
                                    ps[:, 0:512],
                                    xnt[:, kc * 512 + su * 128: kc * 512 + (su + 1) * 128],
                                    B(WQ + kc * 3072 + 2048 + half * 512, 512),
                                    start=(kc == 0), stop=(kc == 7)),
                                inc="pe" if kc == 7 else None)
                    slot = psi % 6
                    wt("scalar", "pe", mm)
                    wt("scalar", "lp", stage_free[slot])
                    a = op("scalar", lambda e, ps=ps, slot=slot: e.copy(
                        B(STAGE + slot * 512, 512), ps[:, 0:512]), inc="ac")
                    qkv_free[psi % 3] = ("ac", a)
                    wt("gpsimd", "ac", a)
                    for pp in range(4):
                        p = half * 4 + pp
                        w = op("gpsimd", lambda e, p=p, pp=pp, c=c, slot=slot:
                               e.dma_start(
                                   out=v_scr[p, :, c * 128:(c + 1) * 128],
                                   in_=B(STAGE + slot * 512 + pp * 128, 128)),
                               inc="lp", k=16)
                    stage_free[slot] = w
                    kvq_w = w
                    psi += 1
            xnt_pe_free = mm

        pe_p1_end = cnt["pe"]

        g_free2 = [0, 0]
        for kc in range(8):
            sl = kc % 2
            woff, moff = gsl[sl]
            wt("sync", "dv", g_free2[sl] or g_free[sl])
            op("sync", lambda e, kc=kc, woff=woff: e.dma_start(
                out=stg[:, woff:woff + 1024],
                in_=woutT[kc * 128:(kc + 1) * 128, :]), inc="ld", k=16)
            lm = op("sync", lambda e, kc=kc, moff=moff: e.dma_start(
                out=stg[:, moff:moff + 1024],
                in_=moutT[kc * 128:(kc + 1) * 128, :]), inc="ld", k=16)
            wt("vector", "ld", lm)
            if kc == 0:
                wt("vector", "pe", pe_p1_end)
            op("vector", lambda e, moff=moff: e.tensor_scalar(
                stg[:, moff:moff + 1024], stg[:, moff:moff + 1024], 0.0, None,
                ALU.is_gt))
            gv2 = op("vector", lambda e, kc=kc, woff=woff, moff=moff: e.tensor_tensor(
                B(WOUT + kc * 1024, 1024),
                stg[:, woff:woff + 1024], stg[:, moff:moff + 1024], ALU.mult),
                inc="dv")
            g_free2[sl] = gv2
        wout_ready = gv2

        q_ready = [0, 0]
        sslots = [0, 2048]
        s_free = [("dv", g_free2[0]), ("dv", g_free2[1])]
        ps4_free_init = 0
        for wi, (wten, nchunk, width, qdst) in enumerate(
                [(w1T, 8, 4096, w1q), (w2T, 32, 1024, w2q)]):
            cw = min(width, 2048)
            ncc = nchunk * (width // cw)
            for ch in range(ncc):
                kc, part = divmod(ch, width // cw)
                sl = ch % 2
                wt("sync", s_free[sl][0], s_free[sl][1])
                lw = op("sync", lambda e, wten=wten, kc=kc, part=part, sl=sl, cw=cw:
                        e.dma_start(
                            out=stg[:, sslots[sl]:sslots[sl] + cw],
                            in_=wten[kc * 128:(kc + 1) * 128,
                                     part * cw:(part + 1) * cw]),
                        inc="ld", k=16)
                wt("scalar", "ld", lw)
                if ch == 0 and wi == 0:
                    wt("scalar", "pe", pe_p1_end)
                a_abs = op("scalar", lambda e, sl=sl, cw=cw, ch=ch, wi=wi:
                           e.activation(
                               B(ABT, cw), stg[:, sslots[sl]:sslots[sl] + cw],
                               AF.Abs,
                               accum_out=gcols[:, 16 * wi + ch:16 * wi + ch + 1]
                               if wi == 0 else gcols[:, 16 + ch:17 + ch]),
                           inc="ac")
                s_free[sl] = ("ac", a_abs)
            a_pad = op("scalar", lambda e: e.copy(
                padsc[:, 0:512], padsc[:, 512:1024]), inc="ac")
            wt("tensor", "ac", a_pad)
            wt("tensor", "dv", ps4_free_init)
            goff = 0 if wi == 0 else 16
            fm = op("tensor", lambda e, goff=goff, ncc=ncc: e.matmul(
                PS4[0:1, 0:ncc], cols[:, 14:15], gcols[:, goff:goff + ncc],
                start=True, stop=True), inc="pe")
            tcol = 10 + 2 * wi
            wt("vector", "pe", fm)
            op("vector", lambda e, ncc=ncc: e.tensor_reduce(
                cols[0:1, 0:1], PS4[0:1, 0:ncc], mybir.AxisListType.X, ALU.add))
            trd = op("vector", lambda e: e.tensor_copy(
                padsc[:, 0:512], padsc[:, 512:1024]), inc="dv")
            ps4_free_init = trd
            wt("scalar", "dv", trd)
            op("scalar", lambda e, tcol=tcol: e.activation(
                cols[0:1, tcol:tcol + 1], cols[0:1, 0:1], AF.Identity,
                scale=0.5 / NW, bias=cols[0:1, 8:9]))
            op("scalar", lambda e, tcol=tcol: e.activation(
                cols[0:1, tcol + 1:tcol + 2], cols[0:1, 0:1], AF.Identity,
                scale=-0.5 / NW, bias=cols[0:1, 9:10]))
            tv = op("scalar", lambda e: e.copy(
                padsc[:, 0:512], padsc[:, 512:1024]), inc="ac")
            wt("gpsimd", "ac", tv)
            cw_ = op("gpsimd", lambda e, wi=wi, tcol=tcol: e.dma_start(
                out=cells_scr[0:1, 2 * wi:2 * wi + 2],
                in_=cols[0:1, tcol:tcol + 2]), inc="lp", k=16)
            wt("sync", "lp", cw_)
            tbc = op("sync", lambda e, wi=wi, tcol=tcol: e.dma_start(
                out=cols[:, tcol:tcol + 2],
                in_=cells_scr[0:1, 2 * wi:2 * wi + 2]
                .partition_broadcast(128).squeeze(1)), inc="ld", k=16)
            for ch in range(ncc):
                kc, part = divmod(ch, width // cw)
                sl = ch % 2
                wt("sync", s_free[sl][0], s_free[sl][1])
                lw = op("sync", lambda e, wten=wten, kc=kc, part=part, sl=sl, cw=cw:
                        e.dma_start(
                            out=stg[:, sslots[sl]:sslots[sl] + cw],
                            in_=wten[kc * 128:(kc + 1) * 128,
                                     part * cw:(part + 1) * cw]),
                        inc="ld", k=16)
                wt("vector", "ld", lw)
                if ch == 0:
                    wt("vector", "ld", tbc)
                    if wi == 0:
                        wt("vector", "lp", kvq_w)
                op("vector", lambda e, sl=sl, tcol=tcol, cw=cw: e.tensor_scalar(
                    B(QPOS, cw), stg[:, sslots[sl]:sslots[sl] + cw],
                    cols[:, tcol:tcol + 1], None, ALU.is_ge))
                op("vector", lambda e, sl=sl, tcol=tcol, cw=cw: e.tensor_scalar(
                    B(QNEG, cw), stg[:, sslots[sl]:sslots[sl] + cw],
                    cols[:, tcol + 1:tcol + 2], None, ALU.is_le))
                qv = op("vector", lambda e, qdst=qdst, kc=kc, part=part, cw=cw,
                        width=width: e.tensor_tensor(
                            qdst[:, kc * width + part * cw:
                                 kc * width + (part + 1) * cw],
                            B(QPOS, cw), B(QNEG, cw), ALU.subtract),
                        inc="dv")
                s_free[sl] = ("dv", qv)
            q_ready[wi] = qv

        PS6F = PS6[:, 0:1024].bitcast(F32)
        ps12 = [PS1, PS2]
        ss = dict(
            kt_free=0, v_free=0, qt_free=0,
            v_load=0, qt_load=0, kt_load=0,
            pt_pv_free=[0, 0, 0, 0], pt_tree_free=[0, 0, 0, 0],
            ps12_free=[0, 0], ps12_dv_free=0,
            att_free=0, psat_free=[0, 0],
            ps4_free_dv=ps4_free_init, tree_slot_free=0, exp_last=0,
            s_w_of=[0] * G, sslot_free=[q_ready[1], q_ready[1]],
            x_st=[0, 0], x_free=[xfree[0], xfree[1]],
            ps3_free=("dv", q_ready[1]),
            ht_reader_pe=0, xnt_reader_pe=pe_p1_end,
            pending_tail=[], rv=0, d1=0, rb=0, att_last_nv=0,
        )
        tree_pool = [list(range(6)), list(range(6))]
        slot_guard = [[0] * 6, [0] * 6]

        def ffn_round_gen(g):
            wt("sync", "lp", xn_w[g])
            wt("sync", "pe", ss["xnt_reader_pe"])
            xl = op("sync", lambda e, g=g: e.dma_start(
                out=xnt[:].rearrange("p (k q) -> p k q", k=8),
                in_=xn_scr[:, :, g * 512:(g + 1) * 512]
                .rearrange("k p q -> p k q")), inc="ld", k=16)
            yield
            last_f1 = 0
            ha = 0
            for fc in range(32):
                fs, fv_ = ss["ps3_free"]
                if fs:
                    wt("tensor", fs, fv_)
                if fc == 0:
                    wt("tensor", "ld", xl)
                    wt("tensor", "dv", q_ready[0])
                    wt("tensor", "pe", ss["ht_reader_pe"])
                for kc in range(8):
                    fmm = op("tensor", lambda e, kc=kc, fc=fc: e.matmul(
                        PS3[:, 0:512],
                        w1q[:, kc * 4096 + fc * 128: kc * 4096 + (fc + 1) * 128],
                        xnt[:, kc * 512:(kc + 1) * 512],
                        start=(kc == 0), stop=(kc == 7)),
                        inc="pe" if kc == 7 else None)
                last_f1 = fmm
                wt("vector", "pe", fmm)
                ha = op("vector", lambda e, fc=fc: e.tensor_scalar(
                    B(HT + fc * 512, 512), PS3[:, 0:512],
                    b1c[:, fc:fc + 1], 0.0, ALU.add, ALU.max), inc="dv")
                ss["ps3_free"] = ("dv", ha)
                yield
            ss["xnt_reader_pe"] = last_f1
            ht_ready_dv = ha
            last_f2 = 0
            for su in range(4):
                t = g * 4 + su
                bi = su % 2
                wt("sync", "lp", ss["s_w_of"][g])
                wt("sync", "dv", ss["sslot_free"][bi])
                sl_ = op("sync", lambda e, t=t, bi=bi: e.dma_start(
                    out=stg[:, 2048 + bi * 1024: 2048 + (bi + 1) * 1024],
                    in_=s_scr[t * 128:(t + 1) * 128, :]), inc="ld", k=16)
                wt("sync", "st", ss["x_st"][bi])
                wt("sync", "dv", ss["x_free"][bi])
                ss["x_free"][bi] = 0
                xl_ = op("sync", lambda e, t=t, bi=bi: e.dma_start(
                    out=xbuf[:, bi * 1024:(bi + 1) * 1024],
                    in_=xc[t * 128:(t + 1) * 128, :]), inc="ld", k=16)
                yield
                fv2 = 0
                gm = 0
                for half in range(2):
                    for blk in range(4):
                        if blk == 0:
                            fs, fv_ = ss["ps3_free"]
                            if fs:
                                wt("tensor", fs, fv_)
                        if su == 0 and half == 0 and blk == 0:
                            wt("tensor", "dv", ht_ready_dv)
                            wt("tensor", "dv", q_ready[1])
                        for fc in range(blk * 8, (blk + 1) * 8):
                            gm = op("tensor", lambda e, fc=fc, su=su, half=half:
                                    e.matmul(
                                        PS3[:, 0:512],
                                        B(HT + fc * 512 + su * 128, 128),
                                        w2q[:, fc * 1024 + half * 512:
                                            fc * 1024 + (half + 1) * 512],
                                        start=(fc == 0), stop=(fc == 31)),
                                    inc="pe" if fc == 31 else None)
                        yield
                    last_f2 = gm
                    wt("vector", "pe", gm)
                    fv2 = op("vector", lambda e, half=half: e.tensor_tensor(
                        stg[:, 1024 + half * 512:1024 + (half + 1) * 512],
                        PS3[:, 0:512], rows[:, half * 512:(half + 1) * 512],
                        ALU.add), inc="dv")
                    ss["ps3_free"] = ("dv", fv2)
                    yield
                wt("scalar", "dv", fv2)
                th = op("scalar", lambda e: e.activation(
                    stg[:, 0:1024], stg[:, 1024:2048], AF.Tanh, scale=0.5),
                    inc="ac")
                wt("vector", "ac", th)
                op("vector", lambda e: e.tensor_scalar(
                    stg[:, 0:1024], stg[:, 0:1024], 0.5, 0.5, ALU.mult, ALU.add))
                op("vector", lambda e: e.tensor_tensor(
                    stg[:, 1024:2048], stg[:, 1024:2048], stg[:, 0:1024],
                    ALU.mult))
                wt("vector", "ld", max(sl_, xl_))
                sv2 = op("vector", lambda e, bi=bi: e.tensor_tensor(
                    stg[:, 1024:2048],
                    stg[:, 2048 + bi * 1024: 2048 + (bi + 1) * 1024],
                    stg[:, 1024:2048], ALU.mult), inc="dv")
                ss["sslot_free"][bi] = sv2
                ov = op("vector", lambda e, bi=bi: e.tensor_tensor(
                    xbuf[:, bi * 1024:(bi + 1) * 1024],
                    xbuf[:, bi * 1024:(bi + 1) * 1024], stg[:, 1024:2048],
                    ALU.add), inc="dv")
                wt("sync", "dv", ov)
                ss["x_st"][bi] = op("sync", lambda e, t=t, bi=bi: e.dma_start(
                    out=yc[t * 128:(t + 1) * 128, :],
                    in_=xbuf[:, bi * 1024:(bi + 1) * 1024]), inc="st", k=16)
                yield
            ss["ht_reader_pe"] = last_f2

        wt("tensor", "dv", v_mem)
        wt("tensor", "ac", cnt["ac"])
        for g in range(G):
            fq = ffn_round_gen(g - 1) if g > 0 else iter(())

            def pop(n=1, fq=fq):
                for _ in range(n):
                    if next(fq, "END") == "END":
                        break
            for p in range(NP):
                ko = KT
                psat = PS5[:, 0:512] if p % 2 == 0 else PS6F
                if g == 0 and p == 0:
                    wt("sync", "lp", kvq_w)
                wt("sync", "pe", ss["kt_free"])
                l1 = op("sync", lambda e, p=p, ko=ko: e.dma_start(
                    out=B(ko, 4096), in_=kt_scr[p, :, :]), inc="ld", k=16)
                wt("sync", "pe", ss["v_free"])
                l2 = op("sync", lambda e, p=p: e.dma_start(
                    out=B(VB, 4096), in_=v_scr[p, :, :]), inc="ld", k=16)
                wt("sync", "pe", ss["qt_free"])
                l3 = op("sync", lambda e, p=p, g=g: e.dma_start(
                    out=B(QT, 512), in_=q_scr[p, :, g * 512:(g + 1) * 512]),
                    inc="ld", k=16)
                ss["kt_load"] = l1
                ss["v_load"] = l2
                ss["qt_load"] = l3
                tree_stack = [[], []]
                troot = [None, None]
                ax_of = [0] * NC
                last_pv = 0
                dvv = 0

                def emit_pv(c, p=p, psat=psat):
                    nonlocal last_pv
                    wt("tensor", "ac", ax_of[c])
                    if c == 0:
                        wt("tensor", "ld", ss["v_load"])
                        wt("tensor", "dv", ss["psat_free"][p % 2])
                    for h in range(2):
                        pv = op("tensor", lambda e, c=c, ci=c % 4, h=h, psat=psat:
                                e.matmul(
                                    psat[h * 64:(h + 1) * 64, :],
                                    B(VB + c * 128 + h * 64, 64),
                                    B(PT[ci] + h * 512, 512),
                                    start=(c == 0), stop=(c == NC - 1)),
                                inc="pe" if h == 1 else None)
                    ss["pt_pv_free"][c % 4] = pv
                    last_pv = pv

                def emit_scores(c):
                    ci = c % 2
                    ps = ps12[ci]
                    if c < 2:
                        wt("tensor", "ld", max(ss["kt_load"], ss["qt_load"]))
                        wt("tensor", "dv", ss["ps12_dv_free"])
                    wt("tensor", "ac", ss["ps12_free"][ci])
                    op("tensor", lambda e, ps=ps, ko=ko, c=c: e.matmul(
                        ps[:, 0:512],
                        big[0:64, ko + c * 128: ko + (c + 1) * 128],
                        big[0:64, QT:QT + 512], start=True, stop=True))
                    return op("tensor", lambda e, ps=ps, ko=ko, c=c: e.matmul(
                        ps[:, 512:1024],
                        big[64:128, ko + c * 128: ko + (c + 1) * 128],
                        big[64:128, QT:QT + 512], start=True, stop=True),
                        inc="pe")

                def emit_exp(c, sm):
                    ci = c % 2
                    ps = ps12[ci]
                    wt("scalar", "pe", sm)
                    wt("scalar", "pe", ss["pt_pv_free"][c % 4])
                    wt("scalar", "dv", ss["pt_tree_free"][c % 4])
                    ax = op("scalar", lambda e, ps=ps, c4=c % 4: e.activation(
                        B(PT[c4], 1024), ps[:, 0:1024], AF.Exp), inc="ac")
                    ax_of[c] = ax
                    ss["ps12_free"][ci] = ax
                    ss["exp_last"] = ax

                def emit_tree(c):
                    nonlocal dvv
                    wt("vector", "ac", ax_of[c])
                    for h in range(2):
                        slot = tree_pool[h].pop(0)
                        if slot_guard[h][slot]:
                            wt("vector", "pe", slot_guard[h][slot])
                            slot_guard[h][slot] = 0
                        dvv = op("vector", lambda e, h=h, slot=slot,
                                 ca=(c - 1) % 4, cb=c % 4:
                                 e.tensor_tensor(
                                     B(TREE + (h * 6 + slot) * 512, 512),
                                     B(PT[ca] + h * 512, 512),
                                     B(PT[cb] + h * 512, 512), ALU.add),
                                 inc="dv")
                        lev = 1
                        while tree_stack[h] and tree_stack[h][-1][0] == lev:
                            _, pslot = tree_stack[h].pop()
                            dvv = op("vector",
                                     lambda e, h=h, slot=slot, pslot=pslot:
                                     e.tensor_tensor(
                                         B(TREE + (h * 6 + slot) * 512, 512),
                                         B(TREE + (h * 6 + slot) * 512, 512),
                                         B(TREE + (h * 6 + pslot) * 512, 512),
                                         ALU.add), inc="dv")
                            tree_pool[h].append(pslot)
                            lev += 1
                        tree_stack[h].append((lev, slot))
                        if c == NC - 1:
                            troot[h] = (tree_stack[h][-1][1], dvv)
                    ss["pt_tree_free"][(c - 1) % 4] = dvv
                    ss["pt_tree_free"][c % 4] = dvv

                def make_tail(p=p, psat=psat, troot_l=None, lpv=None):
                    def t1():
                        wt("tensor", "dv", troot_l[1][1])
                        wt("tensor", "dv", ss["ps4_free_dv"])
                        op("tensor", lambda e, sl=troot_l[0][0]: e.matmul(
                            PS4[0:1, 0:512], onesb[:], B(TREE + sl * 512, 512),
                            start=True, stop=True))
                        d1 = op("tensor", lambda e, sl=troot_l[1][0]: e.matmul(
                            PS4[64:65, 0:512], onesb[:],
                            B(TREE + (6 + sl) * 512, 512),
                            start=True, stop=True), inc="pe")
                        ss["d1"] = d1
                        for h in range(2):
                            rslot = troot_l[h][0]
                            tree_pool[h].append(rslot)
                            slot_guard[h][rslot] = d1

                    def _rec(e, off, row):
                        with nc.allow_low_precision(reason="recip row bf16"):
                            return e.reciprocal(big[0:1, off:off + 512],
                                                PS4[row:row + 1, 0:512])

                    def t2():
                        wt("vector", "pe", ss["d1"])
                        op("vector", lambda e: _rec(e, RRA, 0))

                    def t3():
                        op("vector", lambda e: _rec(e, RRB, 64))
                        ss["rv"] = op("vector", lambda e: e.tensor_copy(
                            padsc[:, 0:512], padsc[:, 512:1024]), inc="dv")

                    def t4():
                        wt("tensor", "dv", ss["rv"])
                        op("tensor", lambda e: e.matmul(
                            PS4[0:64, 0:512], onesrb[:], big[0:1, RRA:RRA + 512],
                            start=True, stop=True))
                        ss["rb"] = op("tensor", lambda e: e.matmul(
                            PS4[64:128, 0:512], onesrb[:],
                            big[0:1, RRB:RRB + 512],
                            start=True, stop=True), inc="pe")

                    def t5():
                        wt("vector", "pe", ss["rb"])
                        rc = op("vector", lambda e: e.tensor_copy(
                            B(RSB, 512), PS4[:, 0:512]), inc="dv")
                        ss["ps4_free_dv"] = rc
                        wt("vector", "pe", lpv[0])
                        wt("vector", "pe", ss["att_free"])
                        nv = op("vector", lambda e, p=p, psat=psat:
                                e.tensor_tensor(
                                    B(ATT + p * 512, 512), psat, B(RSB, 512),
                                    ALU.mult), inc="dv")
                        ss["psat_free"][p % 2] = nv
                        ss["att_last_nv"] = nv
                    return [t1, t2, t3, t4, t5]

                TAIL_STEPS = {2: 0, 6: 1, 12: 2, 16: 3, 20: 4}
                for c2 in range(0, NC, 2):
                    smA = emit_scores(c2)
                    smB = emit_scores(c2 + 1)
                    sm = smB
                    if c2 >= 2:
                        emit_pv(c2 - 2)
                        emit_pv(c2 - 1)
                    if c2 in TAIL_STEPS and ss["pending_tail"]:
                        ss["pending_tail"][TAIL_STEPS[c2]]()
                    if (c2 // 2) % 3 != 0:
                        pop(1)
                    emit_exp(c2, smA)
                    emit_exp(c2 + 1, smB)
                    emit_tree(c2 + 1)
                emit_pv(NC - 2)
                emit_pv(NC - 1)
                lpv = [last_pv]
                stages = make_tail(troot_l=troot, lpv=lpv)
                if p == NP - 1:
                    for f in stages:
                        f()
                    ss["pending_tail"] = []
                else:
                    ss["pending_tail"] = stages
                ss["kt_free"] = sm
                ss["v_free"] = last_pv
                ss["qt_free"] = sm
                pop(2)
            att_ready = ss["att_last_nv"]
            pop(200)
            s_slots = [2048, 3072]
            for su in range(4):
                if su == 0:
                    wt("tensor", "dv", att_ready)
                    wt("tensor", "dv", wout_ready)
                wt("tensor", "ac", ss["exp_last"])
                wt("tensor", "dv", ss["ps12_dv_free"])
                for p_ in range(NP):
                    for half in range(2):
                        om = op("tensor", lambda e, p_=p_, su=su, half=half:
                                e.matmul(
                                    PS1[:, half * 512:(half + 1) * 512],
                                    B(ATT + p_ * 512 + su * 128, 128),
                                    B(WOUT + p_ * 1024 + half * 512, 512),
                                    start=(p_ == 0), stop=(p_ == NP - 1)),
                                inc="pe" if (p_ == NP - 1 and half == 1) else None)
                wt("vector", "pe", om)
                wt("vector", "dv", ss["sslot_free"][su % 2])
                sv = op("vector", lambda e, su=su: e.tensor_tensor(
                    stg[:, s_slots[su % 2]:s_slots[su % 2] + 1024],
                    PS1[:, 0:1024], rows[:, 1024:2048], ALU.add), inc="dv")
                ss["ps12_dv_free"] = sv
                ss["sslot_free"][su % 2] = sv
                wt("gpsimd", "dv", sv)
                sw = op("gpsimd", lambda e, g=g, su=su: e.dma_start(
                    out=s_scr[(g * 4 + su) * 128:(g * 4 + su + 1) * 128, :],
                    in_=stg[:, s_slots[su % 2]:s_slots[su % 2] + 1024]),
                    inc="lp", k=16)
                ss["s_w_of"][g] = sw
            ss["att_free"] = om
        fq = ffn_round_gen(G - 1)
        for _ in range(500):
            if next(fq, "END") == "END":
                break

        wt("sync", "st", cnt["st"])

        @block.sync
        def _(eng):
            for f in streams["sync"]:
                f(eng)

        @block.gpsimd
        def _(eng):
            for f in streams["gpsimd"]:
                f(eng)

        @block.tensor
        def _(eng):
            for f in streams["tensor"]:
                f(eng)

        @block.vector
        def _(eng):
            for f in streams["vector"]:
                f(eng)

        @block.scalar
        def _(eng):
            for f in streams["scalar"]:
                f(eng)

    return nc


def kernel(**inputs):
    x = np.ascontiguousarray(np.asarray(inputs["L0_attn_x"], np.float32))
    C = lambda a: np.ascontiguousarray(np.asarray(a, np.float32))
    shared = dict(
        wqkvT=C(np.asarray(inputs["L0_attn_qkv_weight"]).T),
        mqkvT=C(np.asarray(inputs["L0_attn_qkv_weight_mask"]).T),
        woutT=C(np.asarray(inputs["L0_attn_out_weight"]).T),
        moutT=C(np.asarray(inputs["L0_attn_out_weight_mask"]).T),
        w1T=C(np.asarray(inputs["L0_ffn_W1"]).T),
        w2T=C(np.asarray(inputs["L0_ffn_W2"]).T),
        b1c=C(np.asarray(inputs["L0_ffn_b1"]).reshape(32, 128).T),
        b2r=C(np.asarray(inputs["L0_ffn_b2"]).reshape(1, D)),
        obr=C(np.asarray(inputs["L0_attn_out_bias"]).reshape(1, D)),
        obm=C(np.asarray(inputs["L0_attn_out_bias_mask"]).reshape(1, D)),
        iden=np.eye(128).astype(ml_dtypes.bfloat16),
    )
    nc = build_nc()
    in_maps = [dict(xc=np.ascontiguousarray(x[c]), **shared) for c in range(8)]
    res = run_bass_kernel_spmd(nc, in_maps, core_ids=list(range(8)))
    return np.stack([np.asarray(r["yc"], np.float32) for r in res.results], axis=0)


# version 11
# speedup vs baseline: 1.3892x; 1.0031x over previous
import numpy as np
import ml_dtypes
import contextlib
import concourse.bass as bass
import concourse.mybir as mybir
from concourse.bass_utils import run_bass_kernel_spmd

F32 = mybir.dt.float32
BF16 = mybir.dt.bfloat16
FP8 = mybir.dt.float8e4
AF = mybir.ActivationFunctionType
ALU = mybir.AluOpType

S, D, NO, FF = 4096, 1024, 3072, 4096
G, NP, NC = 8, 8, 32
NW = 4194304.0
EPS = 1e-5
ENGS = ("sync", "gpsimd", "tensor", "vector", "scalar")

WQ = 0
KT = 0
VB = 4096
QT = 8192
WOUT = 8704
PT = (16896, 17920, 18944, 19968)
ATT = 20992
TREE = 25088
HT = 31232
RRA, RRB = 47616, 48128
RSB = 48640
XNBF = (25088, 26112)
TRASH = 27136
STAGE = 28160
ABT = 25088
QPOS, QNEG = 31232, 33280
BIGN = 50176


def build_nc():
    nc = bass.Bass("TRN2", target_bir_lowering=False)
    I = lambda n, s, d: nc.dram_tensor(n, s, d, kind="ExternalInput").ap()
    xc = I("xc", [S, D], F32)
    wqkvT = I("wqkvT", [D, NO], F32)
    mqkvT = I("mqkvT", [D, NO], F32)
    woutT = I("woutT", [D, D], F32)
    moutT = I("moutT", [D, D], F32)
    w1T = I("w1T", [D, FF], F32)
    w2T = I("w2T", [FF, D], F32)
    b1cI = I("b1c", [128, 32], F32)
    b2r = I("b2r", [1, D], F32)
    obr = I("obr", [1, D], F32)
    obm = I("obm", [1, D], F32)
    iden = I("iden", [128, 128], BF16)
    yc = nc.dram_tensor("yc", [S, D], F32, kind="ExternalOutput").ap()
    kt_scr = nc.dram_tensor("kt_scr", [NP, 128, S], BF16, kind="Internal").ap()
    v_scr = nc.dram_tensor("v_scr", [NP, 128, S], BF16, kind="Internal").ap()
    q_scr = nc.dram_tensor("q_scr", [NP, 128, S], BF16, kind="Internal").ap()
    xn_scr = nc.dram_tensor("xn_scr", [8, 128, S], BF16, kind="Internal").ap()
    s_scr = nc.dram_tensor("s_scr", [S, D], F32, kind="Internal").ap()
    cells_scr = nc.dram_tensor("cells_scr", [1, 4], F32, kind="Internal").ap()

    with contextlib.ExitStack() as est:
        E = est.enter_context
        big = E(nc.sbuf_tensor("big", [128, BIGN], BF16))
        w1q = E(nc.sbuf_tensor("w1q", [128, 32768], FP8))
        w2q = E(nc.sbuf_tensor("w2q", [128, 32768], FP8))
        stg = E(nc.sbuf_tensor("stg", [128, 4096], F32))
        xbuf = E(nc.sbuf_tensor("xbuf", [128, 2048], F32))
        rows = E(nc.sbuf_tensor("rows", [128, 2048], F32))
        xnt = E(nc.sbuf_tensor("xnt", [128, 4096], BF16))
        cols = E(nc.sbuf_tensor("cols", [128, 16], F32))
        gcols = E(nc.sbuf_tensor("gcols", [128, 64], F32))
        b1c = E(nc.sbuf_tensor("b1cs", [128, 32], F32))
        onesb = E(nc.sbuf_tensor("onesb", [128, 1], BF16))
        onesrow = E(nc.sbuf_tensor("onesrow", [1, 128], F32))
        onesrb = E(nc.sbuf_tensor("onesrb", [1, 64], BF16))
        idsb = E(nc.sbuf_tensor("idsb", [128, 128], BF16))
        padsc = E(nc.sbuf_tensor("padsc", [128, 1024], BF16))
        PS1 = E(nc.psum_tensor("PS1", [128, 1024], F32))
        PS2 = E(nc.psum_tensor("PS2", [128, 1024], F32))
        PS3 = E(nc.psum_tensor("PS3", [128, 512], F32))
        PS4 = E(nc.psum_tensor("PS4", [128, 512], F32))
        PS5 = E(nc.psum_tensor("PS5", [128, 512], F32))
        PS6 = E(nc.psum_tensor("PS6", [128, 1024], BF16))
        sem = {n: E(nc.semaphore(n)) for n in ("ld", "lp", "pe", "ac", "dv", "st")}
        block = E(nc.Block())

        streams = {e: [] for e in ENGS}
        cnt = dict(ld=0, lp=0, pe=0, ac=0, dv=0, st=0)

        def wt(e, s_, v):
            if v and v > 0:
                streams[e].append(lambda eng, ss=sem[s_], vv=v: eng.wait_ge(ss, vv))

        def op(e, f, inc=None, k=1):
            if inc is None:
                streams[e].append(f)
                return None
            cnt[inc] += k
            streams[e].append(lambda eng, f=f, ss=sem[inc], kk=k: f(eng).then_inc(ss, kk))
            return cnt[inc]

        def B(lo, n):
            return big[:, lo:lo + n]

        op("vector", lambda e: e.memset(onesb[:], 1.0))
        op("vector", lambda e: e.memset(onesrow[:], 1.0))
        op("vector", lambda e: e.memset(onesrb[:], 1.0))
        op("vector", lambda e: e.memset(cols[:, 14:15], 1.0))
        op("vector", lambda e: e.memset(cols[:, 15:16], EPS))
        op("vector", lambda e: e.memset(cols[:, 8:9], 0.5 * EPS))
        v_mem = op("vector", lambda e: e.memset(cols[:, 9:10], -0.5 * EPS), inc="dv")
        ld_id = op("sync", lambda e: e.dma_start(out=idsb[:], in_=iden), inc="ld", k=16)
        op("sync", lambda e: e.dma_start(out=b1c[:], in_=b1cI), inc="ld", k=16)
        op("sync", lambda e: e.dma_start(
            out=rows[:, 0:1024], in_=b2r.partition_broadcast(128).squeeze(1)),
            inc="ld", k=16)
        op("sync", lambda e: e.dma_start(
            out=rows[:, 1024:2048], in_=obr.partition_broadcast(128).squeeze(1)),
            inc="ld", k=16)
        ld_obm = op("sync", lambda e: e.dma_start(
            out=stg[:, 0:1024], in_=obm.partition_broadcast(128).squeeze(1)),
            inc="ld", k=16)
        wt("vector", "ld", ld_obm)
        op("vector", lambda e: e.tensor_scalar(
            stg[:, 0:1024], stg[:, 0:1024], 0.0, None, ALU.is_gt))
        v_ob = op("vector", lambda e: e.tensor_tensor(
            rows[:, 1024:2048], rows[:, 1024:2048], stg[:, 0:1024], ALU.mult),
            inc="dv")

        gsl = [(0, 1024), (2048, 3072)]
        g_free = [v_ob, v_ob]
        for j in range(24):
            kc, part = divmod(j, 3)
            sl = j % 2
            woff, moff = gsl[sl]
            wt("sync", "dv", g_free[sl])
            op("sync", lambda e, kc=kc, part=part, woff=woff: e.dma_start(
                out=stg[:, woff:woff + 1024],
                in_=wqkvT[kc * 128:(kc + 1) * 128, part * 1024:(part + 1) * 1024]),
                inc="ld", k=16)
            lm = op("sync", lambda e, kc=kc, part=part, moff=moff: e.dma_start(
                out=stg[:, moff:moff + 1024],
                in_=mqkvT[kc * 128:(kc + 1) * 128, part * 1024:(part + 1) * 1024]),
                inc="ld", k=16)
            wt("vector", "ld", lm)
            op("vector", lambda e, moff=moff: e.tensor_scalar(
                stg[:, moff:moff + 1024], stg[:, moff:moff + 1024], 0.0, None,
                ALU.is_gt))
            gv = op("vector", lambda e, kc=kc, part=part, woff=woff, moff=moff:
                    e.tensor_tensor(
                        B(WQ + kc * 3072 + part * 1024, 1024),
                        stg[:, woff:woff + 1024], stg[:, moff:moff + 1024],
                        ALU.mult), inc="dv")
            g_free[sl] = gv
        wq_all_ready = gv

        xload = [0, 0]
        xfree = [0, 0]
        xnbf_free = [0, 0]
        ps6_free = 0
        xnT_evac = [0] * G
        xn_w = [0] * G
        qkv_free = [(None, 0)] * 3
        stage_free = [0] * 6
        kvq_w = 0
        xnt_pe_free = 0
        psi = 0

        for g in range(G):
            def emit_ln_stats(t):
                bi = t % 2
                sc = cols[:, 0:8] if t % 2 == 0 else gcols[:, 48:56]
                wt("sync", "dv", xfree[bi])
                xload[bi] = op("sync", lambda e, t=t, bi=bi: e.dma_start(
                    out=xbuf[:, bi * 1024:(bi + 1) * 1024],
                    in_=xc[t * 128:(t + 1) * 128, :]), inc="ld", k=16)
                wt("vector", "ld", xload[bi])
                if t == 0:
                    wt("vector", "dv", v_mem)
                op("vector", lambda e, bi=bi, sc=sc: e.tensor_scalar(
                    B(TRASH, 1024), xbuf[:, bi * 1024:(bi + 1) * 1024], 0.0, None,
                    ALU.add, ALU.add, accum_out=sc[:, 0:1]))
                op("vector", lambda e, bi=bi: e.tensor_tensor(
                    B(TRASH, 1024), xbuf[:, bi * 1024:(bi + 1) * 1024],
                    xbuf[:, bi * 1024:(bi + 1) * 1024], ALU.mult))
                v_var = op("vector", lambda e, sc=sc: e.tensor_scalar(
                    B(TRASH, 1024), B(TRASH, 1024), 0.0, None, ALU.add, ALU.add,
                    accum_out=sc[:, 1:2]), inc="dv")
                wt("scalar", "dv", v_var)
                op("scalar", lambda e, sc=sc: e.mul(sc[:, 2:3], sc[:, 0:1],
                                                    1.0 / 1024.0))
                op("scalar", lambda e, sc=sc: e.activation(
                    sc[:, 4:5], sc[:, 2:3], AF.Identity, scale=sc[:, 2:3]))
                op("scalar", lambda e, sc=sc: e.activation(
                    sc[:, 5:6], sc[:, 4:5], AF.Identity, scale=-1.0,
                    bias=cols[:, 15:16]))
                op("scalar", lambda e, sc=sc: e.mul(sc[:, 3:4], sc[:, 1:2],
                                                    1.0 / 1024.0))
                op("scalar", lambda e, sc=sc: e.activation(
                    sc[:, 6:7], sc[:, 3:4], AF.Ln, bias=sc[:, 5:6]))
                op("scalar", lambda e, sc=sc: e.activation(
                    sc[:, 7:8], sc[:, 6:7], AF.Exp, scale=-0.5))
                return op("scalar", lambda e: e.copy(
                    padsc[:, 0:512], padsc[:, 512:1024]), inc="ac")

            def emit_ln_apply(t, a_rstd):
                nonlocal ps6_free, xnt_pe_free
                bi = t % 2
                su = t % 4
                sc = cols[:, 0:8] if t % 2 == 0 else gcols[:, 48:56]
                wt("vector", "ac", a_rstd)
                wt("vector", "pe", xnbf_free[bi])
                xv = op("vector", lambda e, bi=bi, sc=sc: e.tensor_scalar(
                    B(XNBF[bi], 1024), xbuf[:, bi * 1024:(bi + 1) * 1024],
                    sc[:, 2:3], sc[:, 7:8], ALU.subtract, ALU.mult), inc="dv")
                xfree[bi] = xv
                wt("tensor", "dv", xv)
                wt("tensor", "dv", ps6_free)
                if t == 0:
                    wt("tensor", "ld", ld_id)
                for kc in range(8):
                    tp = op("tensor", lambda e, kc=kc, bi=bi: e.transpose(
                        PS6[:, kc * 128:(kc + 1) * 128],
                        B(XNBF[bi] + kc * 128, 128), idsb[:]),
                        inc="pe" if kc == 7 else None)
                xnbf_free[bi] = tp
                wt("vector", "pe", tp)
                if su == 0:
                    wt("vector", "pe", xnt_pe_free)
                    wt("vector", "lp", xn_w[g - 1] if g > 0 else 0)
                ev = op("vector", lambda e, su=su: e.tensor_copy(
                    xnt[:, :].rearrange("p (k r) -> p k r", r=512)
                    [:, 0:8, su * 128:(su + 1) * 128],
                    PS6[:].rearrange("p (k r) -> p k r", r=128)), inc="dv")
                ps6_free = ev

            prev = None
            for su in range(4):
                t = g * 4 + su
                ar = emit_ln_stats(t)
                if prev is not None:
                    emit_ln_apply(prev[0], prev[1])
                prev = (t, ar)
            emit_ln_apply(prev[0], prev[1])
            xnT_evac[g] = ps6_free
            wt("gpsimd", "dv", xnT_evac[g])
            xn_w[g] = op("gpsimd", lambda e, g=g: e.dma_start(
                out=xn_scr[:, :, g * 512:(g + 1) * 512].rearrange("k p q -> p k q"),
                in_=xnt[:].rearrange("p (k q) -> p k q", k=8)), inc="lp", k=16)

            if g == 0:
                wt("tensor", "dv", wq_all_ready)
            for sect in range(2):
                obase = 1024 if sect == 0 else 0
                for p in range(NP):
                    ps = (PS3, PS4, PS5)[psi % 3]
                    fs, fv_ = qkv_free[psi % 3]
                    if fs:
                        wt("tensor", fs, fv_)
                    if sect == 0 and p == 0:
                        wt("tensor", "dv", xnT_evac[g])
                    for kc in range(8):
                        mm = op("tensor", lambda e, ps=ps, kc=kc, p=p, obase=obase:
                                e.matmul(
                                    ps[:, 0:512],
                                    B(WQ + kc * 3072 + obase + p * 128, 128),
                                    xnt[:, kc * 512:(kc + 1) * 512],
                                    start=(kc == 0), stop=(kc == 7)),
                                inc="pe" if kc == 7 else None)
                    slot = psi % 6
                    wt("scalar", "pe", mm)
                    wt("scalar", "lp", stage_free[slot])
                    if sect == 0:
                        a = op("scalar", lambda e, ps=ps, slot=slot: e.copy(
                            B(STAGE + slot * 512, 512), ps[:, 0:512]), inc="ac")
                    else:
                        a = op("scalar", lambda e, ps=ps, slot=slot: e.mul(
                            B(STAGE + slot * 512, 512), ps[:, 0:512], 0.03125),
                            inc="ac")
                    qkv_free[psi % 3] = ("ac", a)
                    wt("gpsimd", "ac", a)
                    dst = kt_scr if sect == 0 else q_scr
                    w = op("gpsimd", lambda e, dst=dst, p=p, g=g, slot=slot:
                           e.dma_start(
                               out=dst[p, :, g * 512:(g + 1) * 512],
                               in_=B(STAGE + slot * 512, 512)), inc="lp", k=16)
                    stage_free[slot] = w
                    kvq_w = w
                    psi += 1
            for su in range(4):
                c = g * 4 + su
                for half in range(2):
                    ps = (PS3, PS4, PS5)[psi % 3]
                    fs, fv_ = qkv_free[psi % 3]
                    if fs:
                        wt("tensor", fs, fv_)
                    for kc in range(8):
                        mm = op("tensor", lambda e, ps=ps, kc=kc, su=su, half=half:
                                e.matmul(
                                    ps[:, 0:512],
                                    xnt[:, kc * 512 + su * 128: kc * 512 + (su + 1) * 128],
                                    B(WQ + kc * 3072 + 2048 + half * 512, 512),
                                    start=(kc == 0), stop=(kc == 7)),
                                inc="pe" if kc == 7 else None)
                    slot = psi % 6
                    wt("scalar", "pe", mm)
                    wt("scalar", "lp", stage_free[slot])
                    a = op("scalar", lambda e, ps=ps, slot=slot: e.copy(
                        B(STAGE + slot * 512, 512), ps[:, 0:512]), inc="ac")
                    qkv_free[psi % 3] = ("ac", a)
                    wt("gpsimd", "ac", a)
                    for pp in range(4):
                        p = half * 4 + pp
                        w = op("gpsimd", lambda e, p=p, pp=pp, c=c, slot=slot:
                               e.dma_start(
                                   out=v_scr[p, :, c * 128:(c + 1) * 128],
                                   in_=B(STAGE + slot * 512 + pp * 128, 128)),
                               inc="lp", k=16)
                    stage_free[slot] = w
                    kvq_w = w
                    psi += 1
            xnt_pe_free = mm

        pe_p1_end = cnt["pe"]

        g_free2 = [0, 0]
        for kc in range(8):
            sl = kc % 2
            woff, moff = gsl[sl]
            wt("sync", "dv", g_free2[sl] or g_free[sl])
            op("sync", lambda e, kc=kc, woff=woff: e.dma_start(
                out=stg[:, woff:woff + 1024],
                in_=woutT[kc * 128:(kc + 1) * 128, :]), inc="ld", k=16)
            lm = op("sync", lambda e, kc=kc, moff=moff: e.dma_start(
                out=stg[:, moff:moff + 1024],
                in_=moutT[kc * 128:(kc + 1) * 128, :]), inc="ld", k=16)
            wt("vector", "ld", lm)
            if kc == 0:
                wt("vector", "pe", pe_p1_end)
            op("vector", lambda e, moff=moff: e.tensor_scalar(
                stg[:, moff:moff + 1024], stg[:, moff:moff + 1024], 0.0, None,
                ALU.is_gt))
            gv2 = op("vector", lambda e, kc=kc, woff=woff, moff=moff: e.tensor_tensor(
                B(WOUT + kc * 1024, 1024),
                stg[:, woff:woff + 1024], stg[:, moff:moff + 1024], ALU.mult),
                inc="dv")
            g_free2[sl] = gv2
        wout_ready = gv2

        q_ready = [0, 0]
        sslots = [0, 2048]
        s_free = [("dv", g_free2[0]), ("dv", g_free2[1])]
        ps4_free_init = 0
        for wi, (wten, nchunk, width, qdst) in enumerate(
                [(w1T, 8, 4096, w1q), (w2T, 32, 1024, w2q)]):
            cw = min(width, 2048)
            ncc = nchunk * (width // cw)
            for ch in range(ncc):
                kc, part = divmod(ch, width // cw)
                sl = ch % 2
                wt("sync", s_free[sl][0], s_free[sl][1])
                lw = op("sync", lambda e, wten=wten, kc=kc, part=part, sl=sl, cw=cw:
                        e.dma_start(
                            out=stg[:, sslots[sl]:sslots[sl] + cw],
                            in_=wten[kc * 128:(kc + 1) * 128,
                                     part * cw:(part + 1) * cw]),
                        inc="ld", k=16)
                wt("scalar", "ld", lw)
                if ch == 0 and wi == 0:
                    wt("scalar", "pe", pe_p1_end)
                a_abs = op("scalar", lambda e, sl=sl, cw=cw, ch=ch, wi=wi:
                           e.activation(
                               B(ABT, cw), stg[:, sslots[sl]:sslots[sl] + cw],
                               AF.Abs,
                               accum_out=gcols[:, 16 * wi + ch:16 * wi + ch + 1]
                               if wi == 0 else gcols[:, 16 + ch:17 + ch]),
                           inc="ac")
                s_free[sl] = ("ac", a_abs)
            a_pad = op("scalar", lambda e: e.copy(
                padsc[:, 0:512], padsc[:, 512:1024]), inc="ac")
            wt("tensor", "ac", a_pad)
            wt("tensor", "dv", ps4_free_init)
            goff = 0 if wi == 0 else 16
            fm = op("tensor", lambda e, goff=goff, ncc=ncc: e.matmul(
                PS4[0:1, 0:ncc], cols[:, 14:15], gcols[:, goff:goff + ncc],
                start=True, stop=True), inc="pe")
            tcol = 10 + 2 * wi
            wt("vector", "pe", fm)
            op("vector", lambda e, ncc=ncc: e.tensor_reduce(
                cols[0:1, 0:1], PS4[0:1, 0:ncc], mybir.AxisListType.X, ALU.add))
            trd = op("vector", lambda e: e.tensor_copy(
                padsc[:, 0:512], padsc[:, 512:1024]), inc="dv")
            ps4_free_init = trd
            wt("scalar", "dv", trd)
            op("scalar", lambda e, tcol=tcol: e.activation(
                cols[0:1, tcol:tcol + 1], cols[0:1, 0:1], AF.Identity,
                scale=0.5 / NW, bias=cols[0:1, 8:9]))
            op("scalar", lambda e, tcol=tcol: e.activation(
                cols[0:1, tcol + 1:tcol + 2], cols[0:1, 0:1], AF.Identity,
                scale=-0.5 / NW, bias=cols[0:1, 9:10]))
            tv = op("scalar", lambda e: e.copy(
                padsc[:, 0:512], padsc[:, 512:1024]), inc="ac")
            wt("gpsimd", "ac", tv)
            cw_ = op("gpsimd", lambda e, wi=wi, tcol=tcol: e.dma_start(
                out=cells_scr[0:1, 2 * wi:2 * wi + 2],
                in_=cols[0:1, tcol:tcol + 2]), inc="lp", k=16)
            wt("sync", "lp", cw_)
            tbc = op("sync", lambda e, wi=wi, tcol=tcol: e.dma_start(
                out=cols[:, tcol:tcol + 2],
                in_=cells_scr[0:1, 2 * wi:2 * wi + 2]
                .partition_broadcast(128).squeeze(1)), inc="ld", k=16)
            for ch in range(ncc):
                kc, part = divmod(ch, width // cw)
                sl = ch % 2
                wt("sync", s_free[sl][0], s_free[sl][1])
                lw = op("sync", lambda e, wten=wten, kc=kc, part=part, sl=sl, cw=cw:
                        e.dma_start(
                            out=stg[:, sslots[sl]:sslots[sl] + cw],
                            in_=wten[kc * 128:(kc + 1) * 128,
                                     part * cw:(part + 1) * cw]),
                        inc="ld", k=16)
                wt("vector", "ld", lw)
                if ch == 0:
                    wt("vector", "ld", tbc)
                    if wi == 0:
                        wt("vector", "lp", kvq_w)
                op("vector", lambda e, sl=sl, tcol=tcol, cw=cw: e.tensor_scalar(
                    B(QPOS, cw), stg[:, sslots[sl]:sslots[sl] + cw],
                    cols[:, tcol:tcol + 1], None, ALU.is_ge))
                op("vector", lambda e, sl=sl, tcol=tcol, cw=cw: e.tensor_scalar(
                    B(QNEG, cw), stg[:, sslots[sl]:sslots[sl] + cw],
                    cols[:, tcol + 1:tcol + 2], None, ALU.is_le))
                qv = op("vector", lambda e, qdst=qdst, kc=kc, part=part, cw=cw,
                        width=width: e.tensor_tensor(
                            qdst[:, kc * width + part * cw:
                                 kc * width + (part + 1) * cw],
                            B(QPOS, cw), B(QNEG, cw), ALU.subtract),
                        inc="dv")
                s_free[sl] = ("dv", qv)
            q_ready[wi] = qv

        PS6F = PS6[:, 0:1024].bitcast(F32)
        ps12 = [PS1, PS2]
        ss = dict(
            kt_free=0, v_free=0, qt_free=0,
            v_load=0, qt_load=0, kt_load=0, ktb_load=0,
            pt_pv_free=[0, 0, 0, 0], pt_tree_free=[0, 0, 0, 0],
            ps12_free=[0, 0], ps12_dv_free=0,
            att_free=0, psat_free=[0, 0],
            ps4_free_dv=ps4_free_init, tree_slot_free=0, exp_last=0,
            s_w_of=[0] * G, sslot_free=[q_ready[1], q_ready[1]],
            x_st=[0, 0], x_free=[xfree[0], xfree[1]],
            ps3_free=("dv", q_ready[1]),
            ht_reader_pe=0, xnt_reader_pe=pe_p1_end,
            pending_tail=[], rv=0, d1=0, rb=0, att_last_nv=0,
        )
        tree_pool = [list(range(6)), list(range(6))]
        slot_guard = [[0] * 6, [0] * 6]

        def ffn_round_gen(g):
            wt("sync", "lp", xn_w[g])
            wt("sync", "pe", ss["xnt_reader_pe"])
            xl = op("sync", lambda e, g=g: e.dma_start(
                out=xnt[:].rearrange("p (k q) -> p k q", k=8),
                in_=xn_scr[:, :, g * 512:(g + 1) * 512]
                .rearrange("k p q -> p k q")), inc="ld", k=16)
            yield
            last_f1 = 0
            ha = 0
            for fc in range(32):
                fs, fv_ = ss["ps3_free"]
                if fs:
                    wt("tensor", fs, fv_)
                if fc == 0:
                    wt("tensor", "ld", xl)
                    wt("tensor", "dv", q_ready[0])
                    wt("tensor", "pe", ss["ht_reader_pe"])
                for kc in range(8):
                    fmm = op("tensor", lambda e, kc=kc, fc=fc: e.matmul(
                        PS3[:, 0:512],
                        w1q[:, kc * 4096 + fc * 128: kc * 4096 + (fc + 1) * 128],
                        xnt[:, kc * 512:(kc + 1) * 512],
                        start=(kc == 0), stop=(kc == 7)),
                        inc="pe" if kc == 7 else None)
                last_f1 = fmm
                wt("vector", "pe", fmm)
                ha = op("vector", lambda e, fc=fc: e.tensor_scalar(
                    B(HT + fc * 512, 512), PS3[:, 0:512],
                    b1c[:, fc:fc + 1], 0.0, ALU.add, ALU.max), inc="dv")
                ss["ps3_free"] = ("dv", ha)
                yield
            ss["xnt_reader_pe"] = last_f1
            ht_ready_dv = ha
            last_f2 = 0
            for su in range(4):
                t = g * 4 + su
                bi = su % 2
                wt("sync", "lp", ss["s_w_of"][g])
                wt("sync", "dv", ss["sslot_free"][bi])
                sl_ = op("sync", lambda e, t=t, bi=bi: e.dma_start(
                    out=stg[:, 2048 + bi * 1024: 2048 + (bi + 1) * 1024],
                    in_=s_scr[t * 128:(t + 1) * 128, :]), inc="ld", k=16)
                wt("sync", "st", ss["x_st"][bi])
                wt("sync", "dv", ss["x_free"][bi])
                ss["x_free"][bi] = 0
                xl_ = op("sync", lambda e, t=t, bi=bi: e.dma_start(
                    out=xbuf[:, bi * 1024:(bi + 1) * 1024],
                    in_=xc[t * 128:(t + 1) * 128, :]), inc="ld", k=16)
                yield
                fv2 = 0
                gm = 0
                for half in range(2):
                    for blk in range(4):
                        if blk == 0:
                            fs, fv_ = ss["ps3_free"]
                            if fs:
                                wt("tensor", fs, fv_)
                        if su == 0 and half == 0 and blk == 0:
                            wt("tensor", "dv", ht_ready_dv)
                            wt("tensor", "dv", q_ready[1])
                        for fc in range(blk * 8, (blk + 1) * 8):
                            gm = op("tensor", lambda e, fc=fc, su=su, half=half:
                                    e.matmul(
                                        PS3[:, 0:512],
                                        B(HT + fc * 512 + su * 128, 128),
                                        w2q[:, fc * 1024 + half * 512:
                                            fc * 1024 + (half + 1) * 512],
                                        start=(fc == 0), stop=(fc == 31)),
                                    inc="pe" if fc == 31 else None)
                        yield
                    last_f2 = gm
                    wt("vector", "pe", gm)
                    fv2 = op("vector", lambda e, half=half: e.tensor_tensor(
                        stg[:, 1024 + half * 512:1024 + (half + 1) * 512],
                        PS3[:, 0:512], rows[:, half * 512:(half + 1) * 512],
                        ALU.add), inc="dv")
                    ss["ps3_free"] = ("dv", fv2)
                    yield
                wt("scalar", "dv", fv2)
                th = op("scalar", lambda e: e.activation(
                    stg[:, 0:1024], stg[:, 1024:2048], AF.Tanh, scale=0.5),
                    inc="ac")
                wt("vector", "ac", th)
                op("vector", lambda e: e.tensor_scalar(
                    stg[:, 0:1024], stg[:, 0:1024], 0.5, 0.5, ALU.mult, ALU.add))
                op("vector", lambda e: e.tensor_tensor(
                    stg[:, 1024:2048], stg[:, 1024:2048], stg[:, 0:1024],
                    ALU.mult))
                wt("vector", "ld", max(sl_, xl_))
                sv2 = op("vector", lambda e, bi=bi: e.tensor_tensor(
                    stg[:, 1024:2048],
                    stg[:, 2048 + bi * 1024: 2048 + (bi + 1) * 1024],
                    stg[:, 1024:2048], ALU.mult), inc="dv")
                ss["sslot_free"][bi] = sv2
                ov = op("vector", lambda e, bi=bi: e.tensor_tensor(
                    xbuf[:, bi * 1024:(bi + 1) * 1024],
                    xbuf[:, bi * 1024:(bi + 1) * 1024], stg[:, 1024:2048],
                    ALU.add), inc="dv")
                wt("sync", "dv", ov)
                ss["x_st"][bi] = op("sync", lambda e, t=t, bi=bi: e.dma_start(
                    out=yc[t * 128:(t + 1) * 128, :],
                    in_=xbuf[:, bi * 1024:(bi + 1) * 1024]), inc="st", k=16)
                yield
            ss["ht_reader_pe"] = last_f2

        wt("tensor", "dv", v_mem)
        wt("tensor", "ac", cnt["ac"])
        for g in range(G):
            fq = ffn_round_gen(g - 1) if g > 0 else iter(())

            def pop(n=1, fq=fq):
                for _ in range(n):
                    if next(fq, "END") == "END":
                        break
            for p in range(NP):
                ko = KT
                psat = PS5[:, 0:512] if p % 2 == 0 else PS6F
                if g == 0 and p == 0:
                    wt("sync", "lp", kvq_w)
                    wt("gpsimd", "lp", kvq_w)
                wt("sync", "pe", ss["qt_free"])
                l3 = op("sync", lambda e, p=p, g=g: e.dma_start(
                    out=B(QT, 512), in_=q_scr[p, :, g * 512:(g + 1) * 512]),
                    inc="ld", k=16)
                wt("sync", "pe", ss["kt_free"])
                l1 = op("sync", lambda e, p=p, ko=ko: e.dma_start(
                    out=B(ko, 2048), in_=kt_scr[p, :, 0:2048]), inc="ld", k=16)
                wt("gpsimd", "pe", ss["kt_free"])
                l1b = op("gpsimd", lambda e, p=p, ko=ko: e.dma_start(
                    out=B(ko + 2048, 2048), in_=kt_scr[p, :, 2048:4096]),
                    inc="lp", k=16)
                wt("gpsimd", "pe", ss["v_free"])
                l2 = op("gpsimd", lambda e, p=p: e.dma_start(
                    out=B(VB, 4096), in_=v_scr[p, :, :]), inc="lp", k=16)
                ss["kt_load"] = l1
                ss["ktb_load"] = l1b
                ss["v_load"] = l2
                ss["qt_load"] = l3
                tree_stack = [[], []]
                troot = [None, None]
                ax_of = [0] * NC
                last_pv = 0
                dvv = 0

                def emit_pv(c, p=p, psat=psat):
                    nonlocal last_pv
                    wt("tensor", "ac", ax_of[c])
                    if c == 0:
                        wt("tensor", "lp", ss["v_load"])
                        wt("tensor", "dv", ss["psat_free"][p % 2])
                    for h in range(2):
                        pv = op("tensor", lambda e, c=c, ci=c % 4, h=h, psat=psat:
                                e.matmul(
                                    psat[h * 64:(h + 1) * 64, :],
                                    B(VB + c * 128 + h * 64, 64),
                                    B(PT[ci] + h * 512, 512),
                                    start=(c == 0), stop=(c == NC - 1)),
                                inc="pe" if h == 1 else None)
                    ss["pt_pv_free"][c % 4] = pv
                    last_pv = pv

                def emit_scores(c):
                    ci = c % 2
                    ps = ps12[ci]
                    if c < 2:
                        wt("tensor", "ld", max(ss["kt_load"], ss["qt_load"]))
                        wt("tensor", "dv", ss["ps12_dv_free"])
                    if c == 14 or c == 15:
                        wt("tensor", "lp", ss["ktb_load"])
                    wt("tensor", "ac", ss["ps12_free"][ci])
                    op("tensor", lambda e, ps=ps, ko=ko, c=c: e.matmul(
                        ps[:, 0:512],
                        big[0:64, ko + c * 128: ko + (c + 1) * 128],
                        big[0:64, QT:QT + 512], start=True, stop=True))
                    return op("tensor", lambda e, ps=ps, ko=ko, c=c: e.matmul(
                        ps[:, 512:1024],
                        big[64:128, ko + c * 128: ko + (c + 1) * 128],
                        big[64:128, QT:QT + 512], start=True, stop=True),
                        inc="pe")

                def emit_exp(c, sm):
                    ci = c % 2
                    ps = ps12[ci]
                    wt("scalar", "pe", sm)
                    wt("scalar", "pe", ss["pt_pv_free"][c % 4])
                    wt("scalar", "dv", ss["pt_tree_free"][c % 4])
                    ax = op("scalar", lambda e, ps=ps, c4=c % 4: e.activation(
                        B(PT[c4], 1024), ps[:, 0:1024], AF.Exp), inc="ac")
                    ax_of[c] = ax
                    ss["ps12_free"][ci] = ax
                    ss["exp_last"] = ax

                def emit_tree(c):
                    nonlocal dvv
                    wt("vector", "ac", ax_of[c])
                    for h in range(2):
                        slot = tree_pool[h].pop(0)
                        if slot_guard[h][slot]:
                            wt("vector", "pe", slot_guard[h][slot])
                            slot_guard[h][slot] = 0
                        dvv = op("vector", lambda e, h=h, slot=slot,
                                 ca=(c - 1) % 4, cb=c % 4:
                                 e.tensor_tensor(
                                     B(TREE + (h * 6 + slot) * 512, 512),
                                     B(PT[ca] + h * 512, 512),
                                     B(PT[cb] + h * 512, 512), ALU.add),
                                 inc="dv")
                        lev = 1
                        while tree_stack[h] and tree_stack[h][-1][0] == lev:
                            _, pslot = tree_stack[h].pop()
                            dvv = op("vector",
                                     lambda e, h=h, slot=slot, pslot=pslot:
                                     e.tensor_tensor(
                                         B(TREE + (h * 6 + slot) * 512, 512),
                                         B(TREE + (h * 6 + slot) * 512, 512),
                                         B(TREE + (h * 6 + pslot) * 512, 512),
                                         ALU.add), inc="dv")
                            tree_pool[h].append(pslot)
                            lev += 1
                        tree_stack[h].append((lev, slot))
                        if c == NC - 1:
                            troot[h] = (tree_stack[h][-1][1], dvv)
                    ss["pt_tree_free"][(c - 1) % 4] = dvv
                    ss["pt_tree_free"][c % 4] = dvv

                def make_tail(p=p, psat=psat, troot_l=None, lpv=None):
                    def t1():
                        wt("tensor", "dv", troot_l[1][1])
                        wt("tensor", "dv", ss["ps4_free_dv"])
                        op("tensor", lambda e, sl=troot_l[0][0]: e.matmul(
                            PS4[0:1, 0:512], onesb[:], B(TREE + sl * 512, 512),
                            start=True, stop=True))
                        d1 = op("tensor", lambda e, sl=troot_l[1][0]: e.matmul(
                            PS4[64:65, 0:512], onesb[:],
                            B(TREE + (6 + sl) * 512, 512),
                            start=True, stop=True), inc="pe")
                        ss["d1"] = d1
                        for h in range(2):
                            rslot = troot_l[h][0]
                            tree_pool[h].append(rslot)
                            slot_guard[h][rslot] = d1

                    def _rec(e, off, row):
                        with nc.allow_low_precision(reason="recip row bf16"):
                            return e.reciprocal(big[0:1, off:off + 512],
                                                PS4[row:row + 1, 0:512])

                    def t2():
                        wt("vector", "pe", ss["d1"])
                        op("vector", lambda e: _rec(e, RRA, 0))

                    def t3():
                        op("vector", lambda e: _rec(e, RRB, 64))
                        ss["rv"] = op("vector", lambda e: e.tensor_copy(
                            padsc[:, 0:512], padsc[:, 512:1024]), inc="dv")

                    def t4():
                        wt("tensor", "dv", ss["rv"])
                        op("tensor", lambda e: e.matmul(
                            PS4[0:64, 0:512], onesrb[:], big[0:1, RRA:RRA + 512],
                            start=True, stop=True))
                        ss["rb"] = op("tensor", lambda e: e.matmul(
                            PS4[64:128, 0:512], onesrb[:],
                            big[0:1, RRB:RRB + 512],
                            start=True, stop=True), inc="pe")

                    def t5():
                        wt("vector", "pe", ss["rb"])
                        rc = op("vector", lambda e: e.tensor_copy(
                            B(RSB, 512), PS4[:, 0:512]), inc="dv")
                        ss["ps4_free_dv"] = rc
                        wt("vector", "pe", lpv[0])
                        wt("vector", "pe", ss["att_free"])
                        nv = op("vector", lambda e, p=p, psat=psat:
                                e.tensor_tensor(
                                    B(ATT + p * 512, 512), psat, B(RSB, 512),
                                    ALU.mult), inc="dv")
                        ss["psat_free"][p % 2] = nv
                        ss["att_last_nv"] = nv
                    return [t1, t2, t3, t4, t5]

                TAIL_STEPS = {2: 0, 6: 1, 12: 2, 16: 3, 20: 4}
                for c2 in range(0, NC, 2):
                    smA = emit_scores(c2)
                    smB = emit_scores(c2 + 1)
                    sm = smB
                    if c2 >= 2:
                        emit_pv(c2 - 2)
                        emit_pv(c2 - 1)
                    if c2 in TAIL_STEPS and ss["pending_tail"]:
                        ss["pending_tail"][TAIL_STEPS[c2]]()
                    if (c2 // 2) % 3 != 0:
                        pop(1)
                    emit_exp(c2, smA)
                    emit_exp(c2 + 1, smB)
                    emit_tree(c2 + 1)
                emit_pv(NC - 2)
                emit_pv(NC - 1)
                lpv = [last_pv]
                stages = make_tail(troot_l=troot, lpv=lpv)
                if p == NP - 1:
                    for f in stages:
                        f()
                    ss["pending_tail"] = []
                else:
                    ss["pending_tail"] = stages
                ss["kt_free"] = sm
                ss["v_free"] = last_pv
                ss["qt_free"] = sm
                pop(2)
            att_ready = ss["att_last_nv"]
            pop(200)
            s_slots = [2048, 3072]
            for su in range(4):
                if su == 0:
                    wt("tensor", "dv", att_ready)
                    wt("tensor", "dv", wout_ready)
                wt("tensor", "ac", ss["exp_last"])
                wt("tensor", "dv", ss["ps12_dv_free"])
                for p_ in range(NP):
                    for half in range(2):
                        om = op("tensor", lambda e, p_=p_, su=su, half=half:
                                e.matmul(
                                    PS1[:, half * 512:(half + 1) * 512],
                                    B(ATT + p_ * 512 + su * 128, 128),
                                    B(WOUT + p_ * 1024 + half * 512, 512),
                                    start=(p_ == 0), stop=(p_ == NP - 1)),
                                inc="pe" if (p_ == NP - 1 and half == 1) else None)
                wt("vector", "pe", om)
                wt("vector", "dv", ss["sslot_free"][su % 2])
                sv = op("vector", lambda e, su=su: e.tensor_tensor(
                    stg[:, s_slots[su % 2]:s_slots[su % 2] + 1024],
                    PS1[:, 0:1024], rows[:, 1024:2048], ALU.add), inc="dv")
                ss["ps12_dv_free"] = sv
                ss["sslot_free"][su % 2] = sv
                wt("gpsimd", "dv", sv)
                sw = op("gpsimd", lambda e, g=g, su=su: e.dma_start(
                    out=s_scr[(g * 4 + su) * 128:(g * 4 + su + 1) * 128, :],
                    in_=stg[:, s_slots[su % 2]:s_slots[su % 2] + 1024]),
                    inc="lp", k=16)
                ss["s_w_of"][g] = sw
            ss["att_free"] = om
        fq = ffn_round_gen(G - 1)
        for _ in range(500):
            if next(fq, "END") == "END":
                break

        wt("sync", "st", cnt["st"])

        @block.sync
        def _(eng):
            for f in streams["sync"]:
                f(eng)

        @block.gpsimd
        def _(eng):
            for f in streams["gpsimd"]:
                f(eng)

        @block.tensor
        def _(eng):
            for f in streams["tensor"]:
                f(eng)

        @block.vector
        def _(eng):
            for f in streams["vector"]:
                f(eng)

        @block.scalar
        def _(eng):
            for f in streams["scalar"]:
                f(eng)

    return nc


def kernel(**inputs):
    x = np.ascontiguousarray(np.asarray(inputs["L0_attn_x"], np.float32))
    C = lambda a: np.ascontiguousarray(np.asarray(a, np.float32))
    shared = dict(
        wqkvT=C(np.asarray(inputs["L0_attn_qkv_weight"]).T),
        mqkvT=C(np.asarray(inputs["L0_attn_qkv_weight_mask"]).T),
        woutT=C(np.asarray(inputs["L0_attn_out_weight"]).T),
        moutT=C(np.asarray(inputs["L0_attn_out_weight_mask"]).T),
        w1T=C(np.asarray(inputs["L0_ffn_W1"]).T),
        w2T=C(np.asarray(inputs["L0_ffn_W2"]).T),
        b1c=C(np.asarray(inputs["L0_ffn_b1"]).reshape(32, 128).T),
        b2r=C(np.asarray(inputs["L0_ffn_b2"]).reshape(1, D)),
        obr=C(np.asarray(inputs["L0_attn_out_bias"]).reshape(1, D)),
        obm=C(np.asarray(inputs["L0_attn_out_bias_mask"]).reshape(1, D)),
        iden=np.eye(128).astype(ml_dtypes.bfloat16),
    )
    nc = build_nc()
    in_maps = [dict(xc=np.ascontiguousarray(x[c]), **shared) for c in range(8)]
    res = run_bass_kernel_spmd(nc, in_maps, core_ids=list(range(8)))
    return np.stack([np.asarray(r["yc"], np.float32) for r in res.results], axis=0)


# version 13
# speedup vs baseline: 1.4864x; 1.0700x over previous
import numpy as np
import ml_dtypes
import contextlib
import concourse.bass as bass
import concourse.mybir as mybir
from concourse.bass_utils import run_bass_kernel_spmd

F32 = mybir.dt.float32
BF16 = mybir.dt.bfloat16
FP8 = mybir.dt.float8e4
AF = mybir.ActivationFunctionType
ALU = mybir.AluOpType

S, D, NO, FF = 4096, 1024, 3072, 4096
G, NP, NC = 8, 8, 32
NW = 4194304.0
EPS = 1e-5
ENGS = ("sync", "gpsimd", "tensor", "vector", "scalar")

WQ = 0
KT = 0
VB = 4096
QT = 8192
WOUT = 8704
PT = (16896, 17920, 18944, 19968)
ATT = 20992
TREE = 25088
HT = 31232
RRA, RRB = 47616, 48128
RSB = 48640
XNBF = (25088, 26112)
TRASH = 27136
STAGE = 28160
ABT = 25088
QPOS, QNEG = 31232, 33280
BIGN = 50176


def build_nc():
    nc = bass.Bass("TRN2", target_bir_lowering=False)
    I = lambda n, s, d: nc.dram_tensor(n, s, d, kind="ExternalInput").ap()
    xc = I("xc", [S, D], F32)
    wqkvT = I("wqkvT", [D, NO], F32)
    mqkvT = I("mqkvT", [D, NO], F32)
    woutT = I("woutT", [D, D], F32)
    moutT = I("moutT", [D, D], F32)
    w1T = I("w1T", [D, FF], F32)
    w2T = I("w2T", [FF, D], F32)
    b1cI = I("b1c", [128, 32], F32)
    b2r = I("b2r", [1, D], F32)
    obr = I("obr", [1, D], F32)
    obm = I("obm", [1, D], F32)
    iden = I("iden", [128, 128], BF16)
    yc = nc.dram_tensor("yc", [S, D], F32, kind="ExternalOutput").ap()
    kt_scr = nc.dram_tensor("kt_scr", [NP, 128, S], BF16, kind="Internal").ap()
    v_scr = nc.dram_tensor("v_scr", [NP, 128, S], BF16, kind="Internal").ap()
    q_scr = nc.dram_tensor("q_scr", [NP, 128, S], BF16, kind="Internal").ap()
    xn_scr = nc.dram_tensor("xn_scr", [8, 128, S], BF16, kind="Internal").ap()
    s_scr = nc.dram_tensor("s_scr", [S, D], F32, kind="Internal").ap()
    cells_scr = nc.dram_tensor("cells_scr", [1, 4], F32, kind="Internal").ap()

    with contextlib.ExitStack() as est:
        E = est.enter_context
        big = E(nc.sbuf_tensor("big", [128, BIGN], BF16))
        w1q = E(nc.sbuf_tensor("w1q", [128, 32768], FP8))
        w2q = E(nc.sbuf_tensor("w2q", [128, 32768], FP8))
        stg = E(nc.sbuf_tensor("stg", [128, 4096], F32))
        xbuf = E(nc.sbuf_tensor("xbuf", [128, 2048], F32))
        rows = E(nc.sbuf_tensor("rows", [128, 2048], F32))
        xnt = E(nc.sbuf_tensor("xnt", [128, 4096], BF16))
        cols = E(nc.sbuf_tensor("cols", [128, 16], F32))
        gcols = E(nc.sbuf_tensor("gcols", [128, 64], F32))
        b1c = E(nc.sbuf_tensor("b1cs", [128, 32], F32))
        onesb = E(nc.sbuf_tensor("onesb", [128, 1], BF16))
        onesrow = E(nc.sbuf_tensor("onesrow", [1, 128], F32))
        onesrb = E(nc.sbuf_tensor("onesrb", [1, 64], BF16))
        idsb = E(nc.sbuf_tensor("idsb", [128, 128], BF16))
        padsc = E(nc.sbuf_tensor("padsc", [128, 1024], BF16))
        qtb = E(nc.sbuf_tensor("qtb", [128, 1024], BF16))
        PS1 = E(nc.psum_tensor("PS1", [128, 1024], F32))
        PS2 = E(nc.psum_tensor("PS2", [128, 1024], F32))
        PS3 = E(nc.psum_tensor("PS3", [128, 512], F32))
        PS4 = E(nc.psum_tensor("PS4", [128, 512], F32))
        PS5 = E(nc.psum_tensor("PS5", [128, 512], F32))
        PS6 = E(nc.psum_tensor("PS6", [128, 1024], BF16))
        sem = {n: E(nc.semaphore(n)) for n in ("ld", "lp", "pe", "ac", "dv", "st")}
        block = E(nc.Block())

        streams = {e: [] for e in ENGS}
        cnt = dict(ld=0, lp=0, pe=0, ac=0, dv=0, st=0)

        def wt(e, s_, v):
            if v and v > 0:
                streams[e].append(lambda eng, ss=sem[s_], vv=v: eng.wait_ge(ss, vv))

        def op(e, f, inc=None, k=1):
            if inc is None:
                streams[e].append(f)
                return None
            cnt[inc] += k
            streams[e].append(lambda eng, f=f, ss=sem[inc], kk=k: f(eng).then_inc(ss, kk))
            return cnt[inc]

        def B(lo, n):
            return big[:, lo:lo + n]

        op("vector", lambda e: e.memset(onesb[:], 1.0))
        op("vector", lambda e: e.memset(onesrow[:], 1.0))
        op("vector", lambda e: e.memset(onesrb[:], 1.0))
        op("vector", lambda e: e.memset(cols[:, 14:15], 1.0))
        op("vector", lambda e: e.memset(cols[:, 15:16], EPS))
        op("vector", lambda e: e.memset(cols[:, 8:9], 0.5 * EPS))
        v_mem = op("vector", lambda e: e.memset(cols[:, 9:10], -0.5 * EPS), inc="dv")
        ld_id = op("sync", lambda e: e.dma_start(out=idsb[:], in_=iden), inc="ld", k=16)
        op("sync", lambda e: e.dma_start(out=b1c[:], in_=b1cI), inc="ld", k=16)
        op("sync", lambda e: e.dma_start(
            out=rows[:, 0:1024], in_=b2r.partition_broadcast(128).squeeze(1)),
            inc="ld", k=16)
        op("sync", lambda e: e.dma_start(
            out=rows[:, 1024:2048], in_=obr.partition_broadcast(128).squeeze(1)),
            inc="ld", k=16)
        ld_obm = op("sync", lambda e: e.dma_start(
            out=stg[:, 0:1024], in_=obm.partition_broadcast(128).squeeze(1)),
            inc="ld", k=16)
        wt("vector", "ld", ld_obm)
        _m = op("vector", lambda e: e.tensor_scalar(
            stg[:, 0:1024], stg[:, 0:1024], 0.0, None, ALU.is_gt), inc="dv")
        wt("vector", "dv", _m)
        v_ob = op("vector", lambda e: e.tensor_tensor(
            rows[:, 1024:2048], rows[:, 1024:2048], stg[:, 0:1024], ALU.mult),
            inc="dv")

        gsl = [(0, 1024), (2048, 3072)]
        g_free = [v_ob, v_ob]
        for j in range(24):
            kc, part = divmod(j, 3)
            sl = j % 2
            woff, moff = gsl[sl]
            wt("sync", "dv", g_free[sl])
            op("sync", lambda e, kc=kc, part=part, woff=woff: e.dma_start(
                out=stg[:, woff:woff + 1024],
                in_=wqkvT[kc * 128:(kc + 1) * 128, part * 1024:(part + 1) * 1024]),
                inc="ld", k=16)
            lm = op("sync", lambda e, kc=kc, part=part, moff=moff: e.dma_start(
                out=stg[:, moff:moff + 1024],
                in_=mqkvT[kc * 128:(kc + 1) * 128, part * 1024:(part + 1) * 1024]),
                inc="ld", k=16)
            wt("vector", "ld", lm)
            _m = op("vector", lambda e, moff=moff: e.tensor_scalar(
                stg[:, moff:moff + 1024], stg[:, moff:moff + 1024], 0.0, None,
                ALU.is_gt), inc="dv")
            wt("vector", "dv", _m)
            gv = op("vector", lambda e, kc=kc, part=part, woff=woff, moff=moff:
                    e.tensor_tensor(
                        B(WQ + kc * 3072 + part * 1024, 1024),
                        stg[:, woff:woff + 1024], stg[:, moff:moff + 1024],
                        ALU.mult), inc="dv")
            g_free[sl] = gv
        wq_all_ready = gv

        xload = [0, 0]
        xfree = [0, 0]
        xnbf_free = [0, 0]
        ps6_free = 0
        xnT_evac = [0] * G
        xn_w = [0] * G
        qkv_free = [(None, 0)] * 3
        stage_free = [0] * 6
        kvq_w = 0
        xnt_pe_free = 0
        psi = 0

        for g in range(G):
            def emit_ln_stats(t):
                bi = t % 2
                sc = cols[:, 0:8] if t % 2 == 0 else gcols[:, 48:56]
                wt("sync", "dv", xfree[bi])
                xload[bi] = op("sync", lambda e, t=t, bi=bi: e.dma_start(
                    out=xbuf[:, bi * 1024:(bi + 1) * 1024],
                    in_=xc[t * 128:(t + 1) * 128, :]), inc="ld", k=16)
                wt("vector", "ld", xload[bi])
                if t == 0:
                    wt("vector", "dv", v_mem)
                op("vector", lambda e, bi=bi, sc=sc: e.tensor_scalar(
                    B(TRASH, 1024), xbuf[:, bi * 1024:(bi + 1) * 1024], 0.0, None,
                    ALU.add, ALU.add, accum_out=sc[:, 0:1]))
                op("vector", lambda e, bi=bi: e.tensor_tensor(
                    B(TRASH, 1024), xbuf[:, bi * 1024:(bi + 1) * 1024],
                    xbuf[:, bi * 1024:(bi + 1) * 1024], ALU.mult))
                v_var = op("vector", lambda e, sc=sc: e.tensor_scalar(
                    B(TRASH, 1024), B(TRASH, 1024), 0.0, None, ALU.add, ALU.add,
                    accum_out=sc[:, 1:2]), inc="dv")
                wt("scalar", "dv", v_var)
                op("scalar", lambda e, sc=sc: e.mul(sc[:, 2:3], sc[:, 0:1],
                                                    1.0 / 1024.0))
                op("scalar", lambda e, sc=sc: e.activation(
                    sc[:, 4:5], sc[:, 2:3], AF.Identity, scale=sc[:, 2:3]))
                op("scalar", lambda e, sc=sc: e.activation(
                    sc[:, 5:6], sc[:, 4:5], AF.Identity, scale=-1.0,
                    bias=cols[:, 15:16]))
                op("scalar", lambda e, sc=sc: e.mul(sc[:, 3:4], sc[:, 1:2],
                                                    1.0 / 1024.0))
                op("scalar", lambda e, sc=sc: e.activation(
                    sc[:, 6:7], sc[:, 3:4], AF.Ln, bias=sc[:, 5:6]))
                op("scalar", lambda e, sc=sc: e.activation(
                    sc[:, 7:8], sc[:, 6:7], AF.Exp, scale=-0.5))
                return op("scalar", lambda e: e.copy(
                    padsc[:, 0:512], padsc[:, 512:1024]), inc="ac")

            def emit_ln_apply(t, a_rstd):
                nonlocal ps6_free, xnt_pe_free
                bi = t % 2
                su = t % 4
                sc = cols[:, 0:8] if t % 2 == 0 else gcols[:, 48:56]
                wt("vector", "ac", a_rstd)
                wt("vector", "pe", xnbf_free[bi])
                xv = op("vector", lambda e, bi=bi, sc=sc: e.tensor_scalar(
                    B(XNBF[bi], 1024), xbuf[:, bi * 1024:(bi + 1) * 1024],
                    sc[:, 2:3], sc[:, 7:8], ALU.subtract, ALU.mult), inc="dv")
                xfree[bi] = xv
                wt("tensor", "dv", xv)
                wt("tensor", "dv", ps6_free)
                if t == 0:
                    wt("tensor", "ld", ld_id)
                for kc in range(8):
                    tp = op("tensor", lambda e, kc=kc, bi=bi: e.transpose(
                        PS6[:, kc * 128:(kc + 1) * 128],
                        B(XNBF[bi] + kc * 128, 128), idsb[:]),
                        inc="pe" if kc == 7 else None)
                xnbf_free[bi] = tp
                wt("vector", "pe", tp)
                if su == 0:
                    wt("vector", "pe", xnt_pe_free)
                    wt("vector", "lp", xn_w[g - 1] if g > 0 else 0)
                ev = op("vector", lambda e, su=su: e.tensor_copy(
                    xnt[:, :].rearrange("p (k r) -> p k r", r=512)
                    [:, 0:8, su * 128:(su + 1) * 128],
                    PS6[:].rearrange("p (k r) -> p k r", r=128)), inc="dv")
                ps6_free = ev

            prev = None
            for su in range(4):
                t = g * 4 + su
                ar = emit_ln_stats(t)
                if prev is not None:
                    emit_ln_apply(prev[0], prev[1])
                prev = (t, ar)
            emit_ln_apply(prev[0], prev[1])
            xnT_evac[g] = ps6_free
            wt("gpsimd", "dv", xnT_evac[g])
            xn_w[g] = op("gpsimd", lambda e, g=g: e.dma_start(
                out=xn_scr[:, :, g * 512:(g + 1) * 512].rearrange("k p q -> p k q"),
                in_=xnt[:].rearrange("p (k q) -> p k q", k=8)), inc="lp", k=16)

            if g == 0:
                wt("tensor", "dv", wq_all_ready)
            for sect in range(2):
                obase = 1024 if sect == 0 else 0
                for p in range(NP):
                    ps = (PS3, PS4, PS5)[psi % 3]
                    fs, fv_ = qkv_free[psi % 3]
                    if fs:
                        wt("tensor", fs, fv_)
                    if sect == 0 and p == 0:
                        wt("tensor", "dv", xnT_evac[g])
                    for kc in range(8):
                        mm = op("tensor", lambda e, ps=ps, kc=kc, p=p, obase=obase:
                                e.matmul(
                                    ps[:, 0:512],
                                    B(WQ + kc * 3072 + obase + p * 128, 128),
                                    xnt[:, kc * 512:(kc + 1) * 512],
                                    start=(kc == 0), stop=(kc == 7)),
                                inc="pe" if kc == 7 else None)
                    slot = psi % 6
                    wt("scalar", "pe", mm)
                    wt("scalar", "lp", stage_free[slot])
                    if sect == 0:
                        a = op("scalar", lambda e, ps=ps, slot=slot: e.copy(
                            B(STAGE + slot * 512, 512), ps[:, 0:512]), inc="ac")
                    else:
                        a = op("scalar", lambda e, ps=ps, slot=slot: e.mul(
                            B(STAGE + slot * 512, 512), ps[:, 0:512], 0.03125),
                            inc="ac")
                    qkv_free[psi % 3] = ("ac", a)
                    wt("gpsimd", "ac", a)
                    dst = kt_scr if sect == 0 else q_scr
                    w = op("gpsimd", lambda e, dst=dst, p=p, g=g, slot=slot:
                           e.dma_start(
                               out=dst[p, :, g * 512:(g + 1) * 512],
                               in_=B(STAGE + slot * 512, 512)), inc="lp", k=16)
                    stage_free[slot] = w
                    kvq_w = w
                    psi += 1
            for su in range(4):
                c = g * 4 + su
                for half in range(2):
                    ps = (PS3, PS4, PS5)[psi % 3]
                    fs, fv_ = qkv_free[psi % 3]
                    if fs:
                        wt("tensor", fs, fv_)
                    for kc in range(8):
                        mm = op("tensor", lambda e, ps=ps, kc=kc, su=su, half=half:
                                e.matmul(
                                    ps[:, 0:512],
                                    xnt[:, kc * 512 + su * 128: kc * 512 + (su + 1) * 128],
                                    B(WQ + kc * 3072 + 2048 + half * 512, 512),
                                    start=(kc == 0), stop=(kc == 7)),
                                inc="pe" if kc == 7 else None)
                    slot = psi % 6
                    wt("scalar", "pe", mm)
                    wt("scalar", "lp", stage_free[slot])
                    a = op("scalar", lambda e, ps=ps, slot=slot: e.copy(
                        B(STAGE + slot * 512, 512), ps[:, 0:512]), inc="ac")
                    qkv_free[psi % 3] = ("ac", a)
                    wt("gpsimd", "ac", a)
                    for pp in range(4):
                        p = half * 4 + pp
                        w = op("gpsimd", lambda e, p=p, pp=pp, c=c, slot=slot:
                               e.dma_start(
                                   out=v_scr[p, :, c * 128:(c + 1) * 128],
                                   in_=B(STAGE + slot * 512 + pp * 128, 128)),
                               inc="lp", k=16)
                    stage_free[slot] = w
                    kvq_w = w
                    psi += 1
            xnt_pe_free = mm

        pe_p1_end = cnt["pe"]

        g_free2 = [0, 0]
        for kc in range(8):
            sl = kc % 2
            woff, moff = gsl[sl]
            wt("sync", "dv", g_free2[sl] or g_free[sl])
            op("sync", lambda e, kc=kc, woff=woff: e.dma_start(
                out=stg[:, woff:woff + 1024],
                in_=woutT[kc * 128:(kc + 1) * 128, :]), inc="ld", k=16)
            lm = op("sync", lambda e, kc=kc, moff=moff: e.dma_start(
                out=stg[:, moff:moff + 1024],
                in_=moutT[kc * 128:(kc + 1) * 128, :]), inc="ld", k=16)
            wt("vector", "ld", lm)
            if kc == 0:
                wt("vector", "pe", pe_p1_end)
            _m = op("vector", lambda e, moff=moff: e.tensor_scalar(
                stg[:, moff:moff + 1024], stg[:, moff:moff + 1024], 0.0, None,
                ALU.is_gt), inc="dv")
            wt("vector", "dv", _m)
            gv2 = op("vector", lambda e, kc=kc, woff=woff, moff=moff: e.tensor_tensor(
                B(WOUT + kc * 1024, 1024),
                stg[:, woff:woff + 1024], stg[:, moff:moff + 1024], ALU.mult),
                inc="dv")
            g_free2[sl] = gv2
        wout_ready = gv2

        q_ready = [0, 0]
        sslots = [0, 2048]
        s_free = [("dv", g_free2[0]), ("dv", g_free2[1])]
        ps4_free_init = 0
        for wi, (wten, nchunk, width, qdst) in enumerate(
                [(w1T, 8, 4096, w1q), (w2T, 32, 1024, w2q)]):
            cw = min(width, 2048)
            ncc = nchunk * (width // cw)
            for ch in range(ncc):
                kc, part = divmod(ch, width // cw)
                sl = ch % 2
                wt("sync", s_free[sl][0], s_free[sl][1])
                lw = op("sync", lambda e, wten=wten, kc=kc, part=part, sl=sl, cw=cw:
                        e.dma_start(
                            out=stg[:, sslots[sl]:sslots[sl] + cw],
                            in_=wten[kc * 128:(kc + 1) * 128,
                                     part * cw:(part + 1) * cw]),
                        inc="ld", k=16)
                wt("scalar", "ld", lw)
                if ch == 0 and wi == 0:
                    wt("scalar", "pe", pe_p1_end)
                a_abs = op("scalar", lambda e, sl=sl, cw=cw, ch=ch, wi=wi:
                           e.activation(
                               B(ABT, cw), stg[:, sslots[sl]:sslots[sl] + cw],
                               AF.Abs,
                               accum_out=gcols[:, 16 * wi + ch:16 * wi + ch + 1]
                               if wi == 0 else gcols[:, 16 + ch:17 + ch]),
                           inc="ac")
                s_free[sl] = ("ac", a_abs)
            a_pad = op("scalar", lambda e: e.copy(
                padsc[:, 0:512], padsc[:, 512:1024]), inc="ac")
            wt("tensor", "ac", a_pad)
            wt("tensor", "dv", ps4_free_init)
            goff = 0 if wi == 0 else 16
            fm = op("tensor", lambda e, goff=goff, ncc=ncc: e.matmul(
                PS4[0:1, 0:ncc], cols[:, 14:15], gcols[:, goff:goff + ncc],
                start=True, stop=True), inc="pe")
            tcol = 10 + 2 * wi
            wt("vector", "pe", fm)
            op("vector", lambda e, ncc=ncc: e.tensor_reduce(
                cols[0:1, 0:1], PS4[0:1, 0:ncc], mybir.AxisListType.X, ALU.add))
            trd = op("vector", lambda e: e.tensor_copy(
                padsc[:, 0:512], padsc[:, 512:1024]), inc="dv")
            ps4_free_init = trd
            wt("scalar", "dv", trd)
            op("scalar", lambda e, tcol=tcol: e.activation(
                cols[0:1, tcol:tcol + 1], cols[0:1, 0:1], AF.Identity,
                scale=0.5 / NW, bias=cols[0:1, 8:9]))
            op("scalar", lambda e, tcol=tcol: e.activation(
                cols[0:1, tcol + 1:tcol + 2], cols[0:1, 0:1], AF.Identity,
                scale=-0.5 / NW, bias=cols[0:1, 9:10]))
            tv = op("scalar", lambda e: e.copy(
                padsc[:, 0:512], padsc[:, 512:1024]), inc="ac")
            wt("gpsimd", "ac", tv)
            cw_ = op("gpsimd", lambda e, wi=wi, tcol=tcol: e.dma_start(
                out=cells_scr[0:1, 2 * wi:2 * wi + 2],
                in_=cols[0:1, tcol:tcol + 2]), inc="lp", k=16)
            wt("sync", "lp", cw_)
            tbc = op("sync", lambda e, wi=wi, tcol=tcol: e.dma_start(
                out=cols[:, tcol:tcol + 2],
                in_=cells_scr[0:1, 2 * wi:2 * wi + 2]
                .partition_broadcast(128).squeeze(1)), inc="ld", k=16)
            for ch in range(ncc):
                kc, part = divmod(ch, width // cw)
                sl = ch % 2
                wt("sync", s_free[sl][0], s_free[sl][1])
                lw = op("sync", lambda e, wten=wten, kc=kc, part=part, sl=sl, cw=cw:
                        e.dma_start(
                            out=stg[:, sslots[sl]:sslots[sl] + cw],
                            in_=wten[kc * 128:(kc + 1) * 128,
                                     part * cw:(part + 1) * cw]),
                        inc="ld", k=16)
                wt("vector", "ld", lw)
                if ch == 0:
                    wt("vector", "ld", tbc)
                    if wi == 0:
                        wt("vector", "lp", kvq_w)
                op("vector", lambda e, sl=sl, tcol=tcol, cw=cw: e.tensor_scalar(
                    B(QPOS, cw), stg[:, sslots[sl]:sslots[sl] + cw],
                    cols[:, tcol:tcol + 1], None, ALU.is_ge))
                _m = op("vector", lambda e, sl=sl, tcol=tcol, cw=cw: e.tensor_scalar(
                    B(QNEG, cw), stg[:, sslots[sl]:sslots[sl] + cw],
                    cols[:, tcol + 1:tcol + 2], None, ALU.is_le), inc="dv")
                wt("vector", "dv", _m)
                qv = op("vector", lambda e, qdst=qdst, kc=kc, part=part, cw=cw,
                        width=width: e.tensor_tensor(
                            qdst[:, kc * width + part * cw:
                                 kc * width + (part + 1) * cw],
                            B(QPOS, cw), B(QNEG, cw), ALU.subtract),
                        inc="dv")
                s_free[sl] = ("dv", qv)
            q_ready[wi] = qv

        PS6F = PS6[:, 0:1024].bitcast(F32)
        ps12 = [PS1, PS2]
        ss = dict(
            ktA_load=0, ktB_load=0, vA_load=0, vB_load=0,
            qt_load=[0, 0], qt_slot_free=[0, 0],
            pt_pv_free=[0, 0, 0, 0], pt_tree_free=[0, 0, 0, 0],
            ps12_free=[0, 0], ps12_dv_free=0,
            att_free=0, psat_free=[0, 0],
            ps4_free_dv=ps4_free_init, tree_slot_free=0, exp_last=0,
            s_w_of=[0] * G, sslot_free=[q_ready[1], q_ready[1]],
            x_st=[0, 0], x_free=[xfree[0], xfree[1]],
            ps3_free=("dv", q_ready[1]),
            ht_reader_pe=0, xnt_reader_pe=pe_p1_end,
            pending_tail=[], rv=0, d1=0, rb=0, att_last_nv=0,
        )
        tree_pool = [list(range(6)), list(range(6))]
        slot_guard = [[0] * 6, [0] * 6]

        def ffn_round_gen(g):
            wt("sync", "lp", xn_w[g])
            wt("sync", "pe", ss["xnt_reader_pe"])
            xl = op("sync", lambda e, g=g: e.dma_start(
                out=xnt[:].rearrange("p (k q) -> p k q", k=8),
                in_=xn_scr[:, :, g * 512:(g + 1) * 512]
                .rearrange("k p q -> p k q")), inc="ld", k=16)
            yield
            last_f1 = 0
            ha = 0
            for fc in range(32):
                fs, fv_ = ss["ps3_free"]
                if fs:
                    wt("tensor", fs, fv_)
                if fc == 0:
                    wt("tensor", "ld", xl)
                    wt("tensor", "dv", q_ready[0])
                    wt("tensor", "pe", ss["ht_reader_pe"])
                for kc in range(8):
                    fmm = op("tensor", lambda e, kc=kc, fc=fc: e.matmul(
                        PS3[:, 0:512],
                        w1q[:, kc * 4096 + fc * 128: kc * 4096 + (fc + 1) * 128],
                        xnt[:, kc * 512:(kc + 1) * 512],
                        start=(kc == 0), stop=(kc == 7)),
                        inc="pe" if kc == 7 else None)
                last_f1 = fmm
                wt("vector", "pe", fmm)
                ha = op("vector", lambda e, fc=fc: e.tensor_scalar(
                    B(HT + fc * 512, 512), PS3[:, 0:512],
                    b1c[:, fc:fc + 1], 0.0, ALU.add, ALU.max), inc="dv")
                ss["ps3_free"] = ("dv", ha)
                yield
            ss["xnt_reader_pe"] = last_f1
            ht_ready_dv = ha
            last_f2 = 0
            for su in range(4):
                t = g * 4 + su
                bi = su % 2
                wt("sync", "lp", ss["s_w_of"][g])
                wt("sync", "dv", ss["sslot_free"][bi])
                sl_ = op("sync", lambda e, t=t, bi=bi: e.dma_start(
                    out=stg[:, 2048 + bi * 1024: 2048 + (bi + 1) * 1024],
                    in_=s_scr[t * 128:(t + 1) * 128, :]), inc="ld", k=16)
                wt("sync", "st", ss["x_st"][bi])
                wt("sync", "dv", ss["x_free"][bi])
                ss["x_free"][bi] = 0
                xl_ = op("sync", lambda e, t=t, bi=bi: e.dma_start(
                    out=xbuf[:, bi * 1024:(bi + 1) * 1024],
                    in_=xc[t * 128:(t + 1) * 128, :]), inc="ld", k=16)
                yield
                fv2 = 0
                gm = 0
                for half in range(2):
                    for blk in range(4):
                        if blk == 0:
                            fs, fv_ = ss["ps3_free"]
                            if fs:
                                wt("tensor", fs, fv_)
                        if su == 0 and half == 0 and blk == 0:
                            wt("tensor", "dv", ht_ready_dv)
                            wt("tensor", "dv", q_ready[1])
                        for fc in range(blk * 8, (blk + 1) * 8):
                            gm = op("tensor", lambda e, fc=fc, su=su, half=half:
                                    e.matmul(
                                        PS3[:, 0:512],
                                        B(HT + fc * 512 + su * 128, 128),
                                        w2q[:, fc * 1024 + half * 512:
                                            fc * 1024 + (half + 1) * 512],
                                        start=(fc == 0), stop=(fc == 31)),
                                    inc="pe" if fc == 31 else None)
                        yield
                    last_f2 = gm
                    wt("vector", "pe", gm)
                    fv2 = op("vector", lambda e, half=half: e.tensor_tensor(
                        stg[:, 1024 + half * 512:1024 + (half + 1) * 512],
                        PS3[:, 0:512], rows[:, half * 512:(half + 1) * 512],
                        ALU.add), inc="dv")
                    ss["ps3_free"] = ("dv", fv2)
                    yield
                wt("scalar", "dv", fv2)
                th = op("scalar", lambda e: e.activation(
                    stg[:, 0:1024], stg[:, 1024:2048], AF.Tanh, scale=0.5),
                    inc="ac")
                wt("vector", "ac", th)
                op("vector", lambda e: e.tensor_scalar(
                    stg[:, 0:1024], stg[:, 0:1024], 0.5, 0.5, ALU.mult, ALU.add))
                op("vector", lambda e: e.tensor_tensor(
                    stg[:, 1024:2048], stg[:, 1024:2048], stg[:, 0:1024],
                    ALU.mult))
                wt("vector", "ld", max(sl_, xl_))
                sv2 = op("vector", lambda e, bi=bi: e.tensor_tensor(
                    stg[:, 1024:2048],
                    stg[:, 2048 + bi * 1024: 2048 + (bi + 1) * 1024],
                    stg[:, 1024:2048], ALU.mult), inc="dv")
                ss["sslot_free"][bi] = sv2
                ov = op("vector", lambda e, bi=bi: e.tensor_tensor(
                    xbuf[:, bi * 1024:(bi + 1) * 1024],
                    xbuf[:, bi * 1024:(bi + 1) * 1024], stg[:, 1024:2048],
                    ALU.add), inc="dv")
                wt("sync", "dv", ov)
                ss["x_st"][bi] = op("sync", lambda e, t=t, bi=bi: e.dma_start(
                    out=yc[t * 128:(t + 1) * 128, :],
                    in_=xbuf[:, bi * 1024:(bi + 1) * 1024]), inc="st", k=16)
                yield
            ss["ht_reader_pe"] = last_f2

        wt("tensor", "dv", v_mem)
        wt("tensor", "ac", cnt["ac"])
        for g in range(G):
            fq = ffn_round_gen(g - 1) if g > 0 else iter(())

            def pop(n=1, fq=fq):
                for _ in range(n):
                    if next(fq, "END") == "END":
                        break
            for p in range(NP):
                ko = KT
                psat = PS5[:, 0:512] if p % 2 == 0 else PS6F
                if g == 0 and p == 0:
                    wt("sync", "lp", kvq_w)
                    wt("gpsimd", "lp", kvq_w)
                    ss["qt_load"][0] = op("sync", lambda e: e.dma_start(
                        out=qtb[:, 0:512], in_=q_scr[0, :, 0:512]),
                        inc="ld", k=16)
                    ss["ktA_load"] = op("sync", lambda e: e.dma_start(
                        out=B(KT, 2048), in_=kt_scr[0, :, 0:2048]),
                        inc="ld", k=16)
                    ss["ktB_load"] = op("sync", lambda e: e.dma_start(
                        out=B(KT + 2048, 2048), in_=kt_scr[0, :, 2048:4096]),
                        inc="ld", k=16)
                    ss["vA_load"] = op("gpsimd", lambda e: e.dma_start(
                        out=B(VB, 2048), in_=v_scr[0, :, 0:2048]),
                        inc="lp", k=16)
                    ss["vB_load"] = op("gpsimd", lambda e: e.dma_start(
                        out=B(VB + 2048, 2048), in_=v_scr[0, :, 2048:4096]),
                        inc="lp", k=16)
                qoff = (p % 2) * 512
                tree_stack = [[], []]
                troot = [None, None]
                ax_of = [0] * NC
                sm_of = [0] * NC
                pv_of = [0] * NC
                last_pv = 0
                dvv = 0
                np_, ng_ = (p + 1, g) if p + 1 < NP else (0, g + 1)
                have_next = ng_ < G

                def emit_pv(c, p=p, psat=psat):
                    nonlocal last_pv
                    wt("tensor", "ac", ax_of[c])
                    if c == 0:
                        wt("tensor", "lp", ss["vA_load"])
                        wt("tensor", "dv", ss["psat_free"][p % 2])
                    if c == 16:
                        wt("tensor", "lp", ss["vB_load"])
                    for h in range(2):
                        pv = op("tensor", lambda e, c=c, ci=c % 4, h=h, psat=psat:
                                e.matmul(
                                    psat[h * 64:(h + 1) * 64, :],
                                    B(VB + c * 128 + h * 64, 64),
                                    B(PT[ci] + h * 512, 512),
                                    start=(c == 0), stop=(c == NC - 1)),
                                inc="pe" if h == 1 else None)
                    ss["pt_pv_free"][c % 4] = pv
                    pv_of[c] = pv
                    last_pv = pv

                def emit_scores(c):
                    ci = c % 2
                    ps = ps12[ci]
                    if c < 2:
                        wt("tensor", "ld", max(ss["ktA_load"], ss["qt_load"][p % 2]))
                        wt("tensor", "dv", ss["ps12_dv_free"])
                    if c == 14 or c == 15:
                        wt("tensor", "ld", ss["ktB_load"])
                    wt("tensor", "ac", ss["ps12_free"][ci])
                    op("tensor", lambda e, ps=ps, ko=ko, c=c, qoff=qoff: e.matmul(
                        ps[:, 0:512],
                        big[0:64, ko + c * 128: ko + (c + 1) * 128],
                        qtb[0:64, qoff:qoff + 512], start=True, stop=True))
                    sv_ = op("tensor", lambda e, ps=ps, ko=ko, c=c, qoff=qoff:
                             e.matmul(
                                 ps[:, 512:1024],
                                 big[64:128, ko + c * 128: ko + (c + 1) * 128],
                                 qtb[64:128, qoff:qoff + 512], start=True,
                                 stop=True), inc="pe")
                    sm_of[c] = sv_
                    return sv_

                def emit_exp(c, sm):
                    ci = c % 2
                    ps = ps12[ci]
                    wt("scalar", "pe", sm)
                    wt("scalar", "pe", ss["pt_pv_free"][c % 4])
                    wt("scalar", "dv", ss["pt_tree_free"][c % 4])
                    ax = op("scalar", lambda e, ps=ps, c4=c % 4: e.activation(
                        B(PT[c4], 1024), ps[:, 0:1024], AF.Exp), inc="ac")
                    ax_of[c] = ax
                    ss["ps12_free"][ci] = ax
                    ss["exp_last"] = ax

                def emit_tree(c):
                    nonlocal dvv
                    wt("vector", "ac", ax_of[c])
                    for h in range(2):
                        slot = tree_pool[h].pop(0)
                        if slot_guard[h][slot]:
                            wt("vector", "pe", slot_guard[h][slot])
                            slot_guard[h][slot] = 0
                        dvv = op("vector", lambda e, h=h, slot=slot,
                                 ca=(c - 1) % 4, cb=c % 4:
                                 e.tensor_tensor(
                                     B(TREE + (h * 6 + slot) * 512, 512),
                                     B(PT[ca] + h * 512, 512),
                                     B(PT[cb] + h * 512, 512), ALU.add),
                                 inc="dv")
                        lev = 1
                        while tree_stack[h] and tree_stack[h][-1][0] == lev:
                            _, pslot = tree_stack[h].pop()
                            dvv = op("vector",
                                     lambda e, h=h, slot=slot, pslot=pslot:
                                     e.tensor_tensor(
                                         B(TREE + (h * 6 + slot) * 512, 512),
                                         B(TREE + (h * 6 + slot) * 512, 512),
                                         B(TREE + (h * 6 + pslot) * 512, 512),
                                         ALU.add), inc="dv")
                            tree_pool[h].append(pslot)
                            lev += 1
                        tree_stack[h].append((lev, slot))
                        if c == NC - 1:
                            troot[h] = (tree_stack[h][-1][1], dvv)
                    ss["pt_tree_free"][(c - 1) % 4] = dvv
                    ss["pt_tree_free"][c % 4] = dvv

                def make_tail(p=p, psat=psat, troot_l=None, lpv=None):
                    def t1():
                        wt("tensor", "dv", troot_l[1][1])
                        wt("tensor", "dv", ss["ps4_free_dv"])
                        op("tensor", lambda e, sl=troot_l[0][0]: e.matmul(
                            PS4[0:1, 0:512], onesb[:], B(TREE + sl * 512, 512),
                            start=True, stop=True))
                        d1 = op("tensor", lambda e, sl=troot_l[1][0]: e.matmul(
                            PS4[64:65, 0:512], onesb[:],
                            B(TREE + (6 + sl) * 512, 512),
                            start=True, stop=True), inc="pe")
                        ss["d1"] = d1
                        for h in range(2):
                            rslot = troot_l[h][0]
                            tree_pool[h].append(rslot)
                            slot_guard[h][rslot] = d1

                    def _rec(e, off, row):
                        with nc.allow_low_precision(reason="recip row bf16"):
                            return e.reciprocal(big[0:1, off:off + 512],
                                                PS4[row:row + 1, 0:512])

                    def t2():
                        wt("vector", "pe", ss["d1"])
                        op("vector", lambda e: _rec(e, RRA, 0))

                    def t3():
                        op("vector", lambda e: _rec(e, RRB, 64))
                        ss["rv"] = op("vector", lambda e: e.tensor_copy(
                            padsc[:, 0:512], padsc[:, 512:1024]), inc="dv")

                    def t4():
                        wt("tensor", "dv", ss["rv"])
                        op("tensor", lambda e: e.matmul(
                            PS4[0:64, 0:512], onesrb[:], big[0:1, RRA:RRA + 512],
                            start=True, stop=True))
                        ss["rb"] = op("tensor", lambda e: e.matmul(
                            PS4[64:128, 0:512], onesrb[:],
                            big[0:1, RRB:RRB + 512],
                            start=True, stop=True), inc="pe")

                    def t5():
                        wt("vector", "pe", ss["rb"])
                        rc = op("vector", lambda e: e.tensor_copy(
                            B(RSB, 512), PS4[:, 0:512]), inc="dv")
                        ss["ps4_free_dv"] = rc
                        wt("vector", "pe", lpv[0])
                        wt("vector", "pe", ss["att_free"])
                        nv = op("vector", lambda e, p=p, psat=psat:
                                e.tensor_tensor(
                                    B(ATT + p * 512, 512), psat, B(RSB, 512),
                                    ALU.mult), inc="dv")
                        ss["psat_free"][p % 2] = nv
                        ss["att_last_nv"] = nv
                    return [t1, t2, t3, t4, t5]

                TAIL_STEPS = {2: 0, 6: 1, 12: 2, 16: 3, 20: 4}
                for c2 in range(0, NC, 2):
                    smA = emit_scores(c2)
                    smB = emit_scores(c2 + 1)
                    sm = smB
                    if c2 >= 2:
                        emit_pv(c2 - 2)
                        emit_pv(c2 - 1)
                    if c2 in TAIL_STEPS and ss["pending_tail"]:
                        ss["pending_tail"][TAIL_STEPS[c2]]()
                    if c2 == 18 and have_next:
                        wt("sync", "pe", sm_of[15])
                        ss["ktA_load"] = op("sync", lambda e, np_=np_:
                                            e.dma_start(out=B(KT, 2048),
                                                        in_=kt_scr[np_, :, 0:2048]),
                                            inc="ld", k=16)
                        wt("sync", "pe", ss["qt_slot_free"][np_ % 2])
                        ss["qt_load"][np_ % 2] = op(
                            "sync", lambda e, np_=np_, ng_=ng_: e.dma_start(
                                out=qtb[:, (np_ % 2) * 512:(np_ % 2) * 512 + 512],
                                in_=q_scr[np_, :, ng_ * 512:(ng_ + 1) * 512]),
                            inc="ld", k=16)
                        wt("gpsimd", "pe", pv_of[15])
                        ss["vA_load"] = op("gpsimd", lambda e, np_=np_:
                                           e.dma_start(out=B(VB, 2048),
                                                       in_=v_scr[np_, :, 0:2048]),
                                           inc="lp", k=16)
                    if (c2 // 2) % 3 != 0:
                        pop(1)
                    emit_exp(c2, smA)
                    emit_exp(c2 + 1, smB)
                    emit_tree(c2 + 1)
                emit_pv(NC - 2)
                emit_pv(NC - 1)
                lpv = [last_pv]
                stages = make_tail(troot_l=troot, lpv=lpv)
                if p == NP - 1:
                    for f in stages:
                        f()
                    ss["pending_tail"] = []
                else:
                    ss["pending_tail"] = stages
                ss["qt_slot_free"][p % 2] = sm
                if have_next:
                    wt("sync", "pe", sm_of[NC - 1])
                    ss["ktB_load"] = op("sync", lambda e, np_=np_: e.dma_start(
                        out=B(KT + 2048, 2048), in_=kt_scr[np_, :, 2048:4096]),
                        inc="ld", k=16)
                    wt("gpsimd", "pe", last_pv)
                    ss["vB_load"] = op("gpsimd", lambda e, np_=np_: e.dma_start(
                        out=B(VB + 2048, 2048), in_=v_scr[np_, :, 2048:4096]),
                        inc="lp", k=16)
                pop(2)
            att_ready = ss["att_last_nv"]
            pop(200)
            s_slots = [2048, 3072]
            for su in range(4):
                if su == 0:
                    wt("tensor", "dv", att_ready)
                    wt("tensor", "dv", wout_ready)
                wt("tensor", "ac", ss["exp_last"])
                wt("tensor", "dv", ss["ps12_dv_free"])
                for p_ in range(NP):
                    for half in range(2):
                        om = op("tensor", lambda e, p_=p_, su=su, half=half:
                                e.matmul(
                                    PS1[:, half * 512:(half + 1) * 512],
                                    B(ATT + p_ * 512 + su * 128, 128),
                                    B(WOUT + p_ * 1024 + half * 512, 512),
                                    start=(p_ == 0), stop=(p_ == NP - 1)),
                                inc="pe" if (p_ == NP - 1 and half == 1) else None)
                wt("vector", "pe", om)
                wt("vector", "dv", ss["sslot_free"][su % 2])
                sv = op("vector", lambda e, su=su: e.tensor_tensor(
                    stg[:, s_slots[su % 2]:s_slots[su % 2] + 1024],
                    PS1[:, 0:1024], rows[:, 1024:2048], ALU.add), inc="dv")
                ss["ps12_dv_free"] = sv
                ss["sslot_free"][su % 2] = sv
                wt("gpsimd", "dv", sv)
                sw = op("gpsimd", lambda e, g=g, su=su: e.dma_start(
                    out=s_scr[(g * 4 + su) * 128:(g * 4 + su + 1) * 128, :],
                    in_=stg[:, s_slots[su % 2]:s_slots[su % 2] + 1024]),
                    inc="lp", k=16)
                ss["s_w_of"][g] = sw
            ss["att_free"] = om
        fq = ffn_round_gen(G - 1)
        for _ in range(500):
            if next(fq, "END") == "END":
                break

        wt("sync", "st", cnt["st"])

        @block.sync
        def _(eng):
            for f in streams["sync"]:
                f(eng)

        @block.gpsimd
        def _(eng):
            for f in streams["gpsimd"]:
                f(eng)

        @block.tensor
        def _(eng):
            for f in streams["tensor"]:
                f(eng)

        @block.vector
        def _(eng):
            for f in streams["vector"]:
                f(eng)

        @block.scalar
        def _(eng):
            for f in streams["scalar"]:
                f(eng)

    return nc


def kernel(**inputs):
    x = np.ascontiguousarray(np.asarray(inputs["L0_attn_x"], np.float32))
    C = lambda a: np.ascontiguousarray(np.asarray(a, np.float32))
    shared = dict(
        wqkvT=C(np.asarray(inputs["L0_attn_qkv_weight"]).T),
        mqkvT=C(np.asarray(inputs["L0_attn_qkv_weight_mask"]).T),
        woutT=C(np.asarray(inputs["L0_attn_out_weight"]).T),
        moutT=C(np.asarray(inputs["L0_attn_out_weight_mask"]).T),
        w1T=C(np.asarray(inputs["L0_ffn_W1"]).T),
        w2T=C(np.asarray(inputs["L0_ffn_W2"]).T),
        b1c=C(np.asarray(inputs["L0_ffn_b1"]).reshape(32, 128).T),
        b2r=C(np.asarray(inputs["L0_ffn_b2"]).reshape(1, D)),
        obr=C(np.asarray(inputs["L0_attn_out_bias"]).reshape(1, D)),
        obm=C(np.asarray(inputs["L0_attn_out_bias_mask"]).reshape(1, D)),
        iden=np.eye(128).astype(ml_dtypes.bfloat16),
    )
    nc = build_nc()
    in_maps = [dict(xc=np.ascontiguousarray(x[c]), **shared) for c in range(8)]
    res = run_bass_kernel_spmd(nc, in_maps, core_ids=list(range(8)))
    return np.stack([np.asarray(r["yc"], np.float32) for r in res.results], axis=0)


# version 22 (final)
# speedup vs baseline: 1.4961x; 1.0065x over previous
import numpy as np
import ml_dtypes
import contextlib
import concourse.bass as bass
import concourse.mybir as mybir
from concourse.bass_utils import run_bass_kernel_spmd

F32 = mybir.dt.float32
BF16 = mybir.dt.bfloat16
FP8 = mybir.dt.float8e4
AF = mybir.ActivationFunctionType
ALU = mybir.AluOpType

S, D, NO, FF = 4096, 1024, 3072, 4096
G, NP, NC = 8, 8, 32
NW = 4194304.0
EPS = 1e-5
ENGS = ("sync", "gpsimd", "tensor", "vector", "scalar")

WQ = 0
KT = 0
VB = 4096
QT = 8192
WOUT = 8704
PT = (16896, 17920, 18944, 19968, 49152)
ATT = 20992
TREE = 25088
HT = 31232
RRA, RRB = 47616, 48128
RSB = 48640
XNBF = (25088, 26112)
TRASH = 27136
STAGE = 28160
ABT = 25088
QPOS, QNEG = 31232, 33280
BIGN = 50176


def build_nc():
    nc = bass.Bass("TRN2", target_bir_lowering=False)
    I = lambda n, s, d: nc.dram_tensor(n, s, d, kind="ExternalInput").ap()
    xc = I("xc", [S, D], F32)
    wqkvT = I("wqkvT", [D, NO], F32)
    mqkvT = I("mqkvT", [D, NO], F32)
    woutT = I("woutT", [D, D], F32)
    moutT = I("moutT", [D, D], F32)
    w1T = I("w1T", [D, FF], F32)
    w2T = I("w2T", [FF, D], F32)
    b1cI = I("b1c", [128, 32], F32)
    b2r = I("b2r", [1, D], F32)
    obr = I("obr", [1, D], F32)
    obm = I("obm", [1, D], F32)
    iden = I("iden", [128, 128], BF16)
    yc = nc.dram_tensor("yc", [S, D], F32, kind="ExternalOutput").ap()
    kt_scr = nc.dram_tensor("kt_scr", [NP, 128, S], BF16, kind="Internal").ap()
    v_scr = nc.dram_tensor("v_scr", [NP, 128, S], BF16, kind="Internal").ap()
    q_scr = nc.dram_tensor("q_scr", [NP, 128, S], BF16, kind="Internal").ap()
    xn_scr = nc.dram_tensor("xn_scr", [8, 128, S], BF16, kind="Internal").ap()
    s_scr = nc.dram_tensor("s_scr", [S, D], F32, kind="Internal").ap()
    cells_scr = nc.dram_tensor("cells_scr", [1, 4], F32, kind="Internal").ap()

    with contextlib.ExitStack() as est:
        E = est.enter_context
        big = E(nc.sbuf_tensor("big", [128, BIGN], BF16))
        w1q = E(nc.sbuf_tensor("w1q", [128, 32768], FP8))
        w2q = E(nc.sbuf_tensor("w2q", [128, 32768], FP8))
        stg = E(nc.sbuf_tensor("stg", [128, 4096], F32))
        xbuf = E(nc.sbuf_tensor("xbuf", [128, 2048], F32))
        rows = E(nc.sbuf_tensor("rows", [128, 2048], F32))
        xnt = E(nc.sbuf_tensor("xnt", [128, 4096], BF16))
        cols = E(nc.sbuf_tensor("cols", [128, 16], F32))
        gcols = E(nc.sbuf_tensor("gcols", [128, 64], F32))
        b1c = E(nc.sbuf_tensor("b1cs", [128, 32], F32))
        onesb = E(nc.sbuf_tensor("onesb", [128, 1], BF16))
        onesrow = E(nc.sbuf_tensor("onesrow", [1, 128], F32))
        onesrb = E(nc.sbuf_tensor("onesrb", [1, 64], BF16))
        idsb = E(nc.sbuf_tensor("idsb", [128, 128], BF16))
        padsc = E(nc.sbuf_tensor("padsc", [128, 1024], BF16))
        qtb = E(nc.sbuf_tensor("qtb", [128, 1024], BF16))
        PS1 = E(nc.psum_tensor("PS1", [128, 1024], F32))
        PS2 = E(nc.psum_tensor("PS2", [128, 1024], F32))
        PS3 = E(nc.psum_tensor("PS3", [128, 512], F32))
        PS4 = E(nc.psum_tensor("PS4", [128, 512], F32))
        PS5 = E(nc.psum_tensor("PS5", [128, 512], F32))
        PS6 = E(nc.psum_tensor("PS6", [128, 1024], BF16))
        sem = {n: E(nc.semaphore(n)) for n in ("ld", "lp", "pe", "ac", "dv", "st", "pl")}
        block = E(nc.Block())

        streams = {e: [] for e in ENGS}
        cnt = dict(ld=0, lp=0, pe=0, ac=0, dv=0, st=0, pl=0)

        def wt(e, s_, v):
            if v and v > 0:
                streams[e].append(lambda eng, ss=sem[s_], vv=v: eng.wait_ge(ss, vv))

        def op(e, f, inc=None, k=1):
            if inc is None:
                streams[e].append(f)
                return None
            cnt[inc] += k
            streams[e].append(lambda eng, f=f, ss=sem[inc], kk=k: f(eng).then_inc(ss, kk))
            return cnt[inc]

        def B(lo, n):
            return big[:, lo:lo + n]

        op("vector", lambda e: e.memset(onesb[:], 1.0))
        op("vector", lambda e: e.memset(onesrow[:], 1.0))
        op("vector", lambda e: e.memset(onesrb[:], 1.0))
        op("vector", lambda e: e.memset(cols[:, 14:15], 1.0))
        op("vector", lambda e: e.memset(cols[:, 15:16], EPS))
        op("vector", lambda e: e.memset(cols[:, 8:9], 0.5 * EPS))
        v_mem = op("vector", lambda e: e.memset(cols[:, 9:10], -0.5 * EPS), inc="dv")
        ld_id = op("sync", lambda e: e.dma_start(out=idsb[:], in_=iden), inc="ld", k=16)
        op("sync", lambda e: e.dma_start(out=b1c[:], in_=b1cI), inc="ld", k=16)
        op("sync", lambda e: e.dma_start(
            out=rows[:, 0:1024], in_=b2r.partition_broadcast(128).squeeze(1)),
            inc="ld", k=16)
        op("sync", lambda e: e.dma_start(
            out=rows[:, 1024:2048], in_=obr.partition_broadcast(128).squeeze(1)),
            inc="ld", k=16)
        ld_obm = op("sync", lambda e: e.dma_start(
            out=stg[:, 0:1024], in_=obm.partition_broadcast(128).squeeze(1)),
            inc="ld", k=16)
        wt("vector", "ld", ld_obm)
        _m = op("vector", lambda e: e.tensor_scalar(
            stg[:, 0:1024], stg[:, 0:1024], 0.0, None, ALU.is_gt), inc="dv")
        wt("vector", "dv", _m)
        v_ob = op("vector", lambda e: e.tensor_tensor(
            rows[:, 1024:2048], rows[:, 1024:2048], stg[:, 0:1024], ALU.mult),
            inc="dv")

        gsl = [(0, 1024), (2048, 3072)]
        g_free = [v_ob, v_ob]
        for j in range(24):
            kc, part = divmod(j, 3)
            sl = j % 2
            woff, moff = gsl[sl]
            wt("sync", "dv", g_free[sl])
            op("sync", lambda e, kc=kc, part=part, woff=woff: e.dma_start(
                out=stg[:, woff:woff + 1024],
                in_=wqkvT[kc * 128:(kc + 1) * 128, part * 1024:(part + 1) * 1024]),
                inc="ld", k=16)
            lm = op("sync", lambda e, kc=kc, part=part, moff=moff: e.dma_start(
                out=stg[:, moff:moff + 1024],
                in_=mqkvT[kc * 128:(kc + 1) * 128, part * 1024:(part + 1) * 1024]),
                inc="ld", k=16)
            wt("vector", "ld", lm)
            _m = op("vector", lambda e, moff=moff: e.tensor_scalar(
                stg[:, moff:moff + 1024], stg[:, moff:moff + 1024], 0.0, None,
                ALU.is_gt), inc="dv")
            wt("vector", "dv", _m)
            gv = op("vector", lambda e, kc=kc, part=part, woff=woff, moff=moff:
                    e.tensor_tensor(
                        B(WQ + kc * 3072 + part * 1024, 1024),
                        stg[:, woff:woff + 1024], stg[:, moff:moff + 1024],
                        ALU.mult), inc="dv")
            g_free[sl] = gv
        wq_all_ready = gv

        q_ready = [0, 0]
        ps4_free_init = 0
        ss_qnt = {"ps4_free_init": 0}
        def qnt_gen():
            nonlocal_dummy = 0
            yield
            sslots = [0, 2048]
            s_free = [("dv", g_free[0]), ("dv", g_free[1])]
            for wi, (wten, nchunk, width, qdst) in enumerate(
                    [(w1T, 8, 4096, w1q), (w2T, 32, 1024, w2q)]):
                cw = min(width, 2048)
                ncc = nchunk * (width // cw)
                for ch in range(ncc):
                    kc, part = divmod(ch, width // cw)
                    sl = ch % 2
                    wt("sync", s_free[sl][0], s_free[sl][1])
                    lw = op("sync", lambda e, wten=wten, kc=kc, part=part, sl=sl, cw=cw:
                            e.dma_start(
                                out=stg[:, sslots[sl]:sslots[sl] + cw],
                                in_=wten[kc * 128:(kc + 1) * 128,
                                         part * cw:(part + 1) * cw]),
                            inc="ld", k=16)
                    wt("scalar", "ld", lw)
                    a_abs = op("scalar", lambda e, sl=sl, cw=cw, ch=ch, wi=wi:
                               e.activation(
                                   B(35328, cw), stg[:, sslots[sl]:sslots[sl] + cw],
                                   AF.Abs,
                                   accum_out=gcols[:, 16 * wi + ch:16 * wi + ch + 1]
                                   if wi == 0 else gcols[:, 16 + ch:17 + ch]),
                               inc="ac")
                    s_free[sl] = ("ac", a_abs)
                    yield
                a_pad = op("scalar", lambda e: e.copy(
                    padsc[:, 0:512], padsc[:, 512:1024]), inc="ac")
                wt("tensor", "ac", a_pad)
                wt("tensor", "dv", ss_qnt["ps4_free_init"])
                goff = 0 if wi == 0 else 16
                fm = op("tensor", lambda e, goff=goff, ncc=ncc: e.matmul(
                    PS1[0:1, 0:ncc], cols[:, 14:15], gcols[:, goff:goff + ncc],
                    start=True, stop=True), inc="pe")
                tcol = 10 + 2 * wi
                wt("vector", "pe", fm)
                op("vector", lambda e, ncc=ncc: e.tensor_reduce(
                    gcols[0:1, 56:57], PS1[0:1, 0:ncc], mybir.AxisListType.X, ALU.add))
                trd = op("vector", lambda e: e.tensor_copy(
                    padsc[:, 0:512], padsc[:, 512:1024]), inc="dv")
                ss_qnt["ps4_free_init"] = trd
                wt("scalar", "dv", trd)
                op("scalar", lambda e, tcol=tcol: e.activation(
                    cols[0:1, tcol:tcol + 1], gcols[0:1, 56:57], AF.Identity,
                    scale=0.5 / NW, bias=cols[0:1, 8:9]))
                op("scalar", lambda e, tcol=tcol: e.activation(
                    cols[0:1, tcol + 1:tcol + 2], gcols[0:1, 56:57], AF.Identity,
                    scale=-0.5 / NW, bias=cols[0:1, 9:10]))
                tv = op("scalar", lambda e: e.copy(
                    padsc[:, 0:512], padsc[:, 512:1024]), inc="ac")
                wt("gpsimd", "ac", tv)
                cw_ = op("gpsimd", lambda e, wi=wi, tcol=tcol: e.dma_start(
                    out=cells_scr[0:1, 2 * wi:2 * wi + 2],
                    in_=cols[0:1, tcol:tcol + 2]), inc="lp", k=16)
                wt("sync", "lp", cw_)
                tbc = op("sync", lambda e, wi=wi, tcol=tcol: e.dma_start(
                    out=cols[:, tcol:tcol + 2],
                    in_=cells_scr[0:1, 2 * wi:2 * wi + 2]
                    .partition_broadcast(128).squeeze(1)), inc="ld", k=16)
                for ch in range(ncc):
                    kc, part = divmod(ch, width // cw)
                    sl = ch % 2
                    wt("sync", s_free[sl][0], s_free[sl][1])
                    lw = op("sync", lambda e, wten=wten, kc=kc, part=part, sl=sl, cw=cw:
                            e.dma_start(
                                out=stg[:, sslots[sl]:sslots[sl] + cw],
                                in_=wten[kc * 128:(kc + 1) * 128,
                                         part * cw:(part + 1) * cw]),
                            inc="ld", k=16)
                    wt("vector", "ld", lw)
                    if ch == 0:
                        wt("vector", "ld", tbc)
                    op("vector", lambda e, sl=sl, tcol=tcol, cw=cw: e.tensor_scalar(
                        B(QPOS, cw), stg[:, sslots[sl]:sslots[sl] + cw],
                        cols[:, tcol:tcol + 1], None, ALU.is_ge))
                    _m = op("vector", lambda e, sl=sl, tcol=tcol, cw=cw: e.tensor_scalar(
                        B(QNEG, cw), stg[:, sslots[sl]:sslots[sl] + cw],
                        cols[:, tcol + 1:tcol + 2], None, ALU.is_le), inc="dv")
                    wt("vector", "dv", _m)
                    qv = op("vector", lambda e, qdst=qdst, kc=kc, part=part, cw=cw,
                            width=width: e.tensor_tensor(
                                qdst[:, kc * width + part * cw:
                                     kc * width + (part + 1) * cw],
                                B(QPOS, cw), B(QNEG, cw), ALU.subtract),
                            inc="dv")
                    s_free[sl] = ("dv", qv)
                    yield
                q_ready[wi] = qv
                yield

        xload = [0, 0]
        xfree = [0, 0]
        xnbf_free = [0, 0]
        ps6_free = 0
        xnT_evac = [0] * G
        xn_w = [0] * G
        qkv_free = [(None, 0)] * 3
        stage_free = [0] * 6
        kvq_w = 0
        xnt_pe_free = 0
        psi = 0

        qq = qnt_gen()

        def qpop(n=1):
            for _ in range(n):
                if next(qq, "END") == "END":
                    break
        for g in range(G):
            def emit_ln_stats(t):
                bi = t % 2
                sc = cols[:, 0:8] if t % 2 == 0 else gcols[:, 48:56]
                wt("sync", "dv", xfree[bi])
                xload[bi] = op("sync", lambda e, t=t, bi=bi: e.dma_start(
                    out=xbuf[:, bi * 1024:(bi + 1) * 1024],
                    in_=xc[t * 128:(t + 1) * 128, :]), inc="ld", k=16)
                wt("vector", "ld", xload[bi])
                if t == 0:
                    wt("vector", "dv", v_mem)
                op("vector", lambda e, bi=bi, sc=sc: e.tensor_scalar(
                    B(TRASH, 1024), xbuf[:, bi * 1024:(bi + 1) * 1024], 0.0, None,
                    ALU.add, ALU.add, accum_out=sc[:, 0:1]))
                op("vector", lambda e, bi=bi: e.tensor_tensor(
                    B(TRASH, 1024), xbuf[:, bi * 1024:(bi + 1) * 1024],
                    xbuf[:, bi * 1024:(bi + 1) * 1024], ALU.mult))
                v_var = op("vector", lambda e, sc=sc: e.tensor_scalar(
                    B(TRASH, 1024), B(TRASH, 1024), 0.0, None, ALU.add, ALU.add,
                    accum_out=sc[:, 1:2]), inc="dv")
                wt("scalar", "dv", v_var)
                op("scalar", lambda e, sc=sc: e.mul(sc[:, 2:3], sc[:, 0:1],
                                                    1.0 / 1024.0))
                op("scalar", lambda e, sc=sc: e.activation(
                    sc[:, 4:5], sc[:, 2:3], AF.Identity, scale=sc[:, 2:3]))
                op("scalar", lambda e, sc=sc: e.activation(
                    sc[:, 5:6], sc[:, 4:5], AF.Identity, scale=-1.0,
                    bias=cols[:, 15:16]))
                op("scalar", lambda e, sc=sc: e.mul(sc[:, 3:4], sc[:, 1:2],
                                                    1.0 / 1024.0))
                op("scalar", lambda e, sc=sc: e.activation(
                    sc[:, 6:7], sc[:, 3:4], AF.Ln, bias=sc[:, 5:6]))
                op("scalar", lambda e, sc=sc: e.activation(
                    sc[:, 7:8], sc[:, 6:7], AF.Exp, scale=-0.5))
                return op("scalar", lambda e: e.copy(
                    padsc[:, 0:512], padsc[:, 512:1024]), inc="ac")

            def emit_ln_apply(t, a_rstd):
                nonlocal ps6_free, xnt_pe_free
                bi = t % 2
                su = t % 4
                sc = cols[:, 0:8] if t % 2 == 0 else gcols[:, 48:56]
                wt("vector", "ac", a_rstd)
                wt("vector", "pe", xnbf_free[bi])
                xv = op("vector", lambda e, bi=bi, sc=sc: e.tensor_scalar(
                    B(XNBF[bi], 1024), xbuf[:, bi * 1024:(bi + 1) * 1024],
                    sc[:, 2:3], sc[:, 7:8], ALU.subtract, ALU.mult), inc="dv")
                xfree[bi] = xv
                wt("tensor", "dv", xv)
                wt("tensor", "dv", ps6_free)
                if t == 0:
                    wt("tensor", "ld", ld_id)
                for kc in range(8):
                    tp = op("tensor", lambda e, kc=kc, bi=bi: e.transpose(
                        PS6[:, kc * 128:(kc + 1) * 128],
                        B(XNBF[bi] + kc * 128, 128), idsb[:]),
                        inc="pe" if kc == 7 else None)
                xnbf_free[bi] = tp
                wt("vector", "pe", tp)
                if su == 0:
                    wt("vector", "pe", xnt_pe_free)
                    wt("vector", "lp", xn_w[g - 1] if g > 0 else 0)
                op("vector", lambda e, su=su: e.tensor_copy(
                    xnt[:, :].rearrange("p (k r) -> p k r", r=512)
                    [:, 0:8, su * 128:(su + 1) * 128],
                    PS6[:].rearrange("p (k r) -> p k r", r=128)))
                ev = op("vector", lambda e: e.tensor_copy(padsc[:, 0:512], padsc[:, 512:1024]), inc="dv")
                ps6_free = ev

            prev = None
            for su in range(4):
                t = g * 4 + su
                ar = emit_ln_stats(t)
                if prev is not None:
                    emit_ln_apply(prev[0], prev[1])
                prev = (t, ar)
            emit_ln_apply(prev[0], prev[1])
            xnT_evac[g] = ps6_free
            wt("gpsimd", "dv", xnT_evac[g])
            xn_w[g] = op("gpsimd", lambda e, g=g: e.dma_start(
                out=xn_scr[:, :, g * 512:(g + 1) * 512].rearrange("k p q -> p k q"),
                in_=xnt[:].rearrange("p (k q) -> p k q", k=8)), inc="lp", k=16)

            if g == 0:
                wt("tensor", "dv", wq_all_ready)
            for sect in range(2):
                obase = 1024 if sect == 0 else 0
                for p in range(NP):
                    ps = (PS3, PS4, PS5)[psi % 3]
                    fs, fv_ = qkv_free[psi % 3]
                    if fs:
                        wt("tensor", fs, fv_)
                    if sect == 0 and p == 0:
                        wt("tensor", "dv", xnT_evac[g])
                    for kc in range(8):
                        mm = op("tensor", lambda e, ps=ps, kc=kc, p=p, obase=obase:
                                e.matmul(
                                    ps[:, 0:512],
                                    B(WQ + kc * 3072 + obase + p * 128, 128),
                                    xnt[:, kc * 512:(kc + 1) * 512],
                                    start=(kc == 0), stop=(kc == 7)),
                                inc="pe" if kc == 7 else None)
                    slot = psi % 6
                    wt("scalar", "pe", mm)
                    wt("scalar", "lp", stage_free[slot])
                    if sect == 0:
                        a = op("scalar", lambda e, ps=ps, slot=slot: e.copy(
                            B(STAGE + slot * 512, 512), ps[:, 0:512]), inc="ac")
                    else:
                        a = op("scalar", lambda e, ps=ps, slot=slot: e.mul(
                            B(STAGE + slot * 512, 512), ps[:, 0:512], 0.03125),
                            inc="ac")
                    qkv_free[psi % 3] = ("ac", a)
                    wt("gpsimd", "ac", a)
                    dst = kt_scr if sect == 0 else q_scr
                    w = op("gpsimd", lambda e, dst=dst, p=p, g=g, slot=slot:
                           e.dma_start(
                               out=dst[p, :, g * 512:(g + 1) * 512],
                               in_=B(STAGE + slot * 512, 512)), inc="lp", k=16)
                    stage_free[slot] = w
                    kvq_w = w
                    psi += 1
                    qpop(1)
            for su in range(4):
                c = g * 4 + su
                for half in range(2):
                    ps = (PS3, PS4, PS5)[psi % 3]
                    fs, fv_ = qkv_free[psi % 3]
                    if fs:
                        wt("tensor", fs, fv_)
                    for kc in range(8):
                        mm = op("tensor", lambda e, ps=ps, kc=kc, su=su, half=half:
                                e.matmul(
                                    ps[:, 0:512],
                                    xnt[:, kc * 512 + su * 128: kc * 512 + (su + 1) * 128],
                                    B(WQ + kc * 3072 + 2048 + half * 512, 512),
                                    start=(kc == 0), stop=(kc == 7)),
                                inc="pe" if kc == 7 else None)
                    slot = psi % 6
                    wt("scalar", "pe", mm)
                    wt("scalar", "lp", stage_free[slot])
                    a = op("scalar", lambda e, ps=ps, slot=slot: e.copy(
                        B(STAGE + slot * 512, 512), ps[:, 0:512]), inc="ac")
                    qkv_free[psi % 3] = ("ac", a)
                    wt("gpsimd", "ac", a)
                    w = op("gpsimd", lambda e, half=half, c=c, slot=slot:
                           e.dma_start(
                               out=v_scr[half * 4:half * 4 + 4, :,
                                         c * 128:(c + 1) * 128]
                               .rearrange("k p q -> p k q"),
                               in_=B(STAGE + slot * 512, 512)
                               .rearrange("p (k q) -> p k q", k=4)),
                           inc="lp", k=16)
                    stage_free[slot] = w
                    kvq_w = w
                    psi += 1
                    qpop(1)
            xnt_pe_free = mm

        qpop(500)
        pe_p1_end = cnt["pe"]

        g_free2 = [0, 0]
        for j, kc in enumerate([1, 2, 3, 4, 5, 6, 7, 0]):
            sl = kc % 2
            woff, moff = gsl[sl]
            wt("sync", "dv", 0)
            wt("sync", "pl", g_free2[sl])
            if j == 0:
                wt("sync", "dv", q_ready[1])
            op("sync", lambda e, kc=kc, woff=woff: e.dma_start(
                out=stg[:, woff:woff + 1024],
                in_=woutT[kc * 128:(kc + 1) * 128, :]), inc="ld", k=16)
            lm = op("sync", lambda e, kc=kc, moff=moff: e.dma_start(
                out=stg[:, moff:moff + 1024],
                in_=moutT[kc * 128:(kc + 1) * 128, :]), inc="ld", k=16)
            wt("gpsimd", "ld", lm)
            if j == 0:
                wt("gpsimd", "pe", pe_p1_end)
            op("gpsimd", lambda e, moff=moff: e.tensor_scalar(
                B(35328, 1024), stg[:, moff:moff + 1024], 0.0, None, ALU.is_gt))
            gv2 = op("gpsimd", lambda e, kc=kc, woff=woff: e.tensor_tensor(
                B(WOUT + kc * 1024, 1024),
                stg[:, woff:woff + 1024], B(35328, 1024), ALU.mult), inc="pl")
            g_free2[sl] = gv2
        wout_pl = gv2

        PS6F = PS6[:, 0:1024].bitcast(F32)
        ps12 = [PS1, PS2]
        ss = dict(
            ktA_load=0, ktB_load=0, vA_load=0, vB_load=0,
            qt_load=[0, 0], qt_slot_free=[0, 0],
            pt_pv_free=[0, 0, 0, 0, 0], pt_tree_free=[0, 0, 0, 0, 0],
            ps12_free=[0, 0], ps12_dv_free=0,
            att_free=0, psat_free=[0, 0],
            ps4_free_dv=ss_qnt["ps4_free_init"], tree_slot_free=0, exp_last=0,
            s_w_of=[0] * G, sslot_free=[q_ready[1], q_ready[1]],
            x_st=[0, 0], x_free=[xfree[0], xfree[1]],
            ps3_free=("dv", q_ready[1]),
            ht_reader_pe=0, xnt_reader_pe=pe_p1_end,
            pending_tail=[], rv=0, d1=0, rb=0, att_last_nv=0,
        )
        tree_pool = [list(range(6)), list(range(6))]
        slot_guard = [[0] * 6, [0] * 6]

        def ffn_round_gen(g):
            wt("sync", "lp", xn_w[g])
            wt("sync", "pe", ss["xnt_reader_pe"])
            xl = op("sync", lambda e, g=g: e.dma_start(
                out=xnt[:].rearrange("p (k q) -> p k q", k=8),
                in_=xn_scr[:, :, g * 512:(g + 1) * 512]
                .rearrange("k p q -> p k q")), inc="ld", k=16)
            yield
            last_f1 = 0
            ha = 0
            for fc in range(32):
                fs, fv_ = ss["ps3_free"]
                if fs:
                    wt("tensor", fs, fv_)
                if fc == 0:
                    wt("tensor", "ld", xl)
                    wt("tensor", "dv", q_ready[0])
                    wt("tensor", "pl", wout_pl)
                    wt("tensor", "pe", ss["ht_reader_pe"])
                for kc in range(8):
                    fmm = op("tensor", lambda e, kc=kc, fc=fc: e.matmul(
                        PS3[:, 0:512],
                        w1q[:, kc * 4096 + fc * 128: kc * 4096 + (fc + 1) * 128],
                        xnt[:, kc * 512:(kc + 1) * 512],
                        start=(kc == 0), stop=(kc == 7)),
                        inc="pe" if kc == 7 else None)
                last_f1 = fmm
                wt("vector", "pe", fmm)
                ha = op("vector", lambda e, fc=fc: e.tensor_scalar(
                    B(HT + fc * 512, 512), PS3[:, 0:512],
                    b1c[:, fc:fc + 1], 0.0, ALU.add, ALU.max), inc="dv")
                ss["ps3_free"] = ("dv", ha)
                yield
            ss["xnt_reader_pe"] = last_f1
            ht_ready_dv = op("vector", lambda e: e.tensor_copy(padsc[:, 0:512], padsc[:, 512:1024]), inc="dv")
            last_f2 = 0
            for su in range(4):
                t = g * 4 + su
                bi = su % 2
                wt("sync", "lp", ss["s_w_of"][g])
                wt("sync", "pl", wout_pl)
                wt("sync", "dv", ss["sslot_free"][bi])
                sl_ = op("sync", lambda e, t=t, bi=bi: e.dma_start(
                    out=stg[:, 2048 + bi * 1024: 2048 + (bi + 1) * 1024],
                    in_=s_scr[t * 128:(t + 1) * 128, :]), inc="ld", k=16)
                wt("sync", "st", ss["x_st"][bi])
                wt("sync", "dv", ss["x_free"][bi])
                ss["x_free"][bi] = 0
                xl_ = op("sync", lambda e, t=t, bi=bi: e.dma_start(
                    out=xbuf[:, bi * 1024:(bi + 1) * 1024],
                    in_=xc[t * 128:(t + 1) * 128, :]), inc="ld", k=16)
                yield
                fv2 = 0
                gm = 0
                for half in range(2):
                    for blk in range(4):
                        if blk == 0:
                            fs, fv_ = ss["ps3_free"]
                            if fs:
                                wt("tensor", fs, fv_)
                        if su == 0 and half == 0 and blk == 0:
                            wt("tensor", "dv", ht_ready_dv)
                            wt("tensor", "dv", q_ready[1])
                            wt("tensor", "pl", wout_pl)
                        for fc in range(blk * 8, (blk + 1) * 8):
                            gm = op("tensor", lambda e, fc=fc, su=su, half=half:
                                    e.matmul(
                                        PS3[:, 0:512],
                                        B(HT + fc * 512 + su * 128, 128),
                                        w2q[:, fc * 1024 + half * 512:
                                            fc * 1024 + (half + 1) * 512],
                                        start=(fc == 0), stop=(fc == 31)),
                                    inc="pe" if fc == 31 else None)
                        yield
                    last_f2 = gm
                    wt("vector", "pe", gm)
                    op("vector", lambda e, half=half: e.tensor_tensor(
                        stg[:, 1024 + half * 512:1024 + (half + 1) * 512],
                        PS3[:, 0:512], rows[:, half * 512:(half + 1) * 512],
                        ALU.add))
                    fv2 = op("vector", lambda e: e.tensor_copy(padsc[:, 0:512], padsc[:, 512:1024]), inc="dv")
                    ss["ps3_free"] = ("dv", fv2)
                    yield
                wt("scalar", "dv", fv2)
                th = op("scalar", lambda e: e.activation(
                    stg[:, 0:1024], stg[:, 1024:2048], AF.Tanh, scale=0.5),
                    inc="ac")
                wt("vector", "ac", th)
                op("vector", lambda e: e.tensor_scalar(
                    stg[:, 0:1024], stg[:, 0:1024], 0.5, 0.5, ALU.mult, ALU.add))
                op("vector", lambda e: e.tensor_tensor(
                    stg[:, 1024:2048], stg[:, 1024:2048], stg[:, 0:1024],
                    ALU.mult))
                wt("vector", "ld", max(sl_, xl_))
                sv2 = op("vector", lambda e, bi=bi: e.tensor_tensor(
                    stg[:, 1024:2048],
                    stg[:, 2048 + bi * 1024: 2048 + (bi + 1) * 1024],
                    stg[:, 1024:2048], ALU.mult), inc="dv")
                ss["sslot_free"][bi] = sv2
                op("vector", lambda e, bi=bi: e.tensor_tensor(
                    xbuf[:, bi * 1024:(bi + 1) * 1024],
                    xbuf[:, bi * 1024:(bi + 1) * 1024], stg[:, 1024:2048],
                    ALU.add))
                ov = op("vector", lambda e: e.tensor_copy(padsc[:, 0:512], padsc[:, 512:1024]), inc="dv")
                wt("sync", "dv", ov)
                ss["x_st"][bi] = op("sync", lambda e, t=t, bi=bi: e.dma_start(
                    out=yc[t * 128:(t + 1) * 128, :],
                    in_=xbuf[:, bi * 1024:(bi + 1) * 1024]), inc="st", k=16)
                yield
            ss["ht_reader_pe"] = last_f2

        for wi2, (wten2, kc2, part2, width2, qdst2, tcol2) in enumerate(
                [(w1T, 7, 1, 4096, w1q, 10), (w2T, 31, 0, 1024, w2q, 12)]):
            cw2 = min(width2, 2048)
            wt("sync", "dv", q_ready[1])
            wt("sync", "pl", wout_pl)
            rl = op("sync", lambda e, wten2=wten2, kc2=kc2, part2=part2, cw2=cw2,
                    wi2=wi2: e.dma_start(
                        out=stg[:, wi2 * 2048:wi2 * 2048 + cw2],
                        in_=wten2[kc2 * 128:(kc2 + 1) * 128,
                                  part2 * cw2:(part2 + 1) * cw2]),
                    inc="ld", k=16)
            wt("vector", "ld", rl)
            op("vector", lambda e, wi2=wi2, tcol2=tcol2, cw2=cw2: e.tensor_scalar(
                B(QPOS, cw2), stg[:, wi2 * 2048:wi2 * 2048 + cw2],
                cols[:, tcol2:tcol2 + 1], None, ALU.is_ge))
            _m2 = op("vector", lambda e, wi2=wi2, tcol2=tcol2, cw2=cw2:
                     e.tensor_scalar(
                         B(QNEG, cw2), stg[:, wi2 * 2048:wi2 * 2048 + cw2],
                         cols[:, tcol2 + 1:tcol2 + 2], None, ALU.is_le), inc="dv")
            wt("vector", "dv", _m2)
            rq = op("vector", lambda e, qdst2=qdst2, kc2=kc2, part2=part2, cw2=cw2,
                    width2=width2: e.tensor_tensor(
                        qdst2[:, kc2 * width2 + part2 * cw2:
                              kc2 * width2 + (part2 + 1) * cw2],
                        B(QPOS, cw2), B(QNEG, cw2), ALU.subtract), inc="dv")
            q_ready[wi2] = max(q_ready[wi2], rq)
        requant_dv = cnt["dv"]

        wt("tensor", "dv", v_mem)
        wt("tensor", "dv", max(q_ready[1], requant_dv))
        wt("tensor", "ac", cnt["ac"])
        for g in range(G):
            fq = ffn_round_gen(g - 1) if g > 0 else iter(())

            def pop(n=1, fq=fq):
                for _ in range(n):
                    if next(fq, "END") == "END":
                        break
            for p in range(NP):
                ko = KT
                psat = PS5[:, 0:512] if p % 2 == 0 else PS6F
                if g == 0 and p == 0:
                    wt("sync", "lp", kvq_w)
                    wt("gpsimd", "lp", kvq_w)
                    ss["qt_load"][0] = op("sync", lambda e: e.dma_start(
                        out=qtb[:, 0:512], in_=q_scr[0, :, 0:512]),
                        inc="ld", k=16)
                    ss["ktA_load"] = op("sync", lambda e: e.dma_start(
                        out=B(KT, 2048), in_=kt_scr[0, :, 0:2048]),
                        inc="ld", k=16)
                    ss["ktB_load"] = op("sync", lambda e: e.dma_start(
                        out=B(KT + 2048, 2048), in_=kt_scr[0, :, 2048:4096]),
                        inc="ld", k=16)
                    ss["vA_load"] = op("gpsimd", lambda e: e.dma_start(
                        out=B(VB, 2048), in_=v_scr[0, :, 0:2048]),
                        inc="lp", k=16)
                    ss["vB_load"] = op("gpsimd", lambda e: e.dma_start(
                        out=B(VB + 2048, 2048), in_=v_scr[0, :, 2048:4096]),
                        inc="lp", k=16)
                qoff = (p % 2) * 512
                tree_stack = [[], []]
                troot = [None, None]
                ax_of = [0] * NC
                sm_of = [0] * NC
                pv_of = [0] * NC
                last_pv = 0
                dvv = 0
                np_, ng_ = (p + 1, g) if p + 1 < NP else (0, g + 1)
                have_next = ng_ < G

                def emit_pv(c, p=p, psat=psat):
                    nonlocal last_pv
                    wt("tensor", "ac", ax_of[c])
                    if c == 0:
                        wt("tensor", "lp", ss["vA_load"])
                        wt("tensor", "dv", ss["psat_free"][p % 2])
                    if c == 16:
                        wt("tensor", "lp", ss["vB_load"])
                    for h in range(2):
                        pv = op("tensor", lambda e, c=c, ci=c % 5, h=h, psat=psat:
                                e.matmul(
                                    psat[h * 64:(h + 1) * 64, :],
                                    B(VB + c * 128 + h * 64, 64),
                                    B(PT[ci] + h * 512, 512),
                                    start=(c == 0), stop=(c == NC - 1)),
                                inc="pe" if h == 1 else None)
                    ss["pt_pv_free"][c % 5] = pv
                    pv_of[c] = pv
                    last_pv = pv

                def emit_scores(c):
                    ci = c % 2
                    ps = ps12[ci]
                    if c < 2:
                        wt("tensor", "ld", max(ss["ktA_load"], ss["qt_load"][p % 2]))
                        wt("tensor", "dv", ss["ps12_dv_free"])
                    if c == 14 or c == 15:
                        wt("tensor", "ld", ss["ktB_load"])
                    wt("tensor", "ac", ss["ps12_free"][ci])
                    op("tensor", lambda e, ps=ps, ko=ko, c=c, qoff=qoff: e.matmul(
                        ps[:, 0:512],
                        big[0:64, ko + c * 128: ko + (c + 1) * 128],
                        qtb[0:64, qoff:qoff + 512], start=True, stop=True))
                    sv_ = op("tensor", lambda e, ps=ps, ko=ko, c=c, qoff=qoff:
                             e.matmul(
                                 ps[:, 512:1024],
                                 big[64:128, ko + c * 128: ko + (c + 1) * 128],
                                 qtb[64:128, qoff:qoff + 512], start=True,
                                 stop=True), inc="pe")
                    sm_of[c] = sv_
                    return sv_

                def emit_exp(c, sm):
                    ci = c % 2
                    ps = ps12[ci]
                    wt("scalar", "pe", sm)
                    wt("scalar", "pe", ss["pt_pv_free"][c % 5])
                    wt("scalar", "dv", ss["pt_tree_free"][c % 5])
                    ax = op("scalar", lambda e, ps=ps, c4=c % 5: e.activation(
                        B(PT[c4], 1024), ps[:, 0:1024], AF.Exp), inc="ac")
                    ax_of[c] = ax
                    ss["ps12_free"][ci] = ax
                    ss["exp_last"] = ax

                def emit_tree(c):
                    nonlocal dvv
                    wt("vector", "ac", ax_of[c])
                    for h in range(2):
                        slot = tree_pool[h].pop(0)
                        if slot_guard[h][slot]:
                            wt("vector", "pe", slot_guard[h][slot])
                            slot_guard[h][slot] = 0
                        dvv = op("vector", lambda e, h=h, slot=slot,
                                 ca=(c - 1) % 5, cb=c % 5:
                                 e.tensor_tensor(
                                     B(TREE + (h * 6 + slot) * 512, 512),
                                     B(PT[ca] + h * 512, 512),
                                     B(PT[cb] + h * 512, 512), ALU.add),
                                 inc="dv")
                        lev = 1
                        while tree_stack[h] and tree_stack[h][-1][0] == lev:
                            _, pslot = tree_stack[h].pop()
                            dvv = op("vector",
                                     lambda e, h=h, slot=slot, pslot=pslot:
                                     e.tensor_tensor(
                                         B(TREE + (h * 6 + slot) * 512, 512),
                                         B(TREE + (h * 6 + slot) * 512, 512),
                                         B(TREE + (h * 6 + pslot) * 512, 512),
                                         ALU.add), inc="dv")
                            tree_pool[h].append(pslot)
                            lev += 1
                        tree_stack[h].append((lev, slot))
                        if c == NC - 1:
                            troot[h] = (tree_stack[h][-1][1], dvv)
                    ss["pt_tree_free"][(c - 1) % 5] = dvv
                    ss["pt_tree_free"][c % 5] = dvv

                def make_tail(p=p, psat=psat, troot_l=None, lpv=None):
                    def t1():
                        wt("tensor", "dv", troot_l[1][1])
                        wt("tensor", "dv", ss["ps4_free_dv"])
                        op("tensor", lambda e, sl=troot_l[0][0]: e.matmul(
                            PS4[0:1, 0:512], onesb[:], B(TREE + sl * 512, 512),
                            start=True, stop=True))
                        d1 = op("tensor", lambda e, sl=troot_l[1][0]: e.matmul(
                            PS4[64:65, 0:512], onesb[:],
                            B(TREE + (6 + sl) * 512, 512),
                            start=True, stop=True), inc="pe")
                        ss["d1"] = d1
                        for h in range(2):
                            rslot = troot_l[h][0]
                            tree_pool[h].append(rslot)
                            slot_guard[h][rslot] = d1

                    def _rec(e, off, row):
                        with nc.allow_low_precision(reason="recip row bf16"):
                            return e.reciprocal(big[0:1, off:off + 512],
                                                PS4[row:row + 1, 0:512])

                    def t2():
                        wt("vector", "pe", ss["d1"])
                        op("vector", lambda e: _rec(e, RRA, 0))

                    def t3():
                        op("vector", lambda e: _rec(e, RRB, 64))
                        ss["rv"] = op("vector", lambda e: e.tensor_copy(
                            padsc[:, 0:512], padsc[:, 512:1024]), inc="dv")

                    def t4():
                        wt("tensor", "dv", ss["rv"])
                        op("tensor", lambda e: e.matmul(
                            PS4[0:64, 0:512], onesrb[:], big[0:1, RRA:RRA + 512],
                            start=True, stop=True))
                        ss["rb"] = op("tensor", lambda e: e.matmul(
                            PS4[64:128, 0:512], onesrb[:],
                            big[0:1, RRB:RRB + 512],
                            start=True, stop=True), inc="pe")

                    def t5():
                        wt("vector", "pe", ss["rb"])
                        rc = op("vector", lambda e: e.tensor_copy(
                            B(RSB, 512), PS4[:, 0:512]), inc="dv")
                        ss["ps4_free_dv"] = rc
                        wt("vector", "pe", lpv[0])
                        wt("vector", "pe", ss["att_free"])
                        op("vector", lambda e, p=p, psat=psat:
                           e.tensor_tensor(
                               B(ATT + p * 512, 512), psat, B(RSB, 512),
                               ALU.mult))
                        nv = op("vector", lambda e: e.tensor_copy(padsc[:, 0:512], padsc[:, 512:1024]), inc="dv")
                        ss["psat_free"][p % 2] = nv
                        ss["att_last_nv"] = nv
                    return [t1, t2, t3, t4, t5]

                TAIL_STEPS = {2: 0, 6: 1, 12: 2, 16: 3, 20: 4}
                for c2 in range(0, NC, 2):
                    smA = emit_scores(c2)
                    smB = emit_scores(c2 + 1)
                    sm = smB
                    if c2 >= 2:
                        emit_pv(c2 - 2)
                        emit_pv(c2 - 1)
                    if c2 in TAIL_STEPS and ss["pending_tail"]:
                        ss["pending_tail"][TAIL_STEPS[c2]]()
                    if c2 == 18 and have_next:
                        wt("sync", "pe", sm_of[15])
                        ss["ktA_load"] = op("sync", lambda e, np_=np_:
                                            e.dma_start(out=B(KT, 2048),
                                                        in_=kt_scr[np_, :, 0:2048]),
                                            inc="ld", k=16)
                        wt("sync", "pe", ss["qt_slot_free"][np_ % 2])
                        ss["qt_load"][np_ % 2] = op(
                            "sync", lambda e, np_=np_, ng_=ng_: e.dma_start(
                                out=qtb[:, (np_ % 2) * 512:(np_ % 2) * 512 + 512],
                                in_=q_scr[np_, :, ng_ * 512:(ng_ + 1) * 512]),
                            inc="ld", k=16)
                        wt("gpsimd", "pe", pv_of[15])
                        ss["vA_load"] = op("gpsimd", lambda e, np_=np_:
                                           e.dma_start(out=B(VB, 2048),
                                                       in_=v_scr[np_, :, 0:2048]),
                                           inc="lp", k=16)
                    if (c2 // 2) % 3 != 0:
                        pop(1)
                    emit_exp(c2, smA)
                    emit_exp(c2 + 1, smB)
                    emit_tree(c2 + 1)
                emit_pv(NC - 2)
                emit_pv(NC - 1)
                lpv = [last_pv]
                stages = make_tail(troot_l=troot, lpv=lpv)
                if p == NP - 1:
                    for f in stages:
                        f()
                    ss["pending_tail"] = []
                else:
                    ss["pending_tail"] = stages
                ss["qt_slot_free"][p % 2] = sm
                if have_next:
                    wt("sync", "pe", sm_of[NC - 1])
                    ss["ktB_load"] = op("sync", lambda e, np_=np_: e.dma_start(
                        out=B(KT + 2048, 2048), in_=kt_scr[np_, :, 2048:4096]),
                        inc="ld", k=16)
                    wt("gpsimd", "pe", last_pv)
                    ss["vB_load"] = op("gpsimd", lambda e, np_=np_: e.dma_start(
                        out=B(VB + 2048, 2048), in_=v_scr[np_, :, 2048:4096]),
                        inc="lp", k=16)
                pop(2)
            att_ready = ss["att_last_nv"]
            pop(200)
            s_slots = [2048, 3072]
            for su in range(4):
                if su == 0:
                    wt("tensor", "dv", att_ready)
                    wt("tensor", "pl", wout_pl)
                wt("tensor", "ac", ss["exp_last"])
                wt("tensor", "dv", ss["ps12_dv_free"])
                for p_ in range(NP):
                    for half in range(2):
                        om = op("tensor", lambda e, p_=p_, su=su, half=half:
                                e.matmul(
                                    PS1[:, half * 512:(half + 1) * 512],
                                    B(ATT + p_ * 512 + su * 128, 128),
                                    B(WOUT + p_ * 1024 + half * 512, 512),
                                    start=(p_ == 0), stop=(p_ == NP - 1)),
                                inc="pe" if (p_ == NP - 1 and half == 1) else None)
                wt("vector", "pe", om)
                wt("vector", "dv", ss["sslot_free"][su % 2])
                op("vector", lambda e, su=su: e.tensor_tensor(
                    stg[:, s_slots[su % 2]:s_slots[su % 2] + 1024],
                    PS1[:, 0:1024], rows[:, 1024:2048], ALU.add))
                sv = op("vector", lambda e: e.tensor_copy(padsc[:, 0:512], padsc[:, 512:1024]), inc="dv")
                ss["ps12_dv_free"] = sv
                ss["sslot_free"][su % 2] = sv
                wt("gpsimd", "dv", sv)
                sw = op("gpsimd", lambda e, g=g, su=su: e.dma_start(
                    out=s_scr[(g * 4 + su) * 128:(g * 4 + su + 1) * 128, :],
                    in_=stg[:, s_slots[su % 2]:s_slots[su % 2] + 1024]),
                    inc="lp", k=16)
                ss["s_w_of"][g] = sw
            ss["att_free"] = om
        fq = ffn_round_gen(G - 1)
        for _ in range(500):
            if next(fq, "END") == "END":
                break

        wt("sync", "st", cnt["st"])

        @block.sync
        def _(eng):
            for f in streams["sync"]:
                f(eng)

        @block.gpsimd
        def _(eng):
            for f in streams["gpsimd"]:
                f(eng)

        @block.tensor
        def _(eng):
            for f in streams["tensor"]:
                f(eng)

        @block.vector
        def _(eng):
            for f in streams["vector"]:
                f(eng)

        @block.scalar
        def _(eng):
            for f in streams["scalar"]:
                f(eng)

    return nc


def kernel(**inputs):
    x = np.ascontiguousarray(np.asarray(inputs["L0_attn_x"], np.float32))
    C = lambda a: np.ascontiguousarray(np.asarray(a, np.float32))
    shared = dict(
        wqkvT=C(np.asarray(inputs["L0_attn_qkv_weight"]).T),
        mqkvT=C(np.asarray(inputs["L0_attn_qkv_weight_mask"]).T),
        woutT=C(np.asarray(inputs["L0_attn_out_weight"]).T),
        moutT=C(np.asarray(inputs["L0_attn_out_weight_mask"]).T),
        w1T=C(np.asarray(inputs["L0_ffn_W1"]).T),
        w2T=C(np.asarray(inputs["L0_ffn_W2"]).T),
        b1c=C(np.asarray(inputs["L0_ffn_b1"]).reshape(32, 128).T),
        b2r=C(np.asarray(inputs["L0_ffn_b2"]).reshape(1, D)),
        obr=C(np.asarray(inputs["L0_attn_out_bias"]).reshape(1, D)),
        obm=C(np.asarray(inputs["L0_attn_out_bias_mask"]).reshape(1, D)),
        iden=np.eye(128).astype(ml_dtypes.bfloat16),
    )
    nc = build_nc()
    in_maps = [dict(xc=np.ascontiguousarray(x[c]), **shared) for c in range(8)]
    res = run_bass_kernel_spmd(nc, in_maps, core_ids=list(range(8)))
    return np.stack([np.asarray(r["yc"], np.float32) for r in res.results], axis=0)
